# version 21
# baseline (speedup 1.0000x reference)
import numpy as np
import ml_dtypes
from contextlib import ExitStack

import concourse.bass as bass
import concourse.mybir as mybir
from concourse.bass_utils import run_bass_kernel_spmd

F32 = mybir.dt.float32
BF16 = mybir.dt.bfloat16
AF = mybir.ActivationFunctionType
ALU = mybir.AluOpType
AX = mybir.AxisListType

D = 4096
DC = 32
HD = 128
NH = 32
TP = 512
NPASS = 4
TALL = 4096
TOWN = 2048
SC = 0.08838834764831845
EPS = 1e-5
NPS = 3
NPT = 6

ENGS = ("sync", "scalar", "vector", "tensor", "gpsimd")
BUILD_IN_ATT0 = True
DBG = False
PHASE_LIMIT = None
OP_LIMIT = None


def _op_ok(wait, ops):
    s, v = wait
    tot = 0
    for (_, _, _, incs) in ops:
        for s2, n in incs:
            if s2 == s:
                tot += n
    return v <= tot


class Sched:
    def __init__(self):
        self.phases = []
        self.ops = None
        self.cnt = {}

    def phase(self, name):
        self.ops = []
        self.phases.append((name, self.ops))

    def emit(self, eng, fn, waits=(), incs=()):
        for s, v in waits:
            have = self.cnt.get(s, 0)
            assert v <= have, f"wait {s}>={v} but only {have} scheduled"
        self.ops.append((eng, fn, tuple(waits), tuple(incs)))
        for s, n in incs:
            self.cnt[s] = self.cnt.get(s, 0) + n

    def c(self, s):
        return self.cnt.get(s, 0)


def _build():
    nc = bass.Bass("TRN2", target_bir_lowering=False)

    x_all = nc.dram_tensor("x_all", [TALL, D], F32, kind="ExternalInput")
    x_own = nc.dram_tensor("x_own", [TOWN, D], F32, kind="ExternalInput")
    x_allT = nc.dram_tensor("x_allT", [D, TALL], F32, kind="ExternalInput")
    wq = nc.dram_tensor("wq", [D, D], F32, kind="ExternalInput")
    wk = nc.dram_tensor("wk", [HD, D], F32, kind="ExternalInput")
    wv = nc.dram_tensor("wv", [HD, D], F32, kind="ExternalInput")
    wo = nc.dram_tensor("wo", [D, D], F32, kind="ExternalInput")
    wf = nc.dram_tensor("wf", [D, D], F32, kind="ExternalInput")
    bqv = nc.dram_tensor("bqv", [32, 128], F32, kind="ExternalInput")
    bov = nc.dram_tensor("bov", [32, 128], F32, kind="ExternalInput")
    bfv = nc.dram_tensor("bfv", [32, 128], F32, kind="ExternalInput")
    bkv = nc.dram_tensor("bkv", [1, 128], F32, kind="ExternalInput")
    bvv = nc.dram_tensor("bvv", [1, 128], F32, kind="ExternalInput")
    av = nc.dram_tensor("av", [128, D], F32, kind="ExternalInput")
    bv_ = nc.dram_tensor("bv_", [8, D, 16], F32, kind="ExternalInput")
    lwr = nc.dram_tensor("lwr", [128, 1], F32, kind="ExternalInput")
    maskv = nc.dram_tensor("maskv", [8, 128, 512], BF16, kind="ExternalInput")
    idbd = nc.dram_tensor("idbd", [128, 128], BF16, kind="ExternalInput")
    idfd = nc.dram_tensor("idfd", [128, 128], F32, kind="ExternalInput")
    outp = nc.dram_tensor("out_own", [TOWN, D], F32, kind="ExternalOutput")
    wq_scr = nc.dram_tensor("wq_scr", [D, D], BF16, kind="Internal")
    wo_scr = nc.dram_tensor("wo_scr", [D, D], BF16, kind="Internal")
    wf_scr = nc.dram_tensor("wf_scr", [D, D], BF16, kind="Internal")
    bt_scr = nc.dram_tensor("bt_scr", [D, 128], BF16, kind="Internal")
    if DBG:
        dKT = nc.dram_tensor("dKT", [128, TALL], BF16, kind="ExternalOutput")
        dVT = nc.dram_tensor("dVT", [128, TALL], BF16, kind="ExternalOutput")
        dXN = nc.dram_tensor("dXN", [128, DC * TP], BF16, kind="ExternalOutput")
        dAT = nc.dram_tensor("dAT", [128, NH * TP], BF16, kind="ExternalOutput")
        dHS = nc.dram_tensor("dHS", [128, TP], BF16, kind="ExternalOutput")

    ctx = ExitStack()
    sb = lambda n, sh, dt: ctx.enter_context(nc.sbuf_tensor(n, sh, dt))
    psr = lambda n, sh, dt: ctx.enter_context(nc.psum_tensor(n, sh, dt))

    KT = sb("KT", [128, TALL], BF16)
    VT = sb("VT", [128, TALL], BF16)
    XN = sb("XN", [128, DC * TP], BF16)
    ATT = sb("ATT", [128, NH * TP], BF16)
    A_T = sb("A_T", [128, D], BF16)
    QT = sb("QT", [128, 2 * TP], BF16)
    SQb = sb("SQb", [128, TP], BF16)
    PT = sb("PT", [128, NPT * TP], BF16)
    HSb = sb("HSb", [128, TP], BF16)
    PTS = sb("PTS", [128, 2 * TP], BF16)
    AUN = sb("AUN", [128, TP], BF16)
    WTA = [sb(f"WTA{i}", [128, D], BF16) for i in range(2)]
    WTB = sb("WTB", [128, D], BF16)
    BTT = [sb(f"BTT{i}", [128, 128], BF16) for i in range(2)]
    WFb = [sb(f"WFb{i}", [128, 512], F32) for i in range(2)]
    BSF = sb("BSF", [128, 128], F32)
    XF0 = [sb(f"XF0{i}", [128, 2048], F32) for i in range(2)]
    XF1 = [sb(f"XF1{i}", [128, 2048], F32) for i in range(2)]
    _xf1b_addr = nc.lookup_mloc(XF1[1]).addr
    WFb += [nc.alloc_sbuf_tensor_at(f"WFb{i}", [128, 512], F32,
                                    offset=_xf1b_addr + 2048 * (i - 2))
            for i in (2, 3)]
    XMf = sb("XMf", [128, D], F32)
    STb = [sb(f"STb{i}", [128, TP], F32) for i in range(2)]
    GOb = sb("GOb", [128, TP], F32)
    SGb = sb("SGb", [128, TP], F32)
    OSb = [sb(f"OSb{i}", [128, 128], F32) for i in range(4)]
    XRb = [sb(f"XRb{i}", [128, 128], F32) for i in range(4)]
    BQ = sb("BQ", [128, 32], F32)
    BO = sb("BO", [128, 32], F32)
    BF_ = sb("BF_", [128, 32], F32)
    BK = sb("BK", [128, 1], F32)
    BVc = sb("BVc", [128, 1], F32)
    COEF = sb("COEF", [128, 1], F32)
    MK = sb("MK", [128, 8 * 512], BF16)
    IDB = sb("IDB", [128, 128], BF16)
    IDF = sb("IDF", [128, 128], F32)
    ONEb = sb("ONEb", [128, 1], BF16)
    ONEr = sb("ONEr", [1, 128], F32)
    ONEbr = sb("ONEbr", [1, 128], BF16)
    BVB = sb("BVB", [1, 128], BF16)
    BALL = sb("BALL", [128, 64], F32)
    BN6 = [sb(f"BN6{i}", [128, 48], F32) for i in range(2)]
    BN2 = [sb(f"BN2{i}", [128, 2], F32) for i in range(2)]
    RST = [sb(f"RST{i}", [128, 1], F32) for i in range(2)]
    NMR = [sb(f"NMR{i}", [128, 1], F32) for i in range(2)]
    QM2 = sb("QM2", [1, 1], F32)
    BT1 = sb("BT1", [1, 1], F32)
    NB1 = sb("NB1", [1, 1], F32)
    KM8 = sb("KM8", [1, 8], F32)
    KSC = sb("KSC", [1, 1], F32)
    LIV = sb("LIV", [1, TP], F32)
    LIVB = sb("LIVB", [128, TP], F32)
    BST = sb("BST", [32, 128], F32)
    WKR = sb("WKR", [1, 128], BF16)
    WVR = sb("WVR", [1, 128], BF16)
    BVBC = sb("BVBC", [128, 128], F32)
    RSTC = sb("RSTC", [128, 4], F32)
    KSQ = ATT[:, 4096:8192]
    WKT = ATT[:, 8192:12288]
    WVT = ATT[:, 12288:16384]

    pSP = psr("pSP", [128, 1024], F32)
    pSQ = psr("pSQ", [128, 1024], F32)
    pAV = psr("pAV", [128, 512], F32)
    pL = psr("pL", [1, 512], F32)
    pMM = psr("pMM", [128, 512], F32)
    pT = psr("pT", [128, 512], F32)
    PXV = [pSP[:, 0:512], pSP[:, 512:1024], pSQ[:, 0:512], pSQ[:, 512:1024]]

    semnames = ["qX00", "qX01", "qX10", "qX11", "qWF0", "qWF1", "qWF2", "qWF3", "qBS",
                "qWA0", "qWA1", "qWB", "qBT0", "qBT1", "qR0", "qR1",
                "qSA0", "qSA1", "qSO0", "qSO1", "qSF", "qSB0", "qSB1",
                "qO0", "qO1", "qDB", "qI",
                "eTp", "eWTe", "eWTs", "eWTp", "eXNe", "eXNs", "eVM", "eXMa", "eBN", "eRS",
                "eCast",
                "eKV", "eKVe", "eQ", "eQev", "eSQ",
                "eB1", "eB2", "eB3", "eB4", "eB5", "eB6",
                "eSc", "eExp", "eMsk", "eAV", "eLS", "ePr",
                "eRec", "eBC", "eBCe", "eATw", "eAVe",
                "eS", "eSTe", "eG", "eGOe", "eSG", "eTsg", "eOS",
                "eHS", "eInit", "eVt", "eKm"]
    S = {n: ctx.enter_context(nc.semaphore(n)) for n in semnames}
    DMASEMS = {n for n in semnames if n.startswith("q")}

    sch = Sched()
    E, C = sch.emit, sch.c

    px_state = {"banks": None, "evts": []}
    wfb_tp = {0: 0, 1: 0, 2: 0, 3: 0}
    wfb_war = {2: [], 3: []}
    wfb_par0 = [0]

    def px_reset(banks):
        px_state["banks"] = banks
        px_state["evts"] = []

    def px_begin():
        banks, evts = px_state["banks"], px_state["evts"]
        n = len(evts)
        d = len(banks)
        w = [evts[n - d]] if n >= d else []
        return banks[n % d], w

    def px_end(sem):
        px_state["evts"].append((sem, C(sem)))

    class SlabSteps:

        def __init__(self, wsrc, row0, dst_wt, war_waits, scr=None, scr_sem=None,
                     bt_oc=None, bt_par=None, bt_sem=None, evac_eng="vector",
                     piecewise=False, tp_waits=()):
            self.wsrc, self.row0, self.dst_wt = wsrc, row0, dst_wt
            self.war = list(war_waits)
            self.scr, self.scr_sem = scr, scr_sem
            self.bt_oc, self.bt_par, self.bt_sem = bt_oc, bt_par, bt_sem
            self.evac_eng, self.piecewise = evac_eng, piecewise
            self.tp_waits = list(tp_waits)
            self.esem = {"vector": "eWTe", "scalar": "eWTs",
                         "gpsimd": "eWTp"}[evac_eng]
            self.par0 = wfb_par0[0]
            wfb_par0[0] = (self.par0 + 8) % 4
            self.total = 8 + (1 if bt_oc is not None else 0)
            self.g = 0
            self.ck = {}

        def remaining(self):
            return self.total - self.g

        def step(self):
            g = self.g
            self.g += 1
            if g < 8:
                par = (self.par0 + g) % 4
                qsem = f"qWF{par}"
                lw = [("eTp", wfb_tp[par])] + wfb_war.get(par, [])
                if par in wfb_war:
                    wfb_war[par] = []
                E("sync", (lambda par=par, g=g, wsrc=self.wsrc, row0=self.row0:
                           nc.sync.dma_start(
                    WFb[par][:, :], wsrc[row0:row0 + 128, 512 * g:512 * g + 512])),
                  waits=lw, incs=[(qsem, 1)])
                bank, pxw = px_begin()
                tw = [(qsem, C(qsem))] + pxw
                if g < 4:
                    tw += self.tp_waits
                for t4 in range(4):
                    def tf(par=par, t4=t4, bank=bank):
                        return nc.tensor.transpose(
                            bank[:, 128 * t4:128 * t4 + 128],
                            WFb[par][:, 128 * t4:128 * t4 + 128], IDF[:, :])
                    E("tensor", tf, waits=(tw if t4 == 0 else []),
                      incs=([("eTp", 1)] if t4 == 3 else []))
                wfb_tp[par] = C("eTp")
                dc0 = g * 4
                ee, es = self.evac_eng, self.esem
                def ef(bank=bank, dc0=dc0, dst_wt=self.dst_wt, ee=ee):
                    if ee == "vector":
                        return nc.vector.tensor_copy(
                            dst_wt[:, 128 * dc0:128 * dc0 + 512], bank[:, :])
                    if ee == "gpsimd":
                        return nc.gpsimd.tensor_copy(
                            dst_wt[:, 128 * dc0:128 * dc0 + 512], bank[:, :])
                    return nc.scalar.copy(
                        dst_wt[:, 128 * dc0:128 * dc0 + 512], bank[:, :])
                E(ee, ef, waits=[("eTp", C("eTp"))] + self.war, incs=[(es, 1)])
                px_end(es)
                if self.scr is not None and self.piecewise:
                    E("gpsimd", (lambda scr=self.scr, row0=self.row0, dc0=dc0,
                                 dst_wt=self.dst_wt: nc.gpsimd.dma_start(
                        scr[row0:row0 + 128, 128 * dc0:128 * dc0 + 512],
                        dst_wt[:, 128 * dc0:128 * dc0 + 512])),
                      waits=[(es, C(es))], incs=[(self.scr_sem, 1)])
                if g == 7:
                    if self.scr is not None and not self.piecewise:
                        E("gpsimd", (lambda scr=self.scr, row0=self.row0,
                                     dst_wt=self.dst_wt: nc.gpsimd.dma_start(
                            scr[row0:row0 + 128, :], dst_wt[:, :])),
                          waits=[(self.esem, C(self.esem))], incs=[(self.scr_sem, 1)])
                    if self.scr is not None:
                        self.ck[self.scr_sem] = C(self.scr_sem)
                    self.ck["eWTe"] = C("eWTe")
                    self.ck["eWTe2"] = C("eWTe")
            else:
                E("sync", (lambda bt_oc=self.bt_oc: nc.sync.dma_start(
                    BSF[:, :].rearrange("p (n r) -> p n r", n=8),
                    bv_[:, 128 * bt_oc:128 * bt_oc + 128, :].rearrange(
                        "n o r -> o n r"))),
                  waits=[("eTp", C("eTp"))], incs=[("qBS", 1)])
                bank, pxw = px_begin()
                E("tensor", (lambda bank=bank: nc.tensor.transpose(
                    bank[:, 0:128], BSF[:, :], IDF[:, :])),
                  waits=[("qBS", C("qBS"))] + pxw, incs=[("eTp", 1)])
                E("vector", (lambda bt_par=self.bt_par, bank=bank:
                             nc.vector.tensor_copy(
                    BTT[bt_par][:, :], bank[:, 0:128])),
                  waits=[("eTp", C("eTp"))] + self.war, incs=[("eWTe", 1)])
                px_end("eWTe")
                E("gpsimd", (lambda bt_oc=self.bt_oc, bt_par=self.bt_par:
                             nc.gpsimd.dma_start(
                    bt_scr[128 * bt_oc:128 * bt_oc + 128, :], BTT[bt_par][:, :])),
                  waits=[("eWTe", C("eWTe"))], incs=[(self.bt_sem, 1)])
                self.ck[self.bt_sem] = C(self.bt_sem)
                self.ck["eWTe"] = C("eWTe")

    def build_slab(wsrc, row0, dst_wt, war_waits, scr=None, scr_sem=None,
                   bt_oc=None, bt_par=None, bt_sem=None, evac_eng="vector",
                   piecewise=False, tp_waits=()):
        s = SlabSteps(wsrc, row0, dst_wt, war_waits, scr, scr_sem,
                      bt_oc, bt_par, bt_sem, evac_eng, piecewise, tp_waits)
        while s.remaining():
            s.step()
        return s.ck

    def ln_stream(x_dram, base_row, ntiles, kv, tile_cb=None, first_waits=()):
        tp_after = {}
        ap_after = {}
        evacs_after = {}

        def xload(tt):
            par = tt % 2
            row = base_row + 128 * tt
            for hh in range(2):
                buf = (XF0 if hh == 0 else XF1)[par]
                ww = ([("eXMa", ap_after[tt - 2])] if tt >= 2
                      else list(first_waits))
                E("sync", (lambda buf=buf, row=row, hh=hh: nc.sync.dma_start(
                    buf[:, :], x_dram[row:row + 128, 2048 * hh:2048 * hh + 2048])),
                  waits=ww, incs=[(f"qX{par}{hh}", 1)])

        xload(0)
        for tt in range(ntiles):
            par = tt % 2
            if tt + 1 < ntiles:
                xload(tt + 1)
            sb6 = BN6[par]
            statw = [("eXMa", ap_after[tt - 2])] if tt >= 2 else []
            for hh in range(2):
                buf = (XF0 if hh == 0 else XF1)[par]
                for k4 in range(4):
                    w = []
                    if k4 == 0:
                        w = [(f"qX{par}{hh}", C(f"qX{par}{hh}"))] + (statw if hh == 0 else [])
                    def bf(buf=buf, k4=k4, hh=hh, sb6=sb6):
                        return nc.vector.bn_stats(
                            sb6[:, 24 * hh + 6 * k4:24 * hh + 6 * k4 + 6],
                            buf[:, 512 * k4:512 * k4 + 512])
                    E("vector", bf, waits=w,
                      incs=([("eBN", 1)] if (hh == 1 and k4 == 3) else []))
            E("vector", (lambda par=par, sb6=sb6: nc.vector.bn_aggr(
                BN2[par][:, :], sb6[:, :])),
              waits=[("eBN", C("eBN"))], incs=[("eBN", 1)])
            E("vector", (lambda par=par: nc.vector.tensor_scalar_add(
                BN2[par][:, 1:2], BN2[par][:, 1:2], EPS)),
              waits=[("eBN", C("eBN"))], incs=[("eBN", 1)])
            E("scalar", (lambda par=par: nc.scalar.activation(
                RST[par][:, :], BN2[par][:, 1:2], AF.Sqrt, scale=1.0)),
              waits=[("eBN", C("eBN"))], incs=[("eRS", 1)])
            E("vector", (lambda par=par: nc.vector.reciprocal(
                RST[par][:, :], RST[par][:, :])),
              waits=[("eRS", C("eRS"))], incs=[("eRS", 1)])
            E("vector", (lambda par=par: nc.vector.scalar_tensor_tensor(
                NMR[par][:, :], BN2[par][:, 0:1], -1.0, RST[par][:, :],
                ALU.mult, ALU.mult)),
              waits=[("eRS", C("eRS"))], incs=[("eRS", 1)])
            rs_now = C("eRS")
            for qq in range(4):
                buf = (XF0 if qq < 2 else XF1)[par]
                bo = 1024 * (qq % 2)
                aw = [("eRS", rs_now)]
                if tt >= 1:
                    aw.append(("eTp", tp_after[(tt - 1, qq)]))
                def af(buf=buf, bo=bo, qq=qq, par=par):
                    return nc.scalar.activation(
                        XMf[:, 1024 * qq:1024 * qq + 1024], buf[:, bo:bo + 1024],
                        AF.Identity, bias=NMR[par][:, :], scale=RST[par][:, :])
                E("scalar", af, waits=aw, incs=[("eXMa", 1)])
                for g4 in range(2):
                    bank, pxw = px_begin()
                    tw = [("eXMa", C("eXMa"))] + pxw
                    for t4 in range(4):
                        dcl = g4 * 4 + t4
                        def tf(dcl=dcl, bank=bank, qq=qq):
                            return nc.tensor.transpose(
                                bank[:, 128 * (dcl % 4):128 * (dcl % 4) + 128],
                                XMf[:, 1024 * qq + 128 * dcl:1024 * qq + 128 * dcl + 128],
                                IDF[:, :])
                        E("tensor", tf, waits=(tw if t4 == 0 else []),
                          incs=([("eTp", 1)] if t4 == 3 else []))
                    dc0 = qq * 8 + g4 * 4
                    eng = "vector" if ((qq + g4) % 2 == 0) else "scalar"
                    esem = "eXNe" if eng == "vector" else "eXNs"
                    ew = [("eTp", C("eTp"))]
                    if kv and tt >= 4:
                        ew.append(("eKV", tt // 4))
                        ew.append(("eVM", tt - 4 + 1))
                    def ef(bank=bank, dc0=dc0, tt=tt, eng=eng):
                        dst = XN[:, :].rearrange("p (dcx f tc) -> p dcx f tc",
                                                 dcx=DC, f=4)[:, dc0:dc0 + 4, tt % 4, :]
                        srcv = bank[:, :].rearrange("p (f tc) -> p f tc", f=4)
                        if eng == "vector":
                            return nc.vector.tensor_copy(dst, srcv)
                        return nc.scalar.copy(dst, srcv)
                    E(eng, ef, waits=ew, incs=[(esem, 1)])
                    px_end(esem)
                tp_after[(tt, qq)] = C("eTp")
            ap_after[tt] = C("eXMa")
            evacs_after[tt] = [("eXNe", C("eXNe")), ("eXNs", C("eXNs"))]
            if kv:
                vb = [pT, pAV][tt % 2]
                for dc in range(DC + 1):
                    w = []
                    if dc == 0:
                        w = list(evacs_after[tt])
                        if tt >= 2:
                            w.append(("eKVe", C("eKVe") - 1))
                    if dc < DC:
                        def vf(dc=dc, tt=tt, vb=vb):
                            c0 = 512 * dc + 128 * (tt % 4)
                            return nc.tensor.matmul(
                                vb[:, 0:128], XN[:, c0:c0 + 128],
                                WVT[:, 128 * dc:128 * dc + 128],
                                start=(dc == 0), stop=False)
                    else:
                        def vf(tt=tt, vb=vb):
                            return nc.tensor.matmul(
                                vb[:, 0:128], ONEbr[:, :], BVB[:, :],
                                start=False, stop=True)
                    E("tensor", vf, waits=w,
                      incs=([("eVM", 1)] if dc == DC else []))
                E("scalar", (lambda tt=tt, vb=vb: nc.scalar.copy(
                    VT[:, 128 * tt:128 * tt + 128], vb[:, 0:128])),
                  waits=[("eVM", C("eVM"))], incs=[("eKVe", 1)])
                if tt % 4 == 3:
                    grp = tt // 4
                    for dc in range(DC):
                        w = []
                        if dc == 0:
                            for t_ in range(tt - 3, tt + 1):
                                w.extend(evacs_after[t_])
                            w = list({(s_, v) for s_, v in w})
                            if grp > 0:
                                w.append(("eKVe", C("eKVe") - 1))
                        def mf(dc=dc):
                            return nc.tensor.matmul(
                                pMM[:, :], WKT[:, 128 * dc:128 * dc + 128],
                                XN[:, 512 * dc:512 * dc + 512],
                                start=(dc == 0), stop=(dc == DC - 1))
                        E("tensor", mf, waits=w,
                          incs=([("eKV", 1)] if dc == DC - 1 else []))
                    E("scalar", (lambda grp=grp: nc.scalar.activation(
                        KT[:, 512 * grp:512 * grp + 512], pMM[:, :], AF.Identity,
                        bias=BK[:, :], scale=1.0)),
                      waits=[("eKV", C("eKV"))], incs=[("eKVe", 1)])
            if tile_cb is not None:
                tile_cb(tt)

    sch.phase("init")
    px_reset(PXV)
    E("gpsimd", lambda: nc.gpsimd.dma_start(
        MK[:, :].rearrange("p (j c) -> p j c", j=8),
        maskv[:, :, :].rearrange("j p c -> p j c")), incs=[("qI", 1)])
    E("gpsimd", lambda: nc.gpsimd.dma_start(IDB[:, :], idbd[:, :]), incs=[("qI", 1)])
    E("gpsimd", lambda: nc.gpsimd.dma_start(IDF[:, :], idfd[:, :]), incs=[("qI", 1)])
    E("gpsimd", lambda: nc.gpsimd.dma_start(COEF[:, :], lwr[:, :]), incs=[("qI", 1)])
    E("vector", lambda: nc.vector.memset(ONEb[:, :], 1.0), incs=[("eInit", 1)])
    E("vector", lambda: nc.vector.memset(ONEr[:, :], 1.0), incs=[("eInit", 1)])
    E("vector", lambda: nc.vector.memset(ONEbr[:, :], 1.0), incs=[("eInit", 1)])
    E("vector", lambda: nc.vector.tensor_scalar_mul(COEF[:, :], COEF[:, :], 2.0),
      waits=[("qI", 4)], incs=[("eInit", 1)])
    for src, dst, w_ in ((bqv, BQ, 32), (bov, BO, 32), (bfv, BF_, 32),
                         (bkv, BK, 1), (bvv, BVc, 1)):
        E("gpsimd", (lambda src=src, w_=w_: nc.gpsimd.dma_start(
            BST[0:w_, :], src[:, :])),
          waits=[("eInit", C("eInit"))], incs=[("qI", 1)])
        E("tensor", (lambda w_=w_: nc.tensor.transpose(
            pT[:, 0:w_], BST[0:w_, :], IDF[0:w_, 0:w_])),
          waits=[("qI", C("qI")), ("eInit", C("eInit"))], incs=[("eTp", 1)])
        E("vector", (lambda dst=dst, w_=w_: nc.vector.tensor_copy(
            dst[:, 0:w_], pT[:, 0:w_])),
          waits=[("eTp", C("eTp"))], incs=[("eInit", 1)])
    E("gpsimd", lambda: nc.gpsimd.dma_start(BSF[0:1, :], bvv[:, :]),
      waits=[("eTp", C("eTp"))], incs=[("qI", 1)])
    E("vector", lambda: nc.vector.tensor_copy(BVB[:, :], BSF[0:1, :]),
      waits=[("qI", C("qI"))], incs=[("eInit", 1)])
    build_slab(av, 0, A_T, war_waits=[])

    sch.phase("kv")
    px_reset(PXV)
    build_slab(wk, 0, WKT, war_waits=[])
    build_slab(wv, 0, WVT, war_waits=[])
    _xf0_addr = nc.lookup_mloc(XF0[0]).addr
    _xf0b_addr = nc.lookup_mloc(XF0[1]).addr
    XTBF = [nc.alloc_sbuf_tensor_at(f"XTBF{i}", [128, 512], F32,
                                    offset=_xf0_addr + 2048 * i) for i in range(4)]
    XTB = [nc.alloc_sbuf_tensor_at(f"XTB{i}", [128, 512], BF16,
                                   offset=_xf0b_addr + 1024 * i) for i in range(4)]
    SQX = [nc.alloc_sbuf_tensor_at(f"SQX{i}", [128, 512], BF16,
                                   offset=_xf0b_addr + 4096 + 1024 * i) for i in range(4)]
    _xf1_addr = nc.lookup_mloc(XF1[0]).addr
    MROW = nc.alloc_sbuf_tensor_at("MROW", [1, 512], F32, offset=_xf1_addr)
    NEGMB = nc.alloc_sbuf_tensor_at("NEGMB", [1, 512], BF16, offset=_xf1_addr + 2048)
    VROW = nc.alloc_sbuf_tensor_at("VROW", [1, 512], F32, offset=_xf1_addr + 4096)
    RSTR = nc.alloc_sbuf_tensor_at("RSTR", [1, 512], F32, offset=_xf1_addr + 6144)
    for name, wt, dst in (("k", WKT, WKR), ("v", WVT, WVR)):
        for dc in range(DC):
            def cf(wt=wt, dc=dc):
                return nc.tensor.matmul(pL[:, 0:128], ONEb[:, :],
                                        wt[:, 128 * dc:128 * dc + 128],
                                        start=(dc == 0), stop=(dc == DC - 1))
            w = ([("eWTe", C("eWTe")), ("eWTs", C("eWTs")),
                  ("eWTp", C("eWTp"))] if dc == 0 else [])
            if dc == 0 and name == "v":
                w.append(("eVM", C("eVM")))
            E("tensor", cf, waits=w, incs=([("eKV", 1)] if dc == DC - 1 else []))
        E("vector", (lambda dst=dst: nc.vector.tensor_copy(dst[:, :], pL[0:1, 0:128])),
          waits=[("eKV", C("eKV"))], incs=[("eVM", 1)])
    E("tensor", (lambda: nc.tensor.matmul(
        pAV[:, 0:128], ONEbr[:, :], BVB[:, :], start=True, stop=True)),
      waits=[("eInit", C("eInit"))], incs=[("eKV", 1)])
    E("vector", (lambda: nc.vector.tensor_copy(BVBC[:, :], pAV[:, 0:128])),
      waits=[("eKV", C("eKV"))], incs=[("eVM", 1)])

    pV = [PXV[0], PXV[1], PXV[2], PXV[3]]
    wq_prev = {0: None, 1: None}
    dcg = 0
    for g in range(8):
        for dc in range(DC):
            par = dc % 4
            qs = f"qX{par % 2}{par // 2}"
            ww = [("eCast", C("eCast") - 3)] if dcg >= 4 else []
            E("sync", (lambda dc=dc, g=g, par=par: nc.sync.dma_start(
                XTBF[par][:, :], x_allT[128 * dc:128 * dc + 128, 512 * g:512 * g + 512])),
              waits=ww, incs=[(qs, 1)])
            cw = [(qs, C(qs))]
            if dcg >= 4:
                cw.append(("eKV", C("eKV") - 3))
            E("vector", (lambda par=par: nc.vector.tensor_copy(
                XTB[par][:, :], XTBF[par][:, :])),
              waits=cw, incs=[("eCast", 1)])
            sw = [("eCast", C("eCast"))]
            if dcg >= 4:
                sw.append(("eKV", C("eKV") - 3))
            E("scalar", (lambda par=par: nc.scalar.activation(
                SQX[par][:, :], XTB[par][:, :], AF.Square)),
              waits=sw, incs=[("eSQ", 1)])
            w0 = [("eCast", C("eCast"))]
            if dc == 0:
                w0.append(("eKVe", C("eKVe")))
                w0.append(("eVM", C("eVM")))
                w0.append(("eBN", C("eBN")))
                w0.append(("eWTe", C("eWTe")))
                w0.append(("eWTs", C("eWTs")))
                w0.append(("eWTp", C("eWTp")))
            def kf(dc=dc):
                return nc.tensor.matmul(pMM[:, :], WKT[:, 128 * dc:128 * dc + 128],
                                        XTB[dc % 4][:, :], start=(dc == 0), stop=False)
            E("tensor", kf, waits=w0)
            def mf(dc=dc):
                return nc.tensor.matmul(pL[:, :], ONEb[:, :], XTB[dc % 4][:, :],
                                        start=(dc == 0), stop=(dc == DC - 1))
            E("tensor", mf)
            def qf(dc=dc):
                return nc.tensor.matmul(pT[0:1, :], ONEb[:, :], SQX[dc % 4][:, :],
                                        start=(dc == 0), stop=(dc == DC - 1))
            E("tensor", qf, waits=[("eSQ", C("eSQ"))])
            def vf(dc=dc):
                return nc.tensor.matmul(
                    pV[0][:, :], WVT[:, 128 * dc:128 * dc + 128],
                    XTB[dc % 4][:, :], start=(dc == 0), stop=False)
            E("tensor", vf, incs=[("eKV", 1)])
            dcg += 1
        E("vector", (lambda: nc.vector.tensor_scalar(
            MROW[:, :], pL[0:1, :], 1.0 / 4096.0, None, ALU.mult)),
          waits=[("eKV", C("eKV"))], incs=[("eBN", 1)])
        E("vector", (lambda: nc.vector.tensor_scalar(
            NEGMB[:, :], pL[0:1, :], -1.0 / 4096.0, None, ALU.mult)),
          waits=[("eBN", C("eBN"))], incs=[("eBN", 1)])
        E("vector", (lambda: nc.vector.tensor_scalar(
            VROW[:, :], pT[0:1, :], 1.0 / 4096.0, None, ALU.mult)),
          waits=[("eBN", C("eBN"))], incs=[("eBN", 1)])
        E("vector", (lambda: nc.vector.scalar_tensor_tensor(
            RSTR[:, :], MROW[:, :], -1.0, MROW[:, :], ALU.bypass, ALU.mult)),
          waits=[("eBN", C("eBN"))], incs=[("eBN", 1)])
        E("vector", (lambda: nc.vector.tensor_sub(VROW[:, :], VROW[:, :], RSTR[:, :])),
          waits=[("eBN", C("eBN"))], incs=[("eBN", 1)])
        E("vector", (lambda: nc.vector.tensor_scalar_add(VROW[:, :], VROW[:, :], EPS)),
          waits=[("eBN", C("eBN"))], incs=[("eBN", 1)])
        E("scalar", (lambda: nc.scalar.activation(RSTR[:, :], VROW[:, :], AF.Sqrt)),
          waits=[("eBN", C("eBN"))], incs=[("eRS", 1)])
        E("vector", (lambda: nc.vector.reciprocal(RSTR[:, :], RSTR[:, :])),
          waits=[("eRS", C("eRS"))], incs=[("eRS", 1)])
        E("tensor", (lambda: nc.tensor.matmul(
            pMM[:, :], WKR[:, :], NEGMB[:, :], start=False, stop=True)),
          waits=[("eBN", C("eBN"))], incs=[("eVM", 1)])
        E("tensor", (lambda: nc.tensor.matmul(
            pV[0][:, :], WVR[:, :], NEGMB[:, :], start=False, stop=True)),
          incs=[("eVM", 1)])
        w = [("eRS", C("eRS"))]
        if g > 0:
            w.append(("eBCe", C("eBCe")))
        E("tensor", (lambda: nc.tensor.matmul(
            pAV[:, :], ONEr[:, :], RSTR[:, :], start=True, stop=True)),
          waits=w, incs=[("eBC", 1)])
        E("vector", (lambda: nc.vector.tensor_copy(LIVB[:, :], pAV[:, :])),
          waits=[("eBC", C("eBC"))], incs=[("eBCe", 1)])
        E("vector", (lambda: nc.vector.tensor_mul(SGb[:, :], pMM[:, :], LIVB[:, :])),
          waits=[("eVM", C("eVM")), ("eBCe", C("eBCe"))], incs=[("eBN", 1)])
        E("vector", (lambda g=g: nc.vector.tensor_scalar(
            KT[:, 512 * g:512 * g + 512], SGb[:, :], BK[:, :], None, ALU.add)),
          waits=[("eBN", C("eBN"))], incs=[("eKVe", 1)])
        E("vector", (lambda: nc.vector.tensor_mul(SGb[:, :], pV[0][:, :], LIVB[:, :])),
          waits=[("eVM", C("eVM")), ("eBCe", C("eBCe")), ("eKVe", C("eKVe"))],
          incs=[("eBN", 1)])
        tp_banks = [pV[1], pV[2], pV[3], pV[1]]
        evac_cnt = C("eKVe")
        for i in range(4):
            w = [("eBN", C("eBN"))]
            if i == 3:
                w.append(("eKVe", evac_cnt + 1))
            def tf(i=i):
                return nc.tensor.transpose(tp_banks[i][:, 0:128],
                                           SGb[:, 128 * i:128 * i + 128],
                                           IDF[:, :])
            E("tensor", tf, waits=w, incs=[("eTp", 1)])
            def ef(g=g, i=i):
                return nc.vector.tensor_add(
                    VT[:, 128 * (4 * g + i):128 * (4 * g + i) + 128],
                    tp_banks[i][:, 0:128], BVBC[:, :])
            E("vector", ef, waits=[("eTp", C("eTp"))], incs=[("eKVe", 1)])
        for hq in range(4):
            hh = 4 * g + hq
            par = hh % 2
            war = [("eKVe", C("eKVe"))]
            if wq_prev[par] is not None:
                war.append((f"qSA{par}", wq_prev[par]))
            ck = build_slab(wq, 128 * hh, WTA[par], war_waits=war,
                            scr=wq_scr, scr_sem=f"qSA{par}",
                            tp_waits=[("eKVe", C("eKVe"))],
                            evac_eng=("vector" if hh % 2 == 0 else "scalar"))
            wq_prev[par] = ck[f"qSA{par}"]

    E("scalar", lambda: nc.scalar.activation(KSQ[:, :], KT[:, :], AF.Square),
      waits=[("eKVe", C("eKVe"))], incs=[("eKm", 1)])
    for g8 in range(8):
        w = [("eKm", C("eKm"))]
        def mf(g8=g8):
            return nc.tensor.matmul(pL[:, :], ONEb[:, :],
                                    KSQ[:, 512 * g8:512 * g8 + 512],
                                    start=True, stop=True)
        E("tensor", mf, waits=w, incs=[("eKm", 1)])
        E("vector", (lambda g8=g8: nc.vector.reduce_max(
            KM8[:, g8:g8 + 1], pL[0:1, :], AX.X)),
          waits=[("eKm", C("eKm"))], incs=[("eKm", 1)])
    E("vector", lambda: nc.vector.reduce_max(KSC[:, :], KM8[:, :], AX.X),
      waits=[("eKm", C("eKm"))], incs=[("eKm", 1)])
    E("vector", lambda: nc.vector.tensor_scalar_mul(
        KSC[:, :], KSC[:, :], -0.04419417382415922),
      waits=[("eKm", C("eKm"))], incs=[("eKm", 1)])
    if DBG:
        E("gpsimd", lambda: nc.gpsimd.dma_start(dKT[:, :], KT[:, :]),
          waits=[("eKVe", C("eKVe"))], incs=[("qDB", 1)])
        E("gpsimd", lambda: nc.gpsimd.dma_start(dVT[:, :], VT[:, :]),
          waits=[("eVt", C("eVt"))], incs=[("qDB", 1)])

    wq_ck = {}
    for p in range(NPASS):
        nk = 8 * p + 8
        rp, wpar = p % 2, (p + 1) % 2
        chain = p < 3

        sch.phase(f"xnb{p}")
        px_reset(PXV)
        fw = [("eCast", C("eCast")), ("eKV", C("eKV")), ("eBC", C("eBC")),
              ("eTp", max(wfb_tp[2], wfb_tp[3]))]
        ln_stream(x_own, TP * p, 4, kv=False, first_waits=fw)
        wfb_war[2] = wfb_war[3] = [("eXMa", C("eXMa"))]
        if DBG and p == 0:
            E("gpsimd", lambda: nc.gpsimd.dma_start(dXN[:, :], XN[:, :]),
              waits=[("eXNe", C("eXNe"))], incs=[("qDB", 1)])

        sch.phase(f"hst{p}")
        for dc in range(DC):
            def mf(dc=dc):
                return nc.tensor.matmul(pMM[:, :], A_T[:, 128 * dc:128 * dc + 128],
                                        XN[:, 512 * dc:512 * dc + 512],
                                        start=(dc == 0), stop=(dc == DC - 1))
            E("tensor", mf, incs=([("eHS", 1)] if dc == DC - 1 else []))
        E("vector", lambda: nc.vector.tensor_scalar(
            HSb[:, :], pMM[:, :], COEF[:, :], None, ALU.mult),
          waits=[("eHS", C("eHS"))], incs=[("eHS", 1)])
        if DBG and p == 0:
            E("gpsimd", lambda: nc.gpsimd.dma_start(dHS[:, :], HSb[:, :]),
              waits=[("eHS", C("eHS"))], incs=[("qDB", 1)])

        sch.phase(f"att{p}")
        px_reset([PXV[2], PXV[3]])
        B0 = {s: C(s) for s in semnames}
        jg = 0
        NPSp = 2 if (p == 0 and BUILD_IN_ATT0) else 3
        wq_gate = [("qSA0", C("qSA0")), ("qSA1", C("qSA1"))]

        def qload(h, p=p, wq_gate=wq_gate):
            w = list(wq_gate) if h < 2 else []
            if h >= 2:
                w.append(("eQ", B0["eQ"] + h - 1))
            E("sync", (lambda h=h: nc.sync.dma_start(
                WTA[h % 2][:, :], wq_scr[128 * h:128 * h + 128, :])),
              waits=w, incs=[(f"qWA{h % 2}", 1)])

        def qproj(h, p=p):
            w0 = [("eQev", B0["eQev"] + h)] if h >= 1 else []
            w0.append((f"qWA{h % 2}", C(f"qWA{h % 2}")))
            for dc in range(DC):
                def mf(h=h, dc=dc):
                    return nc.tensor.matmul(
                        pMM[:, :], WTA[h % 2][:, 128 * dc:128 * dc + 128],
                        XN[:, 512 * dc:512 * dc + 512],
                        start=(dc == 0), stop=(dc == DC - 1))
                E("tensor", mf, waits=(w0 if dc == 0 else []),
                  incs=([("eQ", 1)] if dc == DC - 1 else []))

        def qevac(h, p=p, chain=chain):
            E("scalar", (lambda h=h: nc.scalar.activation(
                QT[:, 512 * (h % 2):512 * (h % 2) + 512], pMM[:, :], AF.Identity,
                bias=BQ[:, h:h + 1], scale=1.0)),
              waits=[("eQ", B0["eQ"] + h + 1)], incs=[("eQev", 1)])
            if chain:
                w = [("eQev", C("eQev"))]
                if h >= 1:
                    w.append(("eB1", B0["eB1"] + h))
                E("scalar", (lambda h=h: nc.scalar.activation(
                    SQb[:, :], QT[:, 512 * (h % 2):512 * (h % 2) + 512], AF.Square)),
                  waits=w, incs=[("eSQ", 1)])

        wob_prev = {"qSO0": None, "qSF": None}

        def build_steps(h):
            war = [("qSF", wob_prev["qSF"])] if wob_prev["qSF"] else []
            if h < 1:
                war += [("qSA0", C("qSA0")), ("qSA1", C("qSA1"))]
            if wob_prev.get(f"qSB{h % 2}"):
                war.append((f"qSB{h % 2}", wob_prev[f"qSB{h % 2}"]))
            s1 = SlabSteps(wo, 128 * h, WTB, war, scr=wo_scr, scr_sem="qSO0",
                           piecewise=True, bt_oc=h, bt_par=h % 2,
                           bt_sem=f"qSB{h % 2}",
                           evac_eng=("vector" if h % 2 == 0 else "scalar"))
            while s1.remaining():
                s1.step()
                yield
            wob_prev["qSO0"] = s1.ck["qSO0"]
            wob_prev[f"qSB{h % 2}"] = s1.ck[f"qSB{h % 2}"]
            s2 = SlabSteps(wf, 128 * h, WTB, [("qSO0", wob_prev["qSO0"])],
                           scr=wf_scr, scr_sem="qSF", piecewise=True,
                           evac_eng=("scalar" if h % 2 == 0 else "vector"))
            while s2.remaining():
                s2.step()
                yield
            wob_prev["qSF"] = s2.ck["qSF"]

        qload(0)
        qload(1)
        qproj(0)
        qevac(0)

        for h in range(NH):
            qcol = 512 * (h % 2)
            filler = build_steps(h) if (p == 0 and BUILD_IN_ATT0) else iter(())
            if chain:
                w = [("eSQ", B0["eSQ"] + h + 1)]
                if h >= 1:
                    w.append(("eBCe", B0["eBCe"] + h))
                E("tensor", (lambda: nc.tensor.matmul(
                    pT[0:1, :], ONEb[:, :], SQb[:, :], start=True, stop=True)),
                  waits=w, incs=[("eB1", 1)])
                w = [("eB1", B0["eB1"] + h + 1)]
                if h >= 1:
                    w.append(("eB4", B0["eB4"] + h))
                E("vector", (lambda: nc.vector.reduce_max(QM2[:, :], pT[0:1, :], AX.X)),
                  waits=w, incs=[("eB2", 1)])
                w = [("eB2", B0["eB2"] + h + 1)]
                if h >= 1:
                    w.append(("eB5", B0["eB5"] + h))
                E("vector", (lambda: nc.vector.tensor_scalar(
                    NB1[:, :], QM2[:, :], -0.04419417382415922, KSC[:, :],
                    ALU.mult, ALU.add)),
                  waits=w, incs=[("eB4", 1)])
            if chain and p == 0 and h == 0:
                w = [("eB4", B0["eB4"] + h + 1), ("eB2", B0["eB2"] + h + 1)]
                E("tensor", (lambda: nc.tensor.matmul(
                    pT[:, 0:1], ONEr[:, :], NB1[:, :], start=True, stop=True)),
                  waits=w, incs=[("eB5", 1)])
                E("vector", (lambda h=h, wpar=wpar: nc.vector.tensor_copy(
                    BALL[:, 32 * wpar + h:32 * wpar + h + 1], pT[:, 0:1])),
                  waits=[("eB5", B0["eB5"] + h + 1)], incs=[("eB6", 1)])
                E("vector", (lambda h=h, rp=rp: nc.vector.tensor_copy(
                    BALL[:, 32 * rp + h:32 * rp + h + 1], pT[:, 0:1])),
                  waits=[("eB6", C("eB6"))], incs=[("eB6", 1)])

            if p == 0:
                nku = nk // 2
                jgp = jg // 2
                hb = max(h - 1, 0)
                for half in range(2):
                    w = [("eQev", B0["eQev"] + h + 1)] if half == 0 else []
                    if half == 0 and jgp >= 1:
                        w.append(("eExp", B0["eExp"] + jgp))
                    def sf(j=half, half=half, qcol=qcol):
                        return nc.tensor.matmul(
                            pSP[:, 512 * half:512 * half + 512],
                            KT[:, 128 * j:128 * j + 128],
                            QT[:, qcol:qcol + 512], start=True, stop=True)
                    E("tensor", sf, waits=w, incs=[("eSc", 1)])
                for u in range(nku):
                    ug = jgp + u
                    w = [("eSc", B0["eSc"] + jg + 2 * u + 2)]
                    if ug >= 3:
                        w.append(("eAV", B0["eAV"] + 2 * (ug - 3) + 2))
                        w.append(("ePr", B0["ePr"] + ug - 2))
                    if chain:
                        w.append(("eB6", B0["eB6"] + 2 * (hb + 1)))
                    def xf(ug=ug, hb=hb, rp=rp):
                        return nc.scalar.activation(
                            PT[:, 1024 * (ug % 3):1024 * (ug % 3) + 1024],
                            pSP[:, :], AF.Exp,
                            bias=BALL[:, 32 * rp + hb:32 * rp + hb + 1], scale=SC)
                    E("scalar", xf, waits=w, incs=[("eExp", 1)])
                    for half in range(2):
                        j = 2 * u + half
                        def mkf(ug=ug, half=half, jo=j):
                            c0 = 1024 * (ug % 3) + 512 * half
                            return nc.vector.tensor_mul(
                                PT[:, c0:c0 + 512], PT[:, c0:c0 + 512],
                                MK[:, 512 * jo:512 * jo + 512])
                        E("vector", mkf, waits=[("eExp", B0["eExp"] + ug + 1)],
                          incs=[("eMsk", 1)])
                    for half in range(2):
                        j = 2 * u + half
                        w2 = [("eMsk", B0["eMsk"] + 8 * h + j + 1)]
                        if j == 0 and h >= 1:
                            w2.append(("eAVe", B0["eAVe"] + h))
                        def avf(ug=ug, half=half, j=j, nk=nk):
                            c0 = 1024 * (ug % 3) + 512 * half
                            return nc.tensor.matmul(
                                pAV[:, :], VT[:, 128 * j:128 * j + 128],
                                PT[:, c0:c0 + 512],
                                start=(j == 0), stop=(j == nk - 1))
                        E("tensor", avf, waits=w2, incs=[("eAV", 1)])
                    w3 = [("eMsk", B0["eMsk"] + 8 * h + 2 * u + 1),
                          ("eMsk", B0["eMsk"] + 8 * h + 2 * u + 2)]
                    if ug >= 2:
                        w3.append(("eLS", B0["eLS"] + ug - 1))
                    def prf(ug=ug):
                        c0 = 1024 * (ug % 3)
                        return nc.vector.tensor_add(
                            PTS[:, 512 * (ug % 2):512 * (ug % 2) + 512],
                            PT[:, c0:c0 + 512], PT[:, c0 + 512:c0 + 1024])
                    E("vector", prf, waits=w3, incs=[("ePr", 1)])
                    w4 = [("ePr", B0["ePr"] + ug + 1)]
                    if u == 0 and h >= 1:
                        w4.append(("eRec", B0["eRec"] + 2 * h - 1))
                    def lf(ug=ug, u=u, nku=nku):
                        return nc.tensor.matmul(
                            pL[:, :], ONEb[:, :],
                            PTS[:, 512 * (ug % 2):512 * (ug % 2) + 512],
                            start=(u == 0), stop=(u == nku - 1))
                    E("tensor", lf, waits=w4, incs=[("eLS", 1)])
                    for _ in range(4):
                        next(filler, None)
                    if chain and u == 1 and h >= 1:
                        w6 = [("eB4", B0["eB4"] + h + 1),
                              ("eB2", B0["eB2"] + h + 1)]
                        E("tensor", (lambda: nc.tensor.matmul(
                            pT[:, 0:1], ONEr[:, :], NB1[:, :],
                            start=True, stop=True)),
                          waits=w6, incs=[("eB5", 1)])
                        E("vector", (lambda h=h, wpar=wpar: nc.vector.tensor_copy(
                            BALL[:, 32 * wpar + h:32 * wpar + h + 1], pT[:, 0:1])),
                          waits=[("eB5", B0["eB5"] + h + 1)], incs=[("eB6", 1)])
                        E("vector", (lambda h=h, rp=rp: nc.vector.tensor_copy(
                            BALL[:, 32 * rp + h:32 * rp + h + 1], pT[:, 0:1])),
                          waits=[("eB6", C("eB6"))], incs=[("eB6", 1)])
                    if h + 1 < NH:
                        for dcq in range(8 * u, 8 * u + 8):
                            w0q = []
                            if dcq == 0:
                                w0q = [("eQev", B0["eQev"] + h + 1),
                                       (f"qWA{(h + 1) % 2}",
                                        C(f"qWA{(h + 1) % 2}"))]
                            def mfq(h1=h + 1, dc=dcq):
                                return nc.tensor.matmul(
                                    pMM[:, :],
                                    WTA[h1 % 2][:, 128 * dc:128 * dc + 128],
                                    XN[:, 512 * dc:512 * dc + 512],
                                    start=(dc == 0), stop=(dc == DC - 1))
                            E("tensor", mfq, waits=w0q,
                              incs=([("eQ", 1)] if dcq == DC - 1 else []))
                    if u + 1 < nku:
                        for half in range(2):
                            j2 = 2 * (u + 1) + half
                            w5 = ([("eExp", B0["eExp"] + ug + 1)]
                                  if half == 0 else [])
                            def sf(j2=j2, half=half, qcol=qcol):
                                return nc.tensor.matmul(
                                    pSP[:, 512 * half:512 * half + 512],
                                    KT[:, 128 * j2:128 * j2 + 128],
                                    QT[:, qcol:qcol + 512], start=True, stop=True)
                            E("tensor", sf, waits=w5, incs=[("eSc", 1)])
            else:
                nku = nk // 2
                jgp = jg // 2
                for up in range(2):
                    for half in range(2):
                        jch = 2 * up + half
                        w = [("eQev", B0["eQev"] + h + 1)] if jch == 0 else []
                        if jgp + up >= 2 and half == 0:
                            w.append(("eExp", B0["eExp"] + jgp + up - 1))
                        def sf(j=jch, u=jgp + up, half=half, qcol=qcol):
                            bank = [pSP, pSQ][u % 2]
                            return nc.tensor.matmul(
                                bank[:, 512 * half:512 * half + 512],
                                KT[:, 128 * j:128 * j + 128],
                                QT[:, qcol:qcol + 512], start=True, stop=True)
                        E("tensor", sf, waits=w, incs=[("eSc", 1)])
                for u in range(nku):
                    ug = jgp + u
                    w = [("eSc", B0["eSc"] + jg + 2 * u + 2)]
                    if u >= 3:
                        w.append(("eAV", B0["eAV"] + jg + 2 * (u - 3) + 2))
                        w.append(("ePr", B0["ePr"] + jgp + u - 2))
                    def xf(ug=ug, h=h, rp=rp):
                        bank = [pSP, pSQ][ug % 2]
                        return nc.scalar.activation(
                            PT[:, 1024 * (ug % 3):1024 * (ug % 3) + 1024],
                            bank[:, :], AF.Exp,
                            bias=BALL[:, 32 * rp + h:32 * rp + h + 1], scale=SC)
                    E("scalar", xf, waits=w, incs=[("eExp", 1)])
                    for half in range(2):
                        j = 2 * u + half
                        if j >= nk - 8:
                            jo = j - (nk - 8)
                            def mkf(ug=ug, half=half, jo=jo):
                                c0 = 1024 * (ug % 3) + 512 * half
                                return nc.vector.tensor_mul(
                                    PT[:, c0:c0 + 512], PT[:, c0:c0 + 512],
                                    MK[:, 512 * jo:512 * jo + 512])
                            E("vector", mkf, waits=[("eExp", B0["eExp"] + ug + 1)],
                              incs=[("eMsk", 1)])
                    for half in range(2):
                        j = 2 * u + half
                        if j >= nk - 8:
                            w2 = [("eMsk", B0["eMsk"] + 8 * h + (j - (nk - 8)) + 1)]
                        else:
                            w2 = [("eExp", B0["eExp"] + ug + 1)]
                        if j == 0 and h >= 1:
                            w2.append(("eAVe", B0["eAVe"] + h))
                        def avf(ug=ug, half=half, j=j, nk=nk):
                            c0 = 1024 * (ug % 3) + 512 * half
                            return nc.tensor.matmul(
                                pAV[:, :], VT[:, 128 * j:128 * j + 128],
                                PT[:, c0:c0 + 512],
                                start=(j == 0), stop=(j == nk - 1))
                        E("tensor", avf, waits=w2, incs=[("eAV", 1)])
                    w3 = []
                    for half in range(2):
                        j = 2 * u + half
                        if j >= nk - 8:
                            w3.append(("eMsk", B0["eMsk"] + 8 * h + (j - (nk - 8)) + 1))
                        else:
                            w3.append(("eExp", B0["eExp"] + ug + 1))
                    if ug >= 2:
                        w3.append(("eLS", B0["eLS"] + ug - 1))
                    def prf(ug=ug):
                        c0 = 1024 * (ug % 3)
                        return nc.vector.tensor_add(
                            PTS[:, 512 * (ug % 2):512 * (ug % 2) + 512],
                            PT[:, c0:c0 + 512], PT[:, c0 + 512:c0 + 1024])
                    E("vector", prf, waits=w3, incs=[("ePr", 1)])
                    w4 = [("ePr", B0["ePr"] + ug + 1)]
                    if u == 0 and h >= 1:
                        w4.append(("eRec", B0["eRec"] + 2 * h - 1))
                    def lf(ug=ug, u=u, nku=nku):
                        return nc.tensor.matmul(
                            pL[:, :], ONEb[:, :],
                            PTS[:, 512 * (ug % 2):512 * (ug % 2) + 512],
                            start=(u == 0), stop=(u == nku - 1))
                    E("tensor", lf, waits=w4, incs=[("eLS", 1)])
                    if u + 2 < nku:
                        for half in range(2):
                            j2 = 2 * (u + 2) + half
                            w5 = [("eExp", B0["eExp"] + ug + 1)] if half == 0 else []
                            def sf(j2=j2, u2=jgp + u + 2, half=half, qcol=qcol):
                                bank = [pSP, pSQ][u2 % 2]
                                return nc.tensor.matmul(
                                    bank[:, 512 * half:512 * half + 512],
                                    KT[:, 128 * j2:128 * j2 + 128],
                                    QT[:, qcol:qcol + 512], start=True, stop=True)
                            E("tensor", sf, waits=w5, incs=[("eSc", 1)])
                    if chain and u == 1:
                        w6 = [("eB4", B0["eB4"] + h + 1), ("eB2", B0["eB2"] + h + 1)]
                        E("tensor", (lambda: nc.tensor.matmul(
                            pT[:, 0:1], ONEr[:, :], NB1[:, :], start=True, stop=True)),
                          waits=w6, incs=[("eB5", 1)])

            if chain and p > 0:
                E("vector", (lambda h=h, wpar=wpar: nc.vector.tensor_copy(
                    BALL[:, 32 * wpar + h:32 * wpar + h + 1], pT[:, 0:1])),
                  waits=[("eB5", B0["eB5"] + h + 1)], incs=[("eB6", 1)])

            w = [("eAV", B0["eAV"] + jg + nk)]
            if h >= 1:
                w.append(("eATw", B0["eATw"] + h))
            E("vector", (lambda: nc.vector.tensor_copy(AUN[:, :], pAV[:, :])),
              waits=w, incs=[("eAVe", 1)])
            w = [("eLS", B0["eLS"] + (jg + nk) // 2)]
            if h >= 1:
                w.append(("eBC", B0["eBC"] + h))
            E("vector", (lambda: nc.vector.tensor_copy(LIV[:, :], pL[0:1, :])),
              waits=w, incs=[("eRec", 1)])
            E("vector", (lambda: nc.vector.reciprocal(LIV[:, :], LIV[:, :])),
              waits=[("eRec", C("eRec"))], incs=[("eRec", 1)])

            if h + 1 < NH:
                if h + 2 < NH:
                    qload(h + 2)
                if p != 0:
                    qproj(h + 1)

            w = [("eRec", B0["eRec"] + 2 * h + 2)]
            if chain:
                w.append(("eB6", B0["eB6"] + (2 * (h + 1) if p == 0 else h + 1)))
            elif h >= 1:
                w.append(("eBCe", B0["eBCe"] + h))
            E("tensor", (lambda: nc.tensor.matmul(
                pT[:, :], ONEr[:, :], LIV[:, :], start=True, stop=True)),
              waits=w, incs=[("eBC", 1)])
            E("vector", (lambda: nc.vector.tensor_copy(LIVB[:, :], pT[:, :])),
              waits=[("eBC", B0["eBC"] + h + 1)], incs=[("eBCe", 1)])
            E("vector", (lambda h=h: nc.vector.tensor_mul(
                ATT[:, 512 * h:512 * h + 512], AUN[:, :], LIVB[:, :])),
              waits=[("eBCe", B0["eBCe"] + h + 1),
                     ("eAVe", B0["eAVe"] + h + 1)], incs=[("eATw", 1)])

            if h + 1 < NH:
                qevac(h + 1)
            for _ in filler:
                pass
            jg += nk

        if DBG and p == 0:
            E("gpsimd", lambda: nc.gpsimd.dma_start(dAT[:, :], ATT[:, :]),
              waits=[("eATw", C("eATw"))], incs=[("qDB", 1)])

        if p == 0 and not BUILD_IN_ATT0:
            prevt = {"qSF": None, "qSO0": None, "qSB0": None, "qSB1": None}
            for oc in range(DC):
                war = [("qSF", prevt["qSF"])] if prevt["qSF"] else []
                if oc < 1:
                    war += [("qSA0", C("qSA0")), ("qSA1", C("qSA1"))]
                if prevt.get(f"qSB{oc % 2}"):
                    war.append((f"qSB{oc % 2}", prevt[f"qSB{oc % 2}"]))
                ck = build_slab(wo, 128 * oc, WTB, war_waits=war,
                                scr=wo_scr, scr_sem="qSO0", piecewise=True,
                                bt_oc=oc, bt_par=oc % 2, bt_sem=f"qSB{oc % 2}",
                                evac_eng=("vector" if oc % 2 == 0 else "scalar"))
                prevt["qSO0"] = ck["qSO0"]
                prevt[f"qSB{oc % 2}"] = ck[f"qSB{oc % 2}"]
                war = [("qSO0", prevt["qSO0"])]
                ck = build_slab(wf, 128 * oc, WTB, war_waits=war,
                                scr=wf_scr, scr_sem="qSF", piecewise=True,
                                evac_eng=("scalar" if oc % 2 == 0 else "vector"))
                prevt["qSF"] = ck["qSF"]

        sch.phase(f"epi{p}")
        B1 = {s: C(s) for s in semnames}
        w_gate = [(s_, C(s_)) for s_ in
                  ("qSO0", "qSO1", "qSF", "qSB0", "qSB1")]

        def eload(oc, p=p, w_gate=w_gate):
            w = list(w_gate) if oc < 2 else []
            if oc >= 2:
                w.append(("eS", B1["eS"] + oc - 1))
            E("sync", (lambda oc=oc: nc.sync.dma_start(
                WTA[oc % 2][:, :], wo_scr[128 * oc:128 * oc + 128, :])),
              waits=w, incs=[(f"qWA{oc % 2}", 1)])
            w = list(w_gate) if oc < 1 else []
            if oc >= 1:
                w.append(("eG", B1["eG"] + oc))
            E("sync", (lambda oc=oc: nc.sync.dma_start(
                WTB[:, :], wf_scr[128 * oc:128 * oc + 128, :])),
              waits=w, incs=[("qWB", 1)])
            w = list(w_gate) if oc < 2 else []
            if oc >= 2:
                w.append(("eG", B1["eG"] + oc - 1))
            E("sync", (lambda oc=oc: nc.sync.dma_start(
                BTT[oc % 2][:, :], bt_scr[128 * oc:128 * oc + 128, :])),
              waits=w, incs=[(f"qBT{oc % 2}", 1)])

        def xrload(oc, p=p):
            for i in range(4):
                w = [("eOS", C("eOS"))] if oc >= 1 else []
                def lf(oc=oc, i=i):
                    return nc.sync.dma_start(
                        XRb[i][:, :],
                        x_own[TP * p + 128 * i:TP * p + 128 * i + 128,
                              128 * oc:128 * oc + 128])
                E("sync", lf, waits=w, incs=[(f"qR{oc % 2}", 1)])

        def sgtrans(oc, p=p):
            for i in range(4):
                w = [("eSG", B1["eSG"] + oc + 1)] if i == 0 else []
                if i == 0 and oc >= 1:
                    w.append(("eOS", B1["eOS"] + 4 * oc))
                def tf(i=i):
                    return nc.tensor.transpose(
                        pAV[:, 128 * i:128 * i + 128], SGb[:, 128 * i:128 * i + 128],
                        IDF[:, :])
                E("tensor", tf, waits=w, incs=[("eTsg", 1)])
            for i in range(4):
                w = [("eTsg", B1["eTsg"] + 4 * oc + 4),
                     (f"qR{oc % 2}", C(f"qR{oc % 2}")),
                     ("qO0", C("qO0")), ("qO1", C("qO1"))]
                def vf(oc=oc, i=i):
                    return nc.vector.tensor_add(
                        OSb[i][:, :], pAV[:, 128 * i:128 * i + 128], XRb[i][:, :])
                E("vector", vf, waits=w, incs=[("eOS", 1)])
            for i in range(4):
                def of(oc=oc, i=i, p=p):
                    return nc.gpsimd.dma_start(
                        outp[TP * p + 128 * i:TP * p + 128 * i + 128,
                             128 * oc:128 * oc + 128],
                        OSb[i][:, :])
                E("gpsimd", of, waits=[("eOS", B1["eOS"] + 4 * oc + i + 1)],
                  incs=[(f"qO{oc % 2}", 1)])

        eload(0)
        xrload(0)

        for oc in range(DC):
            w0 = [("eGOe", B1["eGOe"] + oc)] if oc >= 1 else []
            w0.append(("qWB", C("qWB")))
            w0.append((f"qBT{oc % 2}", C(f"qBT{oc % 2}")))
            for dc in range(DC):
                def gf(dc=dc):
                    return nc.tensor.matmul(
                        pMM[:, :], WTB[:, 128 * dc:128 * dc + 128],
                        XN[:, 512 * dc:512 * dc + 512],
                        start=(dc == 0), stop=False)
                E("tensor", gf, waits=(w0 if dc == 0 else []))
            def gl(oc=oc):
                return nc.tensor.matmul(pMM[:, :], BTT[oc % 2][:, :], HSb[:, :],
                                        start=False, stop=True)
            E("tensor", gl, incs=[("eG", 1)])
            w1 = [("eSTe", B1["eSTe"] + oc)] if oc >= 1 else []
            w1.append((f"qWA{oc % 2}", C(f"qWA{oc % 2}")))
            for hc in range(DC):
                def sf(hc=hc, oc=oc):
                    return nc.tensor.matmul(
                        pT[:, :], WTA[oc % 2][:, 128 * hc:128 * hc + 128],
                        ATT[:, 512 * hc:512 * hc + 512],
                        start=(hc == 0), stop=(hc == DC - 1))
                E("tensor", sf, waits=(w1 if hc == 0 else []),
                  incs=([("eS", 1)] if hc == DC - 1 else []))
            w = [("eG", B1["eG"] + oc + 1)]
            if oc >= 1:
                w.append(("eSG", B1["eSG"] + oc))
            E("scalar", (lambda oc=oc: nc.scalar.activation(
                GOb[:, :], pMM[:, :], AF.Silu, bias=BF_[:, oc:oc + 1], scale=1.0)),
              waits=w, incs=[("eGOe", 1)])
            w = [("eS", B1["eS"] + oc + 1)]
            if oc >= 2:
                w.append(("eSG", B1["eSG"] + oc - 1))
            E("scalar", (lambda oc=oc: nc.scalar.activation(
                STb[oc % 2][:, :], pT[:, :], AF.Identity, bias=BO[:, oc:oc + 1], scale=1.0)),
              waits=w, incs=[("eSTe", 1)])
            w = [("eGOe", B1["eGOe"] + oc + 1), ("eSTe", B1["eSTe"] + oc + 1)]
            if oc >= 1:
                w.append(("eTsg", B1["eTsg"] + 4 * oc))
            E("vector", (lambda oc=oc: nc.vector.tensor_mul(
                SGb[:, :], STb[oc % 2][:, :], GOb[:, :])),
              waits=w, incs=[("eSG", 1)])
            if oc + 1 < DC:
                eload(oc + 1)
            sgtrans(oc)
            if oc + 1 < DC:
                xrload(oc + 1)
        if p == 0:
            scr_gate = [(s_, C(s_)) for s_ in
                        ("qSA0", "qSA1", "qSO0", "qSO1", "qSF", "qSB0", "qSB1")]

    sch.phase("fin")
    fw = [("qO0", C("qO0")), ("qO1", C("qO1"))]
    if C("qDB"):
        fw.append(("qDB", C("qDB")))
    E("gpsimd", None, waits=fw)

    phases = sch.phases if PHASE_LIMIT is None else sch.phases[:PHASE_LIMIT]
    if OP_LIMIT is not None:
        nm, ops0 = phases[-1]
        phases = phases[:-1] + [(nm, [(e, f, tuple(w for w in ws if _op_ok(w, ops0[:OP_LIMIT])), i)
                                      for (e, f, ws, i) in ops0[:OP_LIMIT]])]
    for name, ops in phases:
        engs = [e for e in ENGS if any(o[0] == e for o in ops)]
        if not engs:
            continue
        with nc.Block(name) as blk:
            for eng in engs:
                myops = [(fn, w, i) for (e, fn, w, i) in ops if e == eng]

                def body(engine, myops=myops):
                    for fn, waits, incs in myops:
                        for s_, v in waits:
                            engine.wait_ge(S[s_], 16 * v if s_ in DMASEMS else v)
                        if fn is None:
                            continue
                        ins = fn()
                        for s_, n in incs:
                            ins = ins.then_inc(S[s_], 16 * n if s_ in DMASEMS else n)

                getattr(blk, eng)(body)
    return nc, ctx


_CACHE = {}


def _get_nc():
    if "nc" not in _CACHE:
        nc, ctx = _build()
        _CACHE["nc"] = nc
        _CACHE["ctx"] = ctx
    return _CACHE["nc"]


def make_core_inputs(L0_attn_x, L0_attn_wq_w, L0_attn_wq_b, L0_attn_wk_w,
                     L0_attn_wk_b, L0_attn_wv_w, L0_attn_wv_b, L0_attn_wo_w,
                     L0_attn_wo_b, L0_ffn_lora_A, L0_ffn_lora_B, L0_ffn_W,
                     L0_ffn_bias, L0_ffn_lora_weights):
    x = np.ascontiguousarray(np.asarray(L0_attn_x, dtype=np.float32))
    f32 = lambda a: np.ascontiguousarray(np.asarray(a, dtype=np.float32))
    shared = {
        "wq": f32(L0_attn_wq_w), "wk": f32(L0_attn_wk_w), "wv": f32(L0_attn_wv_w),
        "wo": f32(L0_attn_wo_w), "wf": f32(L0_ffn_W),
        "bqv": f32(L0_attn_wq_b).reshape(32, 128),
        "bov": f32(L0_attn_wo_b).reshape(32, 128),
        "bfv": f32(L0_ffn_bias).reshape(32, 128),
        "bkv": f32(L0_attn_wk_b).reshape(1, 128),
        "bvv": f32(L0_attn_wv_b).reshape(1, 128),
        "av": f32(L0_ffn_lora_A).reshape(128, D),
        "bv_": f32(L0_ffn_lora_B),
        "lwr": np.ascontiguousarray(
            np.repeat(np.asarray(L0_ffn_lora_weights, dtype=np.float32), 16)[:, None]),
        "idbd": np.eye(128, dtype=ml_dtypes.bfloat16),
        "idfd": np.eye(128, dtype=np.float32),
    }
    kk = np.arange(128)[:, None]
    qq = np.arange(256)[None, :]
    tri = (kk <= qq).astype(np.float32)
    tri1 = (kk + 128 <= qq).astype(np.float32)
    ones = np.ones((128, 256), np.float32)
    zero = np.zeros((128, 256), np.float32)

    def mk(h):
        if h == 0:
            pat = [(tri, ones), (tri1, ones), (zero, ones), (zero, ones),
                   (zero, tri), (zero, tri1), (zero, zero), (zero, zero)]
        else:
            pat = [(ones, ones), (ones, ones), (tri, ones), (tri1, ones),
                   (zero, ones), (zero, ones), (zero, tri), (zero, tri1)]
        m = np.stack([np.concatenate(pq, axis=1) for pq in pat])
        return np.ascontiguousarray(m.astype(ml_dtypes.bfloat16))

    masks = [mk(0), mk(1)]
    in_maps = []
    xaT_cache = {}
    for c in range(8):
        b, h = c // 2, c % 2
        xb = x[b]
        if b not in xaT_cache:
            xaT_cache[b] = np.ascontiguousarray(xb.T)
        xaT = xaT_cache[b]
        m = dict(shared)
        m["x_all"] = xb
        m["x_allT"] = xaT
        m["x_own"] = np.ascontiguousarray(xb.reshape(16, 256, D)[h::2].reshape(TOWN, D))
        m["maskv"] = masks[h]
        in_maps.append(m)
    return in_maps


def kernel(**inputs):
    nc = _get_nc()
    in_maps = make_core_inputs(**inputs)
    res = run_bass_kernel_spmd(nc, in_maps, core_ids=list(range(8)))
    out = np.empty((4, TALL, D), dtype=np.float32)
    for c in range(8):
        b, h = c // 2, c % 2
        r = np.asarray(res.results[c]["out_own"], dtype=np.float32)
        out[b].reshape(16, 256, D)[h::2] = r.reshape(8, 256, D)
    return out


# version 22
# speedup vs baseline: 1.0266x; 1.0266x over previous
import numpy as np
import ml_dtypes
from contextlib import ExitStack

import concourse.bass as bass
import concourse.mybir as mybir
from concourse.bass_utils import run_bass_kernel_spmd

F32 = mybir.dt.float32
BF16 = mybir.dt.bfloat16
AF = mybir.ActivationFunctionType
ALU = mybir.AluOpType
AX = mybir.AxisListType

D = 4096
DC = 32
HD = 128
NH = 32
TP = 512
NPASS = 4
TALL = 4096
TOWN = 2048
SC = 0.08838834764831845
EPS = 1e-5
NPS = 3
NPT = 6

ENGS = ("sync", "scalar", "vector", "tensor", "gpsimd")
BUILD_IN_ATT0 = True
DBG = False
PHASE_LIMIT = None
OP_LIMIT = None


def _op_ok(wait, ops):
    s, v = wait
    tot = 0
    for (_, _, _, incs) in ops:
        for s2, n in incs:
            if s2 == s:
                tot += n
    return v <= tot


class Sched:
    def __init__(self):
        self.phases = []
        self.ops = None
        self.cnt = {}

    def phase(self, name):
        self.ops = []
        self.phases.append((name, self.ops))

    def emit(self, eng, fn, waits=(), incs=()):
        for s, v in waits:
            have = self.cnt.get(s, 0)
            assert v <= have, f"wait {s}>={v} but only {have} scheduled"
        self.ops.append((eng, fn, tuple(waits), tuple(incs)))
        for s, n in incs:
            self.cnt[s] = self.cnt.get(s, 0) + n

    def c(self, s):
        return self.cnt.get(s, 0)


def _build():
    nc = bass.Bass("TRN2", target_bir_lowering=False)

    x_all = nc.dram_tensor("x_all", [TALL, D], F32, kind="ExternalInput")
    x_own = nc.dram_tensor("x_own", [TOWN, D], F32, kind="ExternalInput")
    x_allT = nc.dram_tensor("x_allT", [D, TALL], F32, kind="ExternalInput")
    wq = nc.dram_tensor("wq", [D, D], F32, kind="ExternalInput")
    wk = nc.dram_tensor("wk", [HD, D], F32, kind="ExternalInput")
    wv = nc.dram_tensor("wv", [HD, D], F32, kind="ExternalInput")
    wo = nc.dram_tensor("wo", [D, D], F32, kind="ExternalInput")
    wf = nc.dram_tensor("wf", [D, D], F32, kind="ExternalInput")
    bqv = nc.dram_tensor("bqv", [32, 128], F32, kind="ExternalInput")
    bov = nc.dram_tensor("bov", [32, 128], F32, kind="ExternalInput")
    bfv = nc.dram_tensor("bfv", [32, 128], F32, kind="ExternalInput")
    bkv = nc.dram_tensor("bkv", [1, 128], F32, kind="ExternalInput")
    bvv = nc.dram_tensor("bvv", [1, 128], F32, kind="ExternalInput")
    av = nc.dram_tensor("av", [128, D], F32, kind="ExternalInput")
    bv_ = nc.dram_tensor("bv_", [8, D, 16], F32, kind="ExternalInput")
    lwr = nc.dram_tensor("lwr", [128, 1], F32, kind="ExternalInput")
    maskv = nc.dram_tensor("maskv", [8, 128, 512], BF16, kind="ExternalInput")
    idbd = nc.dram_tensor("idbd", [128, 128], BF16, kind="ExternalInput")
    idfd = nc.dram_tensor("idfd", [128, 128], F32, kind="ExternalInput")
    outp = nc.dram_tensor("out_own", [TOWN, D], F32, kind="ExternalOutput")
    wq_scr = nc.dram_tensor("wq_scr", [D, D], BF16, kind="Internal")
    wo_scr = nc.dram_tensor("wo_scr", [D, D], BF16, kind="Internal")
    wf_scr = nc.dram_tensor("wf_scr", [D, D], BF16, kind="Internal")
    bt_scr = nc.dram_tensor("bt_scr", [D, 128], BF16, kind="Internal")
    if DBG:
        dKT = nc.dram_tensor("dKT", [128, TALL], BF16, kind="ExternalOutput")
        dVT = nc.dram_tensor("dVT", [128, TALL], BF16, kind="ExternalOutput")
        dXN = nc.dram_tensor("dXN", [128, DC * TP], BF16, kind="ExternalOutput")
        dAT = nc.dram_tensor("dAT", [128, NH * TP], BF16, kind="ExternalOutput")
        dHS = nc.dram_tensor("dHS", [128, TP], BF16, kind="ExternalOutput")

    ctx = ExitStack()
    sb = lambda n, sh, dt: ctx.enter_context(nc.sbuf_tensor(n, sh, dt))
    psr = lambda n, sh, dt: ctx.enter_context(nc.psum_tensor(n, sh, dt))

    KT = sb("KT", [128, TALL], BF16)
    VT = sb("VT", [128, TALL], BF16)
    XN = sb("XN", [128, DC * TP], BF16)
    ATT = sb("ATT", [128, NH * TP], BF16)
    A_T = sb("A_T", [128, D], BF16)
    QT = sb("QT", [128, 2 * TP], BF16)
    SQb = sb("SQb", [128, TP], BF16)
    PT = sb("PT", [128, NPT * TP], BF16)
    HSb = sb("HSb", [128, TP], BF16)
    PTS = sb("PTS", [128, 2 * TP], BF16)
    AUN = sb("AUN", [128, TP], BF16)
    WTA = [sb(f"WTA{i}", [128, D], BF16) for i in range(2)]
    WTB = sb("WTB", [128, D], BF16)
    BTT = [sb(f"BTT{i}", [128, 128], BF16) for i in range(2)]
    WFb = [sb(f"WFb{i}", [128, 512], F32) for i in range(2)]
    BSF = sb("BSF", [128, 128], F32)
    XF0 = [sb(f"XF0{i}", [128, 2048], F32) for i in range(2)]
    XF1 = [sb(f"XF1{i}", [128, 2048], F32) for i in range(2)]
    _xf1b_addr = nc.lookup_mloc(XF1[1]).addr
    WFb += [nc.alloc_sbuf_tensor_at(f"WFb{i}", [128, 512], F32,
                                    offset=_xf1b_addr + 2048 * (i - 2))
            for i in (2, 3)]
    XMf = sb("XMf", [128, D], F32)
    STb = [sb(f"STb{i}", [128, TP], F32) for i in range(2)]
    GOb = sb("GOb", [128, TP], F32)
    SGb = sb("SGb", [128, TP], F32)
    OSb = [sb(f"OSb{i}", [128, 128], F32) for i in range(4)]
    XRb = [sb(f"XRb{i}", [128, 128], F32) for i in range(4)]
    BQ = sb("BQ", [128, 32], F32)
    BO = sb("BO", [128, 32], F32)
    BF_ = sb("BF_", [128, 32], F32)
    BK = sb("BK", [128, 1], F32)
    BVc = sb("BVc", [128, 1], F32)
    COEF = sb("COEF", [128, 1], F32)
    MK = sb("MK", [128, 8 * 512], BF16)
    IDB = sb("IDB", [128, 128], BF16)
    IDF = sb("IDF", [128, 128], F32)
    ONEb = sb("ONEb", [128, 1], BF16)
    ONEr = sb("ONEr", [1, 128], F32)
    ONEbr = sb("ONEbr", [1, 128], BF16)
    BVB = sb("BVB", [1, 128], BF16)
    BALL = sb("BALL", [128, 64], F32)
    BN6 = [sb(f"BN6{i}", [128, 48], F32) for i in range(2)]
    BN2 = [sb(f"BN2{i}", [128, 2], F32) for i in range(2)]
    RST = [sb(f"RST{i}", [128, 1], F32) for i in range(2)]
    NMR = [sb(f"NMR{i}", [128, 1], F32) for i in range(2)]
    QM2 = sb("QM2", [1, 1], F32)
    BT1 = sb("BT1", [1, 1], F32)
    NB1 = sb("NB1", [1, 1], F32)
    KM8 = sb("KM8", [1, 8], F32)
    KSC = sb("KSC", [1, 1], F32)
    LIV = sb("LIV", [1, TP], F32)
    LIVB = sb("LIVB", [128, TP], F32)
    BST = sb("BST", [32, 128], F32)
    WKR = sb("WKR", [1, 128], BF16)
    WVR = sb("WVR", [1, 128], BF16)
    BVBC = sb("BVBC", [128, 128], F32)
    RSTC = sb("RSTC", [128, 4], F32)
    KSQ = ATT[:, 4096:8192]
    WKT = ATT[:, 8192:12288]
    WVT = ATT[:, 12288:16384]

    pSP = psr("pSP", [128, 1024], F32)
    pSQ = psr("pSQ", [128, 1024], F32)
    pAV = psr("pAV", [128, 512], F32)
    pL = psr("pL", [1, 512], F32)
    pMM = psr("pMM", [128, 512], F32)
    pT = psr("pT", [128, 512], F32)
    PXV = [pSP[:, 0:512], pSP[:, 512:1024], pSQ[:, 0:512], pSQ[:, 512:1024]]

    semnames = ["qX00", "qX01", "qX10", "qX11", "qWF0", "qWF1", "qWF2", "qWF3", "qBS",
                "qWA0", "qWA1", "qWB", "qBT0", "qBT1", "qR0", "qR1",
                "qSA0", "qSA1", "qSO0", "qSO1", "qSF", "qSB0", "qSB1",
                "qO0", "qO1", "qDB", "qI",
                "eTp", "eWTe", "eWTs", "eWTp", "eXNe", "eXNs", "eVM", "eXMa", "eBN", "eRS",
                "eCast",
                "eKV", "eKVe", "eQ", "eQev", "eSQ",
                "eB1", "eB2", "eB3", "eB4", "eB5", "eB6",
                "eSc", "eExp", "eMsk", "eAV", "eLS", "ePr",
                "eRec", "eBC", "eBCe", "eATw", "eAVe",
                "eS", "eSTe", "eG", "eGOe", "eSG", "eTsg", "eOS",
                "eHS", "eInit", "eVt", "eKm"]
    S = {n: ctx.enter_context(nc.semaphore(n)) for n in semnames}
    DMASEMS = {n for n in semnames if n.startswith("q")}

    sch = Sched()
    E, C = sch.emit, sch.c

    px_state = {"banks": None, "evts": []}
    wfb_tp = {0: 0, 1: 0, 2: 0, 3: 0}
    wfb_war = {2: [], 3: []}
    wfb_par0 = [0]

    def px_reset(banks):
        px_state["banks"] = banks
        px_state["evts"] = []

    def px_begin():
        banks, evts = px_state["banks"], px_state["evts"]
        n = len(evts)
        d = len(banks)
        w = [evts[n - d]] if n >= d else []
        return banks[n % d], w

    def px_end(sem):
        px_state["evts"].append((sem, C(sem)))

    class SlabSteps:

        def __init__(self, wsrc, row0, dst_wt, war_waits, scr=None, scr_sem=None,
                     bt_oc=None, bt_par=None, bt_sem=None, evac_eng="vector",
                     piecewise=False, tp_waits=()):
            self.wsrc, self.row0, self.dst_wt = wsrc, row0, dst_wt
            self.war = list(war_waits)
            self.scr, self.scr_sem = scr, scr_sem
            self.bt_oc, self.bt_par, self.bt_sem = bt_oc, bt_par, bt_sem
            self.evac_eng, self.piecewise = evac_eng, piecewise
            self.tp_waits = list(tp_waits)
            self.esem = {"vector": "eWTe", "scalar": "eWTs",
                         "gpsimd": "eWTp"}[evac_eng]
            self.par0 = wfb_par0[0]
            wfb_par0[0] = (self.par0 + 8) % 4
            self.total = 8 + (1 if bt_oc is not None else 0)
            self.g = 0
            self.ck = {}

        def remaining(self):
            return self.total - self.g

        def step(self):
            g = self.g
            self.g += 1
            if g < 8:
                par = (self.par0 + g) % 4
                qsem = f"qWF{par}"
                lw = [("eTp", wfb_tp[par])] + wfb_war.get(par, [])
                if par in wfb_war:
                    wfb_war[par] = []
                E("sync", (lambda par=par, g=g, wsrc=self.wsrc, row0=self.row0:
                           nc.sync.dma_start(
                    WFb[par][:, :], wsrc[row0:row0 + 128, 512 * g:512 * g + 512])),
                  waits=lw, incs=[(qsem, 1)])
                bank, pxw = px_begin()
                tw = [(qsem, C(qsem))] + pxw
                if g < 4:
                    tw += self.tp_waits
                for t4 in range(4):
                    def tf(par=par, t4=t4, bank=bank):
                        return nc.tensor.transpose(
                            bank[:, 128 * t4:128 * t4 + 128],
                            WFb[par][:, 128 * t4:128 * t4 + 128], IDF[:, :])
                    E("tensor", tf, waits=(tw if t4 == 0 else []),
                      incs=([("eTp", 1)] if t4 == 3 else []))
                wfb_tp[par] = C("eTp")
                dc0 = g * 4
                ee, es = self.evac_eng, self.esem
                def ef(bank=bank, dc0=dc0, dst_wt=self.dst_wt, ee=ee):
                    if ee == "vector":
                        return nc.vector.tensor_copy(
                            dst_wt[:, 128 * dc0:128 * dc0 + 512], bank[:, :])
                    if ee == "gpsimd":
                        return nc.gpsimd.tensor_copy(
                            dst_wt[:, 128 * dc0:128 * dc0 + 512], bank[:, :])
                    return nc.scalar.copy(
                        dst_wt[:, 128 * dc0:128 * dc0 + 512], bank[:, :])
                E(ee, ef, waits=[("eTp", C("eTp"))] + self.war, incs=[(es, 1)])
                px_end(es)
                if self.scr is not None and self.piecewise:
                    E("gpsimd", (lambda scr=self.scr, row0=self.row0, dc0=dc0,
                                 dst_wt=self.dst_wt: nc.gpsimd.dma_start(
                        scr[row0:row0 + 128, 128 * dc0:128 * dc0 + 512],
                        dst_wt[:, 128 * dc0:128 * dc0 + 512])),
                      waits=[(es, C(es))], incs=[(self.scr_sem, 1)])
                if g == 7:
                    if self.scr is not None and not self.piecewise:
                        E("gpsimd", (lambda scr=self.scr, row0=self.row0,
                                     dst_wt=self.dst_wt: nc.gpsimd.dma_start(
                            scr[row0:row0 + 128, :], dst_wt[:, :])),
                          waits=[(self.esem, C(self.esem))], incs=[(self.scr_sem, 1)])
                    if self.scr is not None:
                        self.ck[self.scr_sem] = C(self.scr_sem)
                    self.ck["eWTe"] = C("eWTe")
                    self.ck["eWTe2"] = C("eWTe")
            else:
                E("sync", (lambda bt_oc=self.bt_oc: nc.sync.dma_start(
                    BSF[:, :].rearrange("p (n r) -> p n r", n=8),
                    bv_[:, 128 * bt_oc:128 * bt_oc + 128, :].rearrange(
                        "n o r -> o n r"))),
                  waits=[("eTp", C("eTp"))], incs=[("qBS", 1)])
                bank, pxw = px_begin()
                E("tensor", (lambda bank=bank: nc.tensor.transpose(
                    bank[:, 0:128], BSF[:, :], IDF[:, :])),
                  waits=[("qBS", C("qBS"))] + pxw, incs=[("eTp", 1)])
                E("vector", (lambda bt_par=self.bt_par, bank=bank:
                             nc.vector.tensor_copy(
                    BTT[bt_par][:, :], bank[:, 0:128])),
                  waits=[("eTp", C("eTp"))] + self.war, incs=[("eWTe", 1)])
                px_end("eWTe")
                E("gpsimd", (lambda bt_oc=self.bt_oc, bt_par=self.bt_par:
                             nc.gpsimd.dma_start(
                    bt_scr[128 * bt_oc:128 * bt_oc + 128, :], BTT[bt_par][:, :])),
                  waits=[("eWTe", C("eWTe"))], incs=[(self.bt_sem, 1)])
                self.ck[self.bt_sem] = C(self.bt_sem)
                self.ck["eWTe"] = C("eWTe")

    def build_slab(wsrc, row0, dst_wt, war_waits, scr=None, scr_sem=None,
                   bt_oc=None, bt_par=None, bt_sem=None, evac_eng="vector",
                   piecewise=False, tp_waits=()):
        s = SlabSteps(wsrc, row0, dst_wt, war_waits, scr, scr_sem,
                      bt_oc, bt_par, bt_sem, evac_eng, piecewise, tp_waits)
        while s.remaining():
            s.step()
        return s.ck

    def ln_stream(x_dram, base_row, ntiles, kv, tile_cb=None, first_waits=()):
        tp_after = {}
        ap_after = {}
        evacs_after = {}

        def xload(tt):
            par = tt % 2
            row = base_row + 128 * tt
            for hh in range(2):
                buf = (XF0 if hh == 0 else XF1)[par]
                ww = ([("eXMa", ap_after[tt - 2])] if tt >= 2
                      else list(first_waits))
                E("sync", (lambda buf=buf, row=row, hh=hh: nc.sync.dma_start(
                    buf[:, :], x_dram[row:row + 128, 2048 * hh:2048 * hh + 2048])),
                  waits=ww, incs=[(f"qX{par}{hh}", 1)])

        xload(0)
        for tt in range(ntiles):
            par = tt % 2
            if tt + 1 < ntiles:
                xload(tt + 1)
            sb6 = BN6[par]
            statw = [("eXMa", ap_after[tt - 2])] if tt >= 2 else []
            for hh in range(2):
                buf = (XF0 if hh == 0 else XF1)[par]
                for k4 in range(4):
                    w = []
                    if k4 == 0:
                        w = [(f"qX{par}{hh}", C(f"qX{par}{hh}"))] + (statw if hh == 0 else [])
                    def bf(buf=buf, k4=k4, hh=hh, sb6=sb6):
                        return nc.vector.bn_stats(
                            sb6[:, 24 * hh + 6 * k4:24 * hh + 6 * k4 + 6],
                            buf[:, 512 * k4:512 * k4 + 512])
                    E("vector", bf, waits=w,
                      incs=([("eBN", 1)] if (hh == 1 and k4 == 3) else []))
            E("vector", (lambda par=par, sb6=sb6: nc.vector.bn_aggr(
                BN2[par][:, :], sb6[:, :])),
              waits=[("eBN", C("eBN"))], incs=[("eBN", 1)])
            E("vector", (lambda par=par: nc.vector.tensor_scalar_add(
                BN2[par][:, 1:2], BN2[par][:, 1:2], EPS)),
              waits=[("eBN", C("eBN"))], incs=[("eBN", 1)])
            E("scalar", (lambda par=par: nc.scalar.activation(
                RST[par][:, :], BN2[par][:, 1:2], AF.Sqrt, scale=1.0)),
              waits=[("eBN", C("eBN"))], incs=[("eRS", 1)])
            E("vector", (lambda par=par: nc.vector.reciprocal(
                RST[par][:, :], RST[par][:, :])),
              waits=[("eRS", C("eRS"))], incs=[("eRS", 1)])
            E("vector", (lambda par=par: nc.vector.scalar_tensor_tensor(
                NMR[par][:, :], BN2[par][:, 0:1], -1.0, RST[par][:, :],
                ALU.mult, ALU.mult)),
              waits=[("eRS", C("eRS"))], incs=[("eRS", 1)])
            rs_now = C("eRS")
            for qq in range(4):
                buf = (XF0 if qq < 2 else XF1)[par]
                bo = 1024 * (qq % 2)
                aw = [("eRS", rs_now)]
                if tt >= 1:
                    aw.append(("eTp", tp_after[(tt - 1, qq)]))
                def af(buf=buf, bo=bo, qq=qq, par=par):
                    return nc.scalar.activation(
                        XMf[:, 1024 * qq:1024 * qq + 1024], buf[:, bo:bo + 1024],
                        AF.Identity, bias=NMR[par][:, :], scale=RST[par][:, :])
                E("scalar", af, waits=aw, incs=[("eXMa", 1)])
                for g4 in range(2):
                    bank, pxw = px_begin()
                    tw = [("eXMa", C("eXMa"))] + pxw
                    for t4 in range(4):
                        dcl = g4 * 4 + t4
                        def tf(dcl=dcl, bank=bank, qq=qq):
                            return nc.tensor.transpose(
                                bank[:, 128 * (dcl % 4):128 * (dcl % 4) + 128],
                                XMf[:, 1024 * qq + 128 * dcl:1024 * qq + 128 * dcl + 128],
                                IDF[:, :])
                        E("tensor", tf, waits=(tw if t4 == 0 else []),
                          incs=([("eTp", 1)] if t4 == 3 else []))
                    dc0 = qq * 8 + g4 * 4
                    eng = "vector" if ((qq + g4) % 2 == 0) else "scalar"
                    esem = "eXNe" if eng == "vector" else "eXNs"
                    ew = [("eTp", C("eTp"))]
                    if kv and tt >= 4:
                        ew.append(("eKV", tt // 4))
                        ew.append(("eVM", tt - 4 + 1))
                    def ef(bank=bank, dc0=dc0, tt=tt, eng=eng):
                        dst = XN[:, :].rearrange("p (dcx f tc) -> p dcx f tc",
                                                 dcx=DC, f=4)[:, dc0:dc0 + 4, tt % 4, :]
                        srcv = bank[:, :].rearrange("p (f tc) -> p f tc", f=4)
                        if eng == "vector":
                            return nc.vector.tensor_copy(dst, srcv)
                        return nc.scalar.copy(dst, srcv)
                    E(eng, ef, waits=ew, incs=[(esem, 1)])
                    px_end(esem)
                tp_after[(tt, qq)] = C("eTp")
            ap_after[tt] = C("eXMa")
            evacs_after[tt] = [("eXNe", C("eXNe")), ("eXNs", C("eXNs"))]
            if kv:
                vb = [pT, pAV][tt % 2]
                for dc in range(DC + 1):
                    w = []
                    if dc == 0:
                        w = list(evacs_after[tt])
                        if tt >= 2:
                            w.append(("eKVe", C("eKVe") - 1))
                    if dc < DC:
                        def vf(dc=dc, tt=tt, vb=vb):
                            c0 = 512 * dc + 128 * (tt % 4)
                            return nc.tensor.matmul(
                                vb[:, 0:128], XN[:, c0:c0 + 128],
                                WVT[:, 128 * dc:128 * dc + 128],
                                start=(dc == 0), stop=False)
                    else:
                        def vf(tt=tt, vb=vb):
                            return nc.tensor.matmul(
                                vb[:, 0:128], ONEbr[:, :], BVB[:, :],
                                start=False, stop=True)
                    E("tensor", vf, waits=w,
                      incs=([("eVM", 1)] if dc == DC else []))
                E("scalar", (lambda tt=tt, vb=vb: nc.scalar.copy(
                    VT[:, 128 * tt:128 * tt + 128], vb[:, 0:128])),
                  waits=[("eVM", C("eVM"))], incs=[("eKVe", 1)])
                if tt % 4 == 3:
                    grp = tt // 4
                    for dc in range(DC):
                        w = []
                        if dc == 0:
                            for t_ in range(tt - 3, tt + 1):
                                w.extend(evacs_after[t_])
                            w = list({(s_, v) for s_, v in w})
                            if grp > 0:
                                w.append(("eKVe", C("eKVe") - 1))
                        def mf(dc=dc):
                            return nc.tensor.matmul(
                                pMM[:, :], WKT[:, 128 * dc:128 * dc + 128],
                                XN[:, 512 * dc:512 * dc + 512],
                                start=(dc == 0), stop=(dc == DC - 1))
                        E("tensor", mf, waits=w,
                          incs=([("eKV", 1)] if dc == DC - 1 else []))
                    E("scalar", (lambda grp=grp: nc.scalar.activation(
                        KT[:, 512 * grp:512 * grp + 512], pMM[:, :], AF.Identity,
                        bias=BK[:, :], scale=1.0)),
                      waits=[("eKV", C("eKV"))], incs=[("eKVe", 1)])
            if tile_cb is not None:
                tile_cb(tt)

    sch.phase("init")
    px_reset(PXV)
    E("gpsimd", lambda: nc.gpsimd.dma_start(
        MK[:, :].rearrange("p (j c) -> p j c", j=8),
        maskv[:, :, :].rearrange("j p c -> p j c")), incs=[("qI", 1)])
    E("gpsimd", lambda: nc.gpsimd.dma_start(IDB[:, :], idbd[:, :]), incs=[("qI", 1)])
    E("gpsimd", lambda: nc.gpsimd.dma_start(IDF[:, :], idfd[:, :]), incs=[("qI", 1)])
    E("gpsimd", lambda: nc.gpsimd.dma_start(COEF[:, :], lwr[:, :]), incs=[("qI", 1)])
    E("vector", lambda: nc.vector.memset(ONEb[:, :], 1.0), incs=[("eInit", 1)])
    E("vector", lambda: nc.vector.memset(ONEr[:, :], 1.0), incs=[("eInit", 1)])
    E("vector", lambda: nc.vector.memset(ONEbr[:, :], 1.0), incs=[("eInit", 1)])
    E("vector", lambda: nc.vector.tensor_scalar_mul(COEF[:, :], COEF[:, :], 2.0),
      waits=[("qI", 4)], incs=[("eInit", 1)])
    for src, dst, w_ in ((bqv, BQ, 32), (bov, BO, 32), (bfv, BF_, 32),
                         (bkv, BK, 1), (bvv, BVc, 1)):
        E("gpsimd", (lambda src=src, w_=w_: nc.gpsimd.dma_start(
            BST[0:w_, :], src[:, :])),
          waits=[("eInit", C("eInit"))], incs=[("qI", 1)])
        E("tensor", (lambda w_=w_: nc.tensor.transpose(
            pT[:, 0:w_], BST[0:w_, :], IDF[0:w_, 0:w_])),
          waits=[("qI", C("qI")), ("eInit", C("eInit"))], incs=[("eTp", 1)])
        E("vector", (lambda dst=dst, w_=w_: nc.vector.tensor_copy(
            dst[:, 0:w_], pT[:, 0:w_])),
          waits=[("eTp", C("eTp"))], incs=[("eInit", 1)])
    E("gpsimd", lambda: nc.gpsimd.dma_start(BSF[0:1, :], bvv[:, :]),
      waits=[("eTp", C("eTp"))], incs=[("qI", 1)])
    E("vector", lambda: nc.vector.tensor_copy(BVB[:, :], BSF[0:1, :]),
      waits=[("qI", C("qI"))], incs=[("eInit", 1)])
    build_slab(av, 0, A_T, war_waits=[])

    sch.phase("kv")
    px_reset(PXV)
    build_slab(wk, 0, WKT, war_waits=[])
    build_slab(wv, 0, WVT, war_waits=[])
    _xf0_addr = nc.lookup_mloc(XF0[0]).addr
    _xf0b_addr = nc.lookup_mloc(XF0[1]).addr
    XTBF = [nc.alloc_sbuf_tensor_at(f"XTBF{i}", [128, 512], F32,
                                    offset=_xf0_addr + 2048 * i) for i in range(4)]
    XTB = [nc.alloc_sbuf_tensor_at(f"XTB{i}", [128, 512], BF16,
                                   offset=_xf0b_addr + 1024 * i) for i in range(4)]
    SQX = [nc.alloc_sbuf_tensor_at(f"SQX{i}", [128, 512], BF16,
                                   offset=_xf0b_addr + 4096 + 1024 * i) for i in range(4)]
    _xf1_addr = nc.lookup_mloc(XF1[0]).addr
    MROW = nc.alloc_sbuf_tensor_at("MROW", [1, 512], F32, offset=_xf1_addr)
    NEGMB = nc.alloc_sbuf_tensor_at("NEGMB", [1, 512], BF16, offset=_xf1_addr + 2048)
    VROW = nc.alloc_sbuf_tensor_at("VROW", [1, 512], F32, offset=_xf1_addr + 4096)
    RSTR = nc.alloc_sbuf_tensor_at("RSTR", [1, 512], F32, offset=_xf1_addr + 6144)
    for name, wt, dst in (("k", WKT, WKR), ("v", WVT, WVR)):
        for dc in range(DC):
            def cf(wt=wt, dc=dc):
                return nc.tensor.matmul(pL[:, 0:128], ONEb[:, :],
                                        wt[:, 128 * dc:128 * dc + 128],
                                        start=(dc == 0), stop=(dc == DC - 1))
            w = ([("eWTe", C("eWTe")), ("eWTs", C("eWTs")),
                  ("eWTp", C("eWTp"))] if dc == 0 else [])
            if dc == 0 and name == "v":
                w.append(("eVM", C("eVM")))
            E("tensor", cf, waits=w, incs=([("eKV", 1)] if dc == DC - 1 else []))
        E("vector", (lambda dst=dst: nc.vector.tensor_copy(dst[:, :], pL[0:1, 0:128])),
          waits=[("eKV", C("eKV"))], incs=[("eVM", 1)])
    E("tensor", (lambda: nc.tensor.matmul(
        pAV[:, 0:128], ONEbr[:, :], BVB[:, :], start=True, stop=True)),
      waits=[("eInit", C("eInit"))], incs=[("eKV", 1)])
    E("vector", (lambda: nc.vector.tensor_copy(BVBC[:, :], pAV[:, 0:128])),
      waits=[("eKV", C("eKV"))], incs=[("eVM", 1)])

    pV = [PXV[0], PXV[1], PXV[2], PXV[3]]
    wq_prev = {0: None, 1: None}
    dcg = 0
    for g in range(8):
        for dc in range(DC):
            par = dc % 4
            qs = f"qX{par % 2}{par // 2}"
            ww = [("eCast", C("eCast") - 3)] if dcg >= 4 else []
            E("sync", (lambda dc=dc, g=g, par=par: nc.sync.dma_start(
                XTBF[par][:, :], x_allT[128 * dc:128 * dc + 128, 512 * g:512 * g + 512])),
              waits=ww, incs=[(qs, 1)])
            cw = [(qs, C(qs))]
            if dcg >= 4:
                cw.append(("eKV", C("eKV") - 3))
            E("vector", (lambda par=par: nc.vector.tensor_copy(
                XTB[par][:, :], XTBF[par][:, :])),
              waits=cw, incs=[("eCast", 1)])
            sw = [("eCast", C("eCast"))]
            if dcg >= 4:
                sw.append(("eKV", C("eKV") - 3))
            E("scalar", (lambda par=par: nc.scalar.activation(
                SQX[par][:, :], XTB[par][:, :], AF.Square)),
              waits=sw, incs=[("eSQ", 1)])
            w0 = [("eCast", C("eCast"))]
            if dc == 0:
                w0.append(("eKVe", C("eKVe")))
                w0.append(("eVM", C("eVM")))
                w0.append(("eBN", C("eBN")))
                w0.append(("eWTe", C("eWTe")))
                w0.append(("eWTs", C("eWTs")))
                w0.append(("eWTp", C("eWTp")))
            def kf(dc=dc):
                return nc.tensor.matmul(pMM[:, :], WKT[:, 128 * dc:128 * dc + 128],
                                        XTB[dc % 4][:, :], start=(dc == 0), stop=False)
            E("tensor", kf, waits=w0)
            def mf(dc=dc):
                return nc.tensor.matmul(pL[:, :], ONEb[:, :], XTB[dc % 4][:, :],
                                        start=(dc == 0), stop=(dc == DC - 1))
            E("tensor", mf)
            def qf(dc=dc):
                return nc.tensor.matmul(pT[0:1, :], ONEb[:, :], SQX[dc % 4][:, :],
                                        start=(dc == 0), stop=(dc == DC - 1))
            E("tensor", qf, waits=[("eSQ", C("eSQ"))])
            def vf(dc=dc):
                return nc.tensor.matmul(
                    pV[0][:, :], WVT[:, 128 * dc:128 * dc + 128],
                    XTB[dc % 4][:, :], start=(dc == 0), stop=False)
            E("tensor", vf, incs=[("eKV", 1)])
            dcg += 1
        E("vector", (lambda: nc.vector.tensor_scalar(
            MROW[:, :], pL[0:1, :], 1.0 / 4096.0, None, ALU.mult)),
          waits=[("eKV", C("eKV"))], incs=[("eBN", 1)])
        E("vector", (lambda: nc.vector.tensor_scalar(
            NEGMB[:, :], pL[0:1, :], -1.0 / 4096.0, None, ALU.mult)),
          waits=[("eBN", C("eBN"))], incs=[("eBN", 1)])
        E("vector", (lambda: nc.vector.tensor_scalar(
            VROW[:, :], pT[0:1, :], 1.0 / 4096.0, None, ALU.mult)),
          waits=[("eBN", C("eBN"))], incs=[("eBN", 1)])
        E("vector", (lambda: nc.vector.scalar_tensor_tensor(
            RSTR[:, :], MROW[:, :], -1.0, MROW[:, :], ALU.bypass, ALU.mult)),
          waits=[("eBN", C("eBN"))], incs=[("eBN", 1)])
        E("vector", (lambda: nc.vector.tensor_sub(VROW[:, :], VROW[:, :], RSTR[:, :])),
          waits=[("eBN", C("eBN"))], incs=[("eBN", 1)])
        E("vector", (lambda: nc.vector.tensor_scalar_add(VROW[:, :], VROW[:, :], EPS)),
          waits=[("eBN", C("eBN"))], incs=[("eBN", 1)])
        E("scalar", (lambda: nc.scalar.activation(RSTR[:, :], VROW[:, :], AF.Sqrt)),
          waits=[("eBN", C("eBN"))], incs=[("eRS", 1)])
        E("vector", (lambda: nc.vector.reciprocal(RSTR[:, :], RSTR[:, :])),
          waits=[("eRS", C("eRS"))], incs=[("eRS", 1)])
        E("tensor", (lambda: nc.tensor.matmul(
            pMM[:, :], WKR[:, :], NEGMB[:, :], start=False, stop=True)),
          waits=[("eBN", C("eBN"))], incs=[("eVM", 1)])
        E("tensor", (lambda: nc.tensor.matmul(
            pV[0][:, :], WVR[:, :], NEGMB[:, :], start=False, stop=True)),
          incs=[("eVM", 1)])
        w = [("eRS", C("eRS"))]
        if g > 0:
            w.append(("eBCe", C("eBCe")))
        E("tensor", (lambda: nc.tensor.matmul(
            pAV[:, :], ONEr[:, :], RSTR[:, :], start=True, stop=True)),
          waits=w, incs=[("eBC", 1)])
        E("vector", (lambda: nc.vector.tensor_copy(LIVB[:, :], pAV[:, :])),
          waits=[("eBC", C("eBC"))], incs=[("eBCe", 1)])
        E("vector", (lambda: nc.vector.tensor_mul(SGb[:, :], pMM[:, :], LIVB[:, :])),
          waits=[("eVM", C("eVM")), ("eBCe", C("eBCe"))], incs=[("eBN", 1)])
        E("vector", (lambda g=g: nc.vector.tensor_scalar(
            KT[:, 512 * g:512 * g + 512], SGb[:, :], BK[:, :], None, ALU.add)),
          waits=[("eBN", C("eBN"))], incs=[("eKVe", 1)])
        E("vector", (lambda: nc.vector.tensor_mul(SGb[:, :], pV[0][:, :], LIVB[:, :])),
          waits=[("eVM", C("eVM")), ("eBCe", C("eBCe")), ("eKVe", C("eKVe"))],
          incs=[("eBN", 1)])
        tp_banks = [pV[1], pV[2], pV[3], pV[1]]
        evac_cnt = C("eKVe")
        for i in range(4):
            w = [("eBN", C("eBN"))]
            if i == 3:
                w.append(("eKVe", evac_cnt + 1))
            def tf(i=i):
                return nc.tensor.transpose(tp_banks[i][:, 0:128],
                                           SGb[:, 128 * i:128 * i + 128],
                                           IDF[:, :])
            E("tensor", tf, waits=w, incs=[("eTp", 1)])
            def ef(g=g, i=i):
                return nc.vector.tensor_add(
                    VT[:, 128 * (4 * g + i):128 * (4 * g + i) + 128],
                    tp_banks[i][:, 0:128], BVBC[:, :])
            E("vector", ef, waits=[("eTp", C("eTp"))], incs=[("eKVe", 1)])
        for hq in range(4):
            hh = 4 * g + hq
            par = hh % 2
            war = [("eKVe", C("eKVe"))]
            if wq_prev[par] is not None:
                war.append((f"qSA{par}", wq_prev[par]))
            ck = build_slab(wq, 128 * hh, WTA[par], war_waits=war,
                            scr=wq_scr, scr_sem=f"qSA{par}",
                            tp_waits=[("eKVe", C("eKVe"))],
                            evac_eng=("vector" if hh % 2 == 0 else "scalar"))
            wq_prev[par] = ck[f"qSA{par}"]

    E("scalar", lambda: nc.scalar.activation(KSQ[:, :], KT[:, :], AF.Square),
      waits=[("eKVe", C("eKVe"))], incs=[("eKm", 1)])
    for g8 in range(8):
        w = [("eKm", C("eKm"))]
        def mf(g8=g8):
            return nc.tensor.matmul(pL[:, :], ONEb[:, :],
                                    KSQ[:, 512 * g8:512 * g8 + 512],
                                    start=True, stop=True)
        E("tensor", mf, waits=w, incs=[("eKm", 1)])
        E("vector", (lambda g8=g8: nc.vector.reduce_max(
            KM8[:, g8:g8 + 1], pL[0:1, :], AX.X)),
          waits=[("eKm", C("eKm"))], incs=[("eKm", 1)])
    E("vector", lambda: nc.vector.reduce_max(KSC[:, :], KM8[:, :], AX.X),
      waits=[("eKm", C("eKm"))], incs=[("eKm", 1)])
    E("vector", lambda: nc.vector.tensor_scalar_mul(
        KSC[:, :], KSC[:, :], -0.04419417382415922),
      waits=[("eKm", C("eKm"))], incs=[("eKm", 1)])
    if DBG:
        E("gpsimd", lambda: nc.gpsimd.dma_start(dKT[:, :], KT[:, :]),
          waits=[("eKVe", C("eKVe"))], incs=[("qDB", 1)])
        E("gpsimd", lambda: nc.gpsimd.dma_start(dVT[:, :], VT[:, :]),
          waits=[("eVt", C("eVt"))], incs=[("qDB", 1)])

    wq_ck = {}
    for p in range(NPASS):
        nk = 8 * p + 8
        rp, wpar = p % 2, (p + 1) % 2
        chain = p < 3

        sch.phase(f"xnb{p}")
        px_reset(PXV)
        fw = [("eCast", C("eCast")), ("eKV", C("eKV")), ("eBC", C("eBC")),
              ("eTp", max(wfb_tp[2], wfb_tp[3]))]
        ln_stream(x_own, TP * p, 4, kv=False, first_waits=fw)
        wfb_war[2] = wfb_war[3] = [("eXMa", C("eXMa"))]
        if DBG and p == 0:
            E("gpsimd", lambda: nc.gpsimd.dma_start(dXN[:, :], XN[:, :]),
              waits=[("eXNe", C("eXNe"))], incs=[("qDB", 1)])

        sch.phase(f"hst{p}")
        for dc in range(DC):
            def mf(dc=dc):
                return nc.tensor.matmul(pMM[:, :], A_T[:, 128 * dc:128 * dc + 128],
                                        XN[:, 512 * dc:512 * dc + 512],
                                        start=(dc == 0), stop=(dc == DC - 1))
            E("tensor", mf, incs=([("eHS", 1)] if dc == DC - 1 else []))
        E("vector", lambda: nc.vector.tensor_scalar(
            HSb[:, :], pMM[:, :], COEF[:, :], None, ALU.mult),
          waits=[("eHS", C("eHS"))], incs=[("eHS", 1)])
        if DBG and p == 0:
            E("gpsimd", lambda: nc.gpsimd.dma_start(dHS[:, :], HSb[:, :]),
              waits=[("eHS", C("eHS"))], incs=[("qDB", 1)])

        sch.phase(f"att{p}")
        px_reset([PXV[2], PXV[3]])
        B0 = {s: C(s) for s in semnames}
        jg = 0
        NPSp = 2 if (p == 0 and BUILD_IN_ATT0) else 3
        wq_gate = [("qSA0", C("qSA0")), ("qSA1", C("qSA1"))]

        def qload(h, p=p, wq_gate=wq_gate):
            w = list(wq_gate) if h < 2 else []
            if h >= 2:
                w.append(("eQ", B0["eQ"] + h - 1))
            E("sync", (lambda h=h: nc.sync.dma_start(
                WTA[h % 2][:, :], wq_scr[128 * h:128 * h + 128, :])),
              waits=w, incs=[(f"qWA{h % 2}", 1)])

        def qproj(h, p=p):
            w0 = [("eQev", B0["eQev"] + h)] if h >= 1 else []
            w0.append((f"qWA{h % 2}", C(f"qWA{h % 2}")))
            for dc in range(DC):
                def mf(h=h, dc=dc):
                    return nc.tensor.matmul(
                        pMM[:, :], WTA[h % 2][:, 128 * dc:128 * dc + 128],
                        XN[:, 512 * dc:512 * dc + 512],
                        start=(dc == 0), stop=(dc == DC - 1))
                E("tensor", mf, waits=(w0 if dc == 0 else []),
                  incs=([("eQ", 1)] if dc == DC - 1 else []))

        def qevac(h, p=p, chain=chain):
            E("scalar", (lambda h=h: nc.scalar.activation(
                QT[:, 512 * (h % 2):512 * (h % 2) + 512], pMM[:, :], AF.Identity,
                bias=BQ[:, h:h + 1], scale=1.0)),
              waits=[("eQ", B0["eQ"] + h + 1)], incs=[("eQev", 1)])
            if chain:
                w = [("eQev", C("eQev"))]
                if h >= 1:
                    w.append(("eB1", B0["eB1"] + h))
                E("scalar", (lambda h=h: nc.scalar.activation(
                    SQb[:, :], QT[:, 512 * (h % 2):512 * (h % 2) + 512], AF.Square)),
                  waits=w, incs=[("eSQ", 1)])

        wob_prev = {"qSO0": None, "qSF": None}

        def build_steps(h):
            war = [("qSF", wob_prev["qSF"])] if wob_prev["qSF"] else []
            if h < 1:
                war += [("qSA0", C("qSA0")), ("qSA1", C("qSA1"))]
            if wob_prev.get(f"qSB{h % 2}"):
                war.append((f"qSB{h % 2}", wob_prev[f"qSB{h % 2}"]))
            s1 = SlabSteps(wo, 128 * h, WTB, war, scr=wo_scr, scr_sem="qSO0",
                           piecewise=True, bt_oc=h, bt_par=h % 2,
                           bt_sem=f"qSB{h % 2}",
                           evac_eng=("vector" if h % 2 == 0 else "scalar"))
            while s1.remaining():
                s1.step()
                yield
            wob_prev["qSO0"] = s1.ck["qSO0"]
            wob_prev[f"qSB{h % 2}"] = s1.ck[f"qSB{h % 2}"]
            s2 = SlabSteps(wf, 128 * h, WTB, [("qSO0", wob_prev["qSO0"])],
                           scr=wf_scr, scr_sem="qSF", piecewise=True,
                           evac_eng=("scalar" if h % 2 == 0 else "vector"))
            while s2.remaining():
                s2.step()
                yield
            wob_prev["qSF"] = s2.ck["qSF"]

        qload(0)
        qload(1)
        qproj(0)
        qevac(0)

        for h in range(NH):
            qcol = 512 * (h % 2)
            filler = build_steps(h) if (p == 0 and BUILD_IN_ATT0) else iter(())
            if chain:
                w = [("eSQ", B0["eSQ"] + h + 1)]
                if h >= 1:
                    w.append(("eBCe", B0["eBCe"] + h))
                E("tensor", (lambda: nc.tensor.matmul(
                    pT[0:1, :], ONEb[:, :], SQb[:, :], start=True, stop=True)),
                  waits=w, incs=[("eB1", 1)])
                w = [("eB1", B0["eB1"] + h + 1)]
                if h >= 1:
                    w.append(("eB4", B0["eB4"] + h))
                E("vector", (lambda: nc.vector.reduce_max(QM2[:, :], pT[0:1, :], AX.X)),
                  waits=w, incs=[("eB2", 1)])
                w = [("eB2", B0["eB2"] + h + 1)]
                if h >= 1:
                    w.append(("eB5", B0["eB5"] + h))
                E("vector", (lambda: nc.vector.tensor_scalar(
                    NB1[:, :], QM2[:, :], -0.04419417382415922, KSC[:, :],
                    ALU.mult, ALU.add)),
                  waits=w, incs=[("eB4", 1)])
            if chain and p == 0 and h == 0:
                w = [("eB4", B0["eB4"] + h + 1), ("eB2", B0["eB2"] + h + 1)]
                E("tensor", (lambda: nc.tensor.matmul(
                    pT[:, 0:1], ONEr[:, :], NB1[:, :], start=True, stop=True)),
                  waits=w, incs=[("eB5", 1)])
                E("vector", (lambda h=h, wpar=wpar: nc.vector.tensor_copy(
                    BALL[:, 32 * wpar + h:32 * wpar + h + 1], pT[:, 0:1])),
                  waits=[("eB5", B0["eB5"] + h + 1)], incs=[("eB6", 1)])
                E("vector", (lambda h=h, rp=rp: nc.vector.tensor_copy(
                    BALL[:, 32 * rp + h:32 * rp + h + 1], pT[:, 0:1])),
                  waits=[("eB6", C("eB6"))], incs=[("eB6", 1)])

            if p == 0:
                nku = nk // 2
                jgp = jg // 2
                hb = max(h - 1, 0)
                for half in range(2):
                    w = [("eQev", B0["eQev"] + h + 1)] if half == 0 else []
                    if half == 0 and jgp >= 1:
                        w.append(("eExp", B0["eExp"] + jgp))
                    def sf(j=half, half=half, qcol=qcol):
                        return nc.tensor.matmul(
                            pSP[:, 512 * half:512 * half + 512],
                            KT[:, 128 * j:128 * j + 128],
                            QT[:, qcol:qcol + 512], start=True, stop=True)
                    E("tensor", sf, waits=w, incs=[("eSc", 1)])
                for u in range(nku):
                    ug = jgp + u
                    w = [("eSc", B0["eSc"] + jg + 2 * u + 2)]
                    if ug >= 3:
                        w.append(("eAV", B0["eAV"] + 2 * (ug - 3) + 2))
                        w.append(("ePr", B0["ePr"] + ug - 2))
                    if chain:
                        w.append(("eB6", B0["eB6"] + 2 * (hb + 1)))
                    def xf(ug=ug, hb=hb, rp=rp):
                        return nc.scalar.activation(
                            PT[:, 1024 * (ug % 3):1024 * (ug % 3) + 1024],
                            pSP[:, :], AF.Exp,
                            bias=BALL[:, 32 * rp + hb:32 * rp + hb + 1], scale=SC)
                    E("scalar", xf, waits=w, incs=[("eExp", 1)])
                    for half in range(2):
                        j = 2 * u + half
                        def mkf(ug=ug, half=half, jo=j):
                            c0 = 1024 * (ug % 3) + 512 * half
                            return nc.vector.tensor_mul(
                                PT[:, c0:c0 + 512], PT[:, c0:c0 + 512],
                                MK[:, 512 * jo:512 * jo + 512])
                        E("vector", mkf, waits=[("eExp", B0["eExp"] + ug + 1)],
                          incs=[("eMsk", 1)])
                    w3 = [("eMsk", B0["eMsk"] + 8 * h + 2 * u + 1),
                          ("eMsk", B0["eMsk"] + 8 * h + 2 * u + 2)]
                    if ug >= 2:
                        w3.append(("eLS", B0["eLS"] + ug - 1))
                    def prf(ug=ug):
                        c0 = 1024 * (ug % 3)
                        return nc.vector.tensor_add(
                            PTS[:, 512 * (ug % 2):512 * (ug % 2) + 512],
                            PT[:, c0:c0 + 512], PT[:, c0 + 512:c0 + 1024])
                    E("vector", prf, waits=w3, incs=[("ePr", 1)])
                    for _ in range(4):
                        next(filler, None)
                    if chain and u == 1 and h >= 1:
                        w6 = [("eB4", B0["eB4"] + h + 1),
                              ("eB2", B0["eB2"] + h + 1)]
                        E("tensor", (lambda: nc.tensor.matmul(
                            pT[:, 0:1], ONEr[:, :], NB1[:, :],
                            start=True, stop=True)),
                          waits=w6, incs=[("eB5", 1)])
                        E("vector", (lambda h=h, wpar=wpar: nc.vector.tensor_copy(
                            BALL[:, 32 * wpar + h:32 * wpar + h + 1], pT[:, 0:1])),
                          waits=[("eB5", B0["eB5"] + h + 1)], incs=[("eB6", 1)])
                        E("vector", (lambda h=h, rp=rp: nc.vector.tensor_copy(
                            BALL[:, 32 * rp + h:32 * rp + h + 1], pT[:, 0:1])),
                          waits=[("eB6", C("eB6"))], incs=[("eB6", 1)])
                    if h + 1 < NH:
                        for dcq in range(8 * u, 8 * u + 8):
                            w0q = []
                            if dcq == 0:
                                w0q = [("eQev", B0["eQev"] + h + 1),
                                       (f"qWA{(h + 1) % 2}",
                                        C(f"qWA{(h + 1) % 2}"))]
                            def mfq(h1=h + 1, dc=dcq):
                                return nc.tensor.matmul(
                                    pMM[:, :],
                                    WTA[h1 % 2][:, 128 * dc:128 * dc + 128],
                                    XN[:, 512 * dc:512 * dc + 512],
                                    start=(dc == 0), stop=(dc == DC - 1))
                            E("tensor", mfq, waits=w0q,
                              incs=([("eQ", 1)] if dcq == DC - 1 else []))
                    for half in range(2):
                        j = 2 * u + half
                        w2 = [("eMsk", B0["eMsk"] + 8 * h + j + 1)]
                        if j == 0 and h >= 1:
                            w2.append(("eAVe", B0["eAVe"] + h))
                        def avf(ug=ug, half=half, j=j, nk=nk):
                            c0 = 1024 * (ug % 3) + 512 * half
                            return nc.tensor.matmul(
                                pAV[:, :], VT[:, 128 * j:128 * j + 128],
                                PT[:, c0:c0 + 512],
                                start=(j == 0), stop=(j == nk - 1))
                        E("tensor", avf, waits=w2, incs=[("eAV", 1)])
                    w4 = [("ePr", B0["ePr"] + ug + 1)]
                    if u == 0 and h >= 1:
                        w4.append(("eRec", B0["eRec"] + 2 * h - 1))
                    def lf(ug=ug, u=u, nku=nku):
                        return nc.tensor.matmul(
                            pL[:, :], ONEb[:, :],
                            PTS[:, 512 * (ug % 2):512 * (ug % 2) + 512],
                            start=(u == 0), stop=(u == nku - 1))
                    E("tensor", lf, waits=w4, incs=[("eLS", 1)])
                    if u + 1 < nku:
                        for half in range(2):
                            j2 = 2 * (u + 1) + half
                            w5 = ([("eExp", B0["eExp"] + ug + 1)]
                                  if half == 0 else [])
                            def sf(j2=j2, half=half, qcol=qcol):
                                return nc.tensor.matmul(
                                    pSP[:, 512 * half:512 * half + 512],
                                    KT[:, 128 * j2:128 * j2 + 128],
                                    QT[:, qcol:qcol + 512], start=True, stop=True)
                            E("tensor", sf, waits=w5, incs=[("eSc", 1)])
            else:
                nku = nk // 2
                jgp = jg // 2
                for up in range(2):
                    for half in range(2):
                        jch = 2 * up + half
                        w = [("eQev", B0["eQev"] + h + 1)] if jch == 0 else []
                        if jgp + up >= 2 and half == 0:
                            w.append(("eExp", B0["eExp"] + jgp + up - 1))
                        def sf(j=jch, u=jgp + up, half=half, qcol=qcol):
                            bank = [pSP, pSQ][u % 2]
                            return nc.tensor.matmul(
                                bank[:, 512 * half:512 * half + 512],
                                KT[:, 128 * j:128 * j + 128],
                                QT[:, qcol:qcol + 512], start=True, stop=True)
                        E("tensor", sf, waits=w, incs=[("eSc", 1)])
                for u in range(nku):
                    ug = jgp + u
                    w = [("eSc", B0["eSc"] + jg + 2 * u + 2)]
                    if u >= 3:
                        w.append(("eAV", B0["eAV"] + jg + 2 * (u - 3) + 2))
                        w.append(("ePr", B0["ePr"] + jgp + u - 2))
                    def xf(ug=ug, h=h, rp=rp):
                        bank = [pSP, pSQ][ug % 2]
                        return nc.scalar.activation(
                            PT[:, 1024 * (ug % 3):1024 * (ug % 3) + 1024],
                            bank[:, :], AF.Exp,
                            bias=BALL[:, 32 * rp + h:32 * rp + h + 1], scale=SC)
                    E("scalar", xf, waits=w, incs=[("eExp", 1)])
                    for half in range(2):
                        j = 2 * u + half
                        if j >= nk - 8:
                            jo = j - (nk - 8)
                            def mkf(ug=ug, half=half, jo=jo):
                                c0 = 1024 * (ug % 3) + 512 * half
                                return nc.vector.tensor_mul(
                                    PT[:, c0:c0 + 512], PT[:, c0:c0 + 512],
                                    MK[:, 512 * jo:512 * jo + 512])
                            E("vector", mkf, waits=[("eExp", B0["eExp"] + ug + 1)],
                              incs=[("eMsk", 1)])
                    for half in range(2):
                        j = 2 * u + half
                        if j >= nk - 8:
                            w2 = [("eMsk", B0["eMsk"] + 8 * h + (j - (nk - 8)) + 1)]
                        else:
                            w2 = [("eExp", B0["eExp"] + ug + 1)]
                        if j == 0 and h >= 1:
                            w2.append(("eAVe", B0["eAVe"] + h))
                        def avf(ug=ug, half=half, j=j, nk=nk):
                            c0 = 1024 * (ug % 3) + 512 * half
                            return nc.tensor.matmul(
                                pAV[:, :], VT[:, 128 * j:128 * j + 128],
                                PT[:, c0:c0 + 512],
                                start=(j == 0), stop=(j == nk - 1))
                        E("tensor", avf, waits=w2, incs=[("eAV", 1)])
                    w3 = []
                    for half in range(2):
                        j = 2 * u + half
                        if j >= nk - 8:
                            w3.append(("eMsk", B0["eMsk"] + 8 * h + (j - (nk - 8)) + 1))
                        else:
                            w3.append(("eExp", B0["eExp"] + ug + 1))
                    if ug >= 2:
                        w3.append(("eLS", B0["eLS"] + ug - 1))
                    def prf(ug=ug):
                        c0 = 1024 * (ug % 3)
                        return nc.vector.tensor_add(
                            PTS[:, 512 * (ug % 2):512 * (ug % 2) + 512],
                            PT[:, c0:c0 + 512], PT[:, c0 + 512:c0 + 1024])
                    E("vector", prf, waits=w3, incs=[("ePr", 1)])
                    w4 = [("ePr", B0["ePr"] + ug + 1)]
                    if u == 0 and h >= 1:
                        w4.append(("eRec", B0["eRec"] + 2 * h - 1))
                    def lf(ug=ug, u=u, nku=nku):
                        return nc.tensor.matmul(
                            pL[:, :], ONEb[:, :],
                            PTS[:, 512 * (ug % 2):512 * (ug % 2) + 512],
                            start=(u == 0), stop=(u == nku - 1))
                    E("tensor", lf, waits=w4, incs=[("eLS", 1)])
                    if u + 2 < nku:
                        for half in range(2):
                            j2 = 2 * (u + 2) + half
                            w5 = [("eExp", B0["eExp"] + ug + 1)] if half == 0 else []
                            def sf(j2=j2, u2=jgp + u + 2, half=half, qcol=qcol):
                                bank = [pSP, pSQ][u2 % 2]
                                return nc.tensor.matmul(
                                    bank[:, 512 * half:512 * half + 512],
                                    KT[:, 128 * j2:128 * j2 + 128],
                                    QT[:, qcol:qcol + 512], start=True, stop=True)
                            E("tensor", sf, waits=w5, incs=[("eSc", 1)])
                    if chain and u == 1:
                        w6 = [("eB4", B0["eB4"] + h + 1), ("eB2", B0["eB2"] + h + 1)]
                        E("tensor", (lambda: nc.tensor.matmul(
                            pT[:, 0:1], ONEr[:, :], NB1[:, :], start=True, stop=True)),
                          waits=w6, incs=[("eB5", 1)])

            if chain and p > 0:
                E("vector", (lambda h=h, wpar=wpar: nc.vector.tensor_copy(
                    BALL[:, 32 * wpar + h:32 * wpar + h + 1], pT[:, 0:1])),
                  waits=[("eB5", B0["eB5"] + h + 1)], incs=[("eB6", 1)])

            w = [("eAV", B0["eAV"] + jg + nk)]
            if h >= 1:
                w.append(("eATw", B0["eATw"] + h))
            E("vector", (lambda: nc.vector.tensor_copy(AUN[:, :], pAV[:, :])),
              waits=w, incs=[("eAVe", 1)])
            w = [("eLS", B0["eLS"] + (jg + nk) // 2)]
            if h >= 1:
                w.append(("eBC", B0["eBC"] + h))
            E("vector", (lambda: nc.vector.tensor_copy(LIV[:, :], pL[0:1, :])),
              waits=w, incs=[("eRec", 1)])
            E("vector", (lambda: nc.vector.reciprocal(LIV[:, :], LIV[:, :])),
              waits=[("eRec", C("eRec"))], incs=[("eRec", 1)])

            if h + 1 < NH:
                if h + 2 < NH:
                    qload(h + 2)
                if p != 0:
                    qproj(h + 1)

            w = [("eRec", B0["eRec"] + 2 * h + 2)]
            if chain:
                w.append(("eB6", B0["eB6"] + (2 * (h + 1) if p == 0 else h + 1)))
            elif h >= 1:
                w.append(("eBCe", B0["eBCe"] + h))
            E("tensor", (lambda: nc.tensor.matmul(
                pT[:, :], ONEr[:, :], LIV[:, :], start=True, stop=True)),
              waits=w, incs=[("eBC", 1)])
            E("vector", (lambda: nc.vector.tensor_copy(LIVB[:, :], pT[:, :])),
              waits=[("eBC", B0["eBC"] + h + 1)], incs=[("eBCe", 1)])
            E("vector", (lambda h=h: nc.vector.tensor_mul(
                ATT[:, 512 * h:512 * h + 512], AUN[:, :], LIVB[:, :])),
              waits=[("eBCe", B0["eBCe"] + h + 1),
                     ("eAVe", B0["eAVe"] + h + 1)], incs=[("eATw", 1)])

            if h + 1 < NH:
                qevac(h + 1)
            for _ in filler:
                pass
            jg += nk

        if DBG and p == 0:
            E("gpsimd", lambda: nc.gpsimd.dma_start(dAT[:, :], ATT[:, :]),
              waits=[("eATw", C("eATw"))], incs=[("qDB", 1)])

        if p == 0 and not BUILD_IN_ATT0:
            prevt = {"qSF": None, "qSO0": None, "qSB0": None, "qSB1": None}
            for oc in range(DC):
                war = [("qSF", prevt["qSF"])] if prevt["qSF"] else []
                if oc < 1:
                    war += [("qSA0", C("qSA0")), ("qSA1", C("qSA1"))]
                if prevt.get(f"qSB{oc % 2}"):
                    war.append((f"qSB{oc % 2}", prevt[f"qSB{oc % 2}"]))
                ck = build_slab(wo, 128 * oc, WTB, war_waits=war,
                                scr=wo_scr, scr_sem="qSO0", piecewise=True,
                                bt_oc=oc, bt_par=oc % 2, bt_sem=f"qSB{oc % 2}",
                                evac_eng=("vector" if oc % 2 == 0 else "scalar"))
                prevt["qSO0"] = ck["qSO0"]
                prevt[f"qSB{oc % 2}"] = ck[f"qSB{oc % 2}"]
                war = [("qSO0", prevt["qSO0"])]
                ck = build_slab(wf, 128 * oc, WTB, war_waits=war,
                                scr=wf_scr, scr_sem="qSF", piecewise=True,
                                evac_eng=("scalar" if oc % 2 == 0 else "vector"))
                prevt["qSF"] = ck["qSF"]

        sch.phase(f"epi{p}")
        B1 = {s: C(s) for s in semnames}
        w_gate = [(s_, C(s_)) for s_ in
                  ("qSO0", "qSO1", "qSF", "qSB0", "qSB1")]

        def eload(oc, p=p, w_gate=w_gate):
            w = list(w_gate) if oc < 2 else []
            if oc >= 2:
                w.append(("eS", B1["eS"] + oc - 1))
            E("sync", (lambda oc=oc: nc.sync.dma_start(
                WTA[oc % 2][:, :], wo_scr[128 * oc:128 * oc + 128, :])),
              waits=w, incs=[(f"qWA{oc % 2}", 1)])
            w = list(w_gate) if oc < 1 else []
            if oc >= 1:
                w.append(("eG", B1["eG"] + oc))
            E("sync", (lambda oc=oc: nc.sync.dma_start(
                WTB[:, :], wf_scr[128 * oc:128 * oc + 128, :])),
              waits=w, incs=[("qWB", 1)])
            w = list(w_gate) if oc < 2 else []
            if oc >= 2:
                w.append(("eG", B1["eG"] + oc - 1))
            E("sync", (lambda oc=oc: nc.sync.dma_start(
                BTT[oc % 2][:, :], bt_scr[128 * oc:128 * oc + 128, :])),
              waits=w, incs=[(f"qBT{oc % 2}", 1)])

        def xrload(oc, p=p):
            for i in range(4):
                w = [("eOS", C("eOS"))] if oc >= 1 else []
                def lf(oc=oc, i=i):
                    return nc.sync.dma_start(
                        XRb[i][:, :],
                        x_own[TP * p + 128 * i:TP * p + 128 * i + 128,
                              128 * oc:128 * oc + 128])
                E("sync", lf, waits=w, incs=[(f"qR{oc % 2}", 1)])

        def sgtrans(oc, p=p):
            for i in range(4):
                w = [("eSG", B1["eSG"] + oc + 1)] if i == 0 else []
                if i == 0 and oc >= 1:
                    w.append(("eOS", B1["eOS"] + 4 * oc))
                def tf(i=i):
                    return nc.tensor.transpose(
                        pAV[:, 128 * i:128 * i + 128], SGb[:, 128 * i:128 * i + 128],
                        IDF[:, :])
                E("tensor", tf, waits=w, incs=[("eTsg", 1)])
            for i in range(4):
                w = [("eTsg", B1["eTsg"] + 4 * oc + 4),
                     (f"qR{oc % 2}", C(f"qR{oc % 2}")),
                     ("qO0", C("qO0")), ("qO1", C("qO1"))]
                def vf(oc=oc, i=i):
                    return nc.vector.tensor_add(
                        OSb[i][:, :], pAV[:, 128 * i:128 * i + 128], XRb[i][:, :])
                E("vector", vf, waits=w, incs=[("eOS", 1)])
            for i in range(4):
                def of(oc=oc, i=i, p=p):
                    return nc.gpsimd.dma_start(
                        outp[TP * p + 128 * i:TP * p + 128 * i + 128,
                             128 * oc:128 * oc + 128],
                        OSb[i][:, :])
                E("gpsimd", of, waits=[("eOS", B1["eOS"] + 4 * oc + i + 1)],
                  incs=[(f"qO{oc % 2}", 1)])

        eload(0)
        xrload(0)

        for oc in range(DC):
            w0 = [("eGOe", B1["eGOe"] + oc)] if oc >= 1 else []
            w0.append(("qWB", C("qWB")))
            w0.append((f"qBT{oc % 2}", C(f"qBT{oc % 2}")))
            for dc in range(DC):
                def gf(dc=dc):
                    return nc.tensor.matmul(
                        pMM[:, :], WTB[:, 128 * dc:128 * dc + 128],
                        XN[:, 512 * dc:512 * dc + 512],
                        start=(dc == 0), stop=False)
                E("tensor", gf, waits=(w0 if dc == 0 else []))
            def gl(oc=oc):
                return nc.tensor.matmul(pMM[:, :], BTT[oc % 2][:, :], HSb[:, :],
                                        start=False, stop=True)
            E("tensor", gl, incs=[("eG", 1)])
            w1 = [("eSTe", B1["eSTe"] + oc)] if oc >= 1 else []
            w1.append((f"qWA{oc % 2}", C(f"qWA{oc % 2}")))
            for hc in range(DC):
                def sf(hc=hc, oc=oc):
                    return nc.tensor.matmul(
                        pT[:, :], WTA[oc % 2][:, 128 * hc:128 * hc + 128],
                        ATT[:, 512 * hc:512 * hc + 512],
                        start=(hc == 0), stop=(hc == DC - 1))
                E("tensor", sf, waits=(w1 if hc == 0 else []),
                  incs=([("eS", 1)] if hc == DC - 1 else []))
            w = [("eG", B1["eG"] + oc + 1)]
            if oc >= 1:
                w.append(("eSG", B1["eSG"] + oc))
            E("scalar", (lambda oc=oc: nc.scalar.activation(
                GOb[:, :], pMM[:, :], AF.Silu, bias=BF_[:, oc:oc + 1], scale=1.0)),
              waits=w, incs=[("eGOe", 1)])
            w = [("eS", B1["eS"] + oc + 1)]
            if oc >= 2:
                w.append(("eSG", B1["eSG"] + oc - 1))
            E("scalar", (lambda oc=oc: nc.scalar.activation(
                STb[oc % 2][:, :], pT[:, :], AF.Identity, bias=BO[:, oc:oc + 1], scale=1.0)),
              waits=w, incs=[("eSTe", 1)])
            w = [("eGOe", B1["eGOe"] + oc + 1), ("eSTe", B1["eSTe"] + oc + 1)]
            if oc >= 1:
                w.append(("eTsg", B1["eTsg"] + 4 * oc))
            E("vector", (lambda oc=oc: nc.vector.tensor_mul(
                SGb[:, :], STb[oc % 2][:, :], GOb[:, :])),
              waits=w, incs=[("eSG", 1)])
            if oc + 1 < DC:
                eload(oc + 1)
            sgtrans(oc)
            if oc + 1 < DC:
                xrload(oc + 1)
        if p == 0:
            scr_gate = [(s_, C(s_)) for s_ in
                        ("qSA0", "qSA1", "qSO0", "qSO1", "qSF", "qSB0", "qSB1")]

    sch.phase("fin")
    fw = [("qO0", C("qO0")), ("qO1", C("qO1"))]
    if C("qDB"):
        fw.append(("qDB", C("qDB")))
    E("gpsimd", None, waits=fw)

    phases = sch.phases if PHASE_LIMIT is None else sch.phases[:PHASE_LIMIT]
    if OP_LIMIT is not None:
        nm, ops0 = phases[-1]
        phases = phases[:-1] + [(nm, [(e, f, tuple(w for w in ws if _op_ok(w, ops0[:OP_LIMIT])), i)
                                      for (e, f, ws, i) in ops0[:OP_LIMIT]])]
    for name, ops in phases:
        engs = [e for e in ENGS if any(o[0] == e for o in ops)]
        if not engs:
            continue
        with nc.Block(name) as blk:
            for eng in engs:
                myops = [(fn, w, i) for (e, fn, w, i) in ops if e == eng]

                def body(engine, myops=myops):
                    for fn, waits, incs in myops:
                        for s_, v in waits:
                            engine.wait_ge(S[s_], 16 * v if s_ in DMASEMS else v)
                        if fn is None:
                            continue
                        ins = fn()
                        for s_, n in incs:
                            ins = ins.then_inc(S[s_], 16 * n if s_ in DMASEMS else n)

                getattr(blk, eng)(body)
    return nc, ctx


_CACHE = {}


def _get_nc():
    if "nc" not in _CACHE:
        nc, ctx = _build()
        _CACHE["nc"] = nc
        _CACHE["ctx"] = ctx
    return _CACHE["nc"]


def make_core_inputs(L0_attn_x, L0_attn_wq_w, L0_attn_wq_b, L0_attn_wk_w,
                     L0_attn_wk_b, L0_attn_wv_w, L0_attn_wv_b, L0_attn_wo_w,
                     L0_attn_wo_b, L0_ffn_lora_A, L0_ffn_lora_B, L0_ffn_W,
                     L0_ffn_bias, L0_ffn_lora_weights):
    x = np.ascontiguousarray(np.asarray(L0_attn_x, dtype=np.float32))
    f32 = lambda a: np.ascontiguousarray(np.asarray(a, dtype=np.float32))
    shared = {
        "wq": f32(L0_attn_wq_w), "wk": f32(L0_attn_wk_w), "wv": f32(L0_attn_wv_w),
        "wo": f32(L0_attn_wo_w), "wf": f32(L0_ffn_W),
        "bqv": f32(L0_attn_wq_b).reshape(32, 128),
        "bov": f32(L0_attn_wo_b).reshape(32, 128),
        "bfv": f32(L0_ffn_bias).reshape(32, 128),
        "bkv": f32(L0_attn_wk_b).reshape(1, 128),
        "bvv": f32(L0_attn_wv_b).reshape(1, 128),
        "av": f32(L0_ffn_lora_A).reshape(128, D),
        "bv_": f32(L0_ffn_lora_B),
        "lwr": np.ascontiguousarray(
            np.repeat(np.asarray(L0_ffn_lora_weights, dtype=np.float32), 16)[:, None]),
        "idbd": np.eye(128, dtype=ml_dtypes.bfloat16),
        "idfd": np.eye(128, dtype=np.float32),
    }
    kk = np.arange(128)[:, None]
    qq = np.arange(256)[None, :]
    tri = (kk <= qq).astype(np.float32)
    tri1 = (kk + 128 <= qq).astype(np.float32)
    ones = np.ones((128, 256), np.float32)
    zero = np.zeros((128, 256), np.float32)

    def mk(h):
        if h == 0:
            pat = [(tri, ones), (tri1, ones), (zero, ones), (zero, ones),
                   (zero, tri), (zero, tri1), (zero, zero), (zero, zero)]
        else:
            pat = [(ones, ones), (ones, ones), (tri, ones), (tri1, ones),
                   (zero, ones), (zero, ones), (zero, tri), (zero, tri1)]
        m = np.stack([np.concatenate(pq, axis=1) for pq in pat])
        return np.ascontiguousarray(m.astype(ml_dtypes.bfloat16))

    masks = [mk(0), mk(1)]
    in_maps = []
    xaT_cache = {}
    for c in range(8):
        b, h = c // 2, c % 2
        xb = x[b]
        if b not in xaT_cache:
            xaT_cache[b] = np.ascontiguousarray(xb.T)
        xaT = xaT_cache[b]
        m = dict(shared)
        m["x_all"] = xb
        m["x_allT"] = xaT
        m["x_own"] = np.ascontiguousarray(xb.reshape(16, 256, D)[h::2].reshape(TOWN, D))
        m["maskv"] = masks[h]
        in_maps.append(m)
    return in_maps


def kernel(**inputs):
    nc = _get_nc()
    in_maps = make_core_inputs(**inputs)
    res = run_bass_kernel_spmd(nc, in_maps, core_ids=list(range(8)))
    out = np.empty((4, TALL, D), dtype=np.float32)
    for c in range(8):
        b, h = c // 2, c % 2
        r = np.asarray(res.results[c]["out_own"], dtype=np.float32)
        out[b].reshape(16, 256, D)[h::2] = r.reshape(8, 256, D)
    return out


# version 23 (final)
# speedup vs baseline: 1.0516x; 1.0243x over previous
import numpy as np
import ml_dtypes
from contextlib import ExitStack

import concourse.bass as bass
import concourse.mybir as mybir
from concourse.bass_utils import run_bass_kernel_spmd

F32 = mybir.dt.float32
BF16 = mybir.dt.bfloat16
AF = mybir.ActivationFunctionType
ALU = mybir.AluOpType
AX = mybir.AxisListType

D = 4096
DC = 32
HD = 128
NH = 32
TP = 512
NPASS = 4
TALL = 4096
TOWN = 2048
SC = 0.08838834764831845
EPS = 1e-5
NPS = 3
NPT = 6

ENGS = ("sync", "scalar", "vector", "tensor", "gpsimd")
BUILD_IN_ATT0 = True
DBG = False
PHASE_LIMIT = None
OP_LIMIT = None


def _op_ok(wait, ops):
    s, v = wait
    tot = 0
    for (_, _, _, incs) in ops:
        for s2, n in incs:
            if s2 == s:
                tot += n
    return v <= tot


class Sched:
    def __init__(self):
        self.phases = []
        self.ops = None
        self.cnt = {}

    def phase(self, name):
        self.ops = []
        self.phases.append((name, self.ops))

    def emit(self, eng, fn, waits=(), incs=()):
        for s, v in waits:
            have = self.cnt.get(s, 0)
            assert v <= have, f"wait {s}>={v} but only {have} scheduled"
        self.ops.append((eng, fn, tuple(waits), tuple(incs)))
        for s, n in incs:
            self.cnt[s] = self.cnt.get(s, 0) + n

    def c(self, s):
        return self.cnt.get(s, 0)


def _build():
    nc = bass.Bass("TRN2", target_bir_lowering=False)

    x_all = nc.dram_tensor("x_all", [TALL, D], F32, kind="ExternalInput")
    x_own = nc.dram_tensor("x_own", [TOWN, D], F32, kind="ExternalInput")
    x_allT = nc.dram_tensor("x_allT", [D, TALL], F32, kind="ExternalInput")
    wq = nc.dram_tensor("wq", [D, D], F32, kind="ExternalInput")
    wk = nc.dram_tensor("wk", [HD, D], F32, kind="ExternalInput")
    wv = nc.dram_tensor("wv", [HD, D], F32, kind="ExternalInput")
    wo = nc.dram_tensor("wo", [D, D], F32, kind="ExternalInput")
    wf = nc.dram_tensor("wf", [D, D], F32, kind="ExternalInput")
    bqv = nc.dram_tensor("bqv", [32, 128], F32, kind="ExternalInput")
    bov = nc.dram_tensor("bov", [32, 128], F32, kind="ExternalInput")
    bfv = nc.dram_tensor("bfv", [32, 128], F32, kind="ExternalInput")
    bkv = nc.dram_tensor("bkv", [1, 128], F32, kind="ExternalInput")
    bvv = nc.dram_tensor("bvv", [1, 128], F32, kind="ExternalInput")
    av = nc.dram_tensor("av", [128, D], F32, kind="ExternalInput")
    bv_ = nc.dram_tensor("bv_", [8, D, 16], F32, kind="ExternalInput")
    lwr = nc.dram_tensor("lwr", [128, 1], F32, kind="ExternalInput")
    maskv = nc.dram_tensor("maskv", [8, 128, 512], BF16, kind="ExternalInput")
    idbd = nc.dram_tensor("idbd", [128, 128], BF16, kind="ExternalInput")
    idfd = nc.dram_tensor("idfd", [128, 128], F32, kind="ExternalInput")
    outp = nc.dram_tensor("out_own", [TOWN, D], F32, kind="ExternalOutput")
    wq_scr = nc.dram_tensor("wq_scr", [D, D], BF16, kind="Internal")
    wo_scr = nc.dram_tensor("wo_scr", [D, D], BF16, kind="Internal")
    wf_scr = nc.dram_tensor("wf_scr", [D, D], BF16, kind="Internal")
    bt_scr = nc.dram_tensor("bt_scr", [D, 128], BF16, kind="Internal")
    if DBG:
        dKT = nc.dram_tensor("dKT", [128, TALL], BF16, kind="ExternalOutput")
        dVT = nc.dram_tensor("dVT", [128, TALL], BF16, kind="ExternalOutput")
        dXN = nc.dram_tensor("dXN", [128, DC * TP], BF16, kind="ExternalOutput")
        dAT = nc.dram_tensor("dAT", [128, NH * TP], BF16, kind="ExternalOutput")
        dHS = nc.dram_tensor("dHS", [128, TP], BF16, kind="ExternalOutput")

    ctx = ExitStack()
    sb = lambda n, sh, dt: ctx.enter_context(nc.sbuf_tensor(n, sh, dt))
    psr = lambda n, sh, dt: ctx.enter_context(nc.psum_tensor(n, sh, dt))

    KT = sb("KT", [128, TALL], BF16)
    VT = sb("VT", [128, TALL], BF16)
    XN = sb("XN", [128, DC * TP], BF16)
    ATT = sb("ATT", [128, NH * TP], BF16)
    A_T = sb("A_T", [128, D], BF16)
    QT = sb("QT", [128, 2 * TP], BF16)
    SQb = sb("SQb", [128, TP], BF16)
    PT = sb("PT", [128, NPT * TP], BF16)
    HSb = sb("HSb", [128, TP], BF16)
    PTS = sb("PTS", [128, 2 * TP], BF16)
    AUN = sb("AUN", [128, TP], BF16)
    WTA = [sb(f"WTA{i}", [128, D], BF16) for i in range(2)]
    WTB = sb("WTB", [128, D], BF16)
    BTT = [sb(f"BTT{i}", [128, 128], BF16) for i in range(2)]
    WFb = [sb(f"WFb{i}", [128, 512], F32) for i in range(2)]
    BSF = sb("BSF", [128, 128], F32)
    XF0 = [sb(f"XF0{i}", [128, 2048], F32) for i in range(2)]
    XF1 = [sb(f"XF1{i}", [128, 2048], F32) for i in range(2)]
    _xf1b_addr = nc.lookup_mloc(XF1[1]).addr
    WFb += [nc.alloc_sbuf_tensor_at(f"WFb{i}", [128, 512], F32,
                                    offset=_xf1b_addr + 2048 * (i - 2))
            for i in (2, 3)]
    XMf = sb("XMf", [128, D], F32)
    STb = [sb(f"STb{i}", [128, TP], F32) for i in range(2)]
    GOb = sb("GOb", [128, TP], F32)
    SGb = sb("SGb", [128, TP], F32)
    OSb = [sb(f"OSb{i}", [128, 128], F32) for i in range(4)]
    XRb = [sb(f"XRb{i}", [128, 128], F32) for i in range(4)]
    BQ = sb("BQ", [128, 32], F32)
    BO = sb("BO", [128, 32], F32)
    BF_ = sb("BF_", [128, 32], F32)
    BK = sb("BK", [128, 1], F32)
    BVc = sb("BVc", [128, 1], F32)
    COEF = sb("COEF", [128, 1], F32)
    MK = sb("MK", [128, 8 * 512], BF16)
    IDB = sb("IDB", [128, 128], BF16)
    IDF = sb("IDF", [128, 128], F32)
    ONEb = sb("ONEb", [128, 1], BF16)
    ONEr = sb("ONEr", [1, 128], F32)
    ONEbr = sb("ONEbr", [1, 128], BF16)
    BVB = sb("BVB", [1, 128], BF16)
    BALL = sb("BALL", [128, 64], F32)
    BN6 = [sb(f"BN6{i}", [128, 48], F32) for i in range(2)]
    BN2 = [sb(f"BN2{i}", [128, 2], F32) for i in range(2)]
    RST = [sb(f"RST{i}", [128, 1], F32) for i in range(2)]
    NMR = [sb(f"NMR{i}", [128, 1], F32) for i in range(2)]
    QM2 = sb("QM2", [1, 1], F32)
    BT1 = sb("BT1", [1, 1], F32)
    NB1 = sb("NB1", [1, 1], F32)
    KM8 = sb("KM8", [1, 8], F32)
    KSC = sb("KSC", [1, 1], F32)
    LIV = sb("LIV", [1, TP], F32)
    LIVB = sb("LIVB", [128, TP], F32)
    BST = sb("BST", [32, 128], F32)
    WKR = sb("WKR", [1, 128], BF16)
    WVR = sb("WVR", [1, 128], BF16)
    BVBC = sb("BVBC", [128, 128], F32)
    RSTC = sb("RSTC", [128, 4], F32)
    KSQ = ATT[:, 4096:8192]
    WKT = ATT[:, 8192:12288]
    WVT = ATT[:, 12288:16384]

    pSP = psr("pSP", [128, 1024], F32)
    pSQ = psr("pSQ", [128, 1024], F32)
    pAV = psr("pAV", [128, 512], F32)
    pL = psr("pL", [1, 512], F32)
    pMM = psr("pMM", [128, 512], F32)
    pT = psr("pT", [128, 512], F32)
    PXV = [pSP[:, 0:512], pSP[:, 512:1024], pSQ[:, 0:512], pSQ[:, 512:1024]]

    semnames = ["qX00", "qX01", "qX10", "qX11", "qWF0", "qWF1", "qWF2", "qWF3", "qBS",
                "qWA0", "qWA1", "qWB", "qBT0", "qBT1", "qR0", "qR1",
                "qSA0", "qSA1", "qSO0", "qSO1", "qSF", "qSB0", "qSB1",
                "qO0", "qO1", "qDB", "qI",
                "eTp", "eWTe", "eWTs", "eWTp", "eXNe", "eXNs", "eVM", "eXMa", "eBN", "eRS",
                "eCast",
                "eKV", "eKVe", "eQ", "eQev", "eSQ",
                "eB1", "eB2", "eB3", "eB4", "eB5", "eB6",
                "eSc", "eExp", "eMsk", "eAV", "eLS", "ePr",
                "eRec", "eBC", "eBCe", "eATw", "eAVe",
                "eS", "eSTe", "eG", "eGOe", "eSG", "eTsg", "eOS",
                "eHS", "eInit", "eVt", "eKm"]
    S = {n: ctx.enter_context(nc.semaphore(n)) for n in semnames}
    DMASEMS = {n for n in semnames if n.startswith("q")}

    sch = Sched()
    E, C = sch.emit, sch.c

    px_state = {"banks": None, "evts": []}
    wfb_tp = {0: 0, 1: 0, 2: 0, 3: 0}
    wfb_war = {2: [], 3: []}
    wfb_par0 = [0]

    def px_reset(banks):
        px_state["banks"] = banks
        px_state["evts"] = []

    def px_begin():
        banks, evts = px_state["banks"], px_state["evts"]
        n = len(evts)
        d = len(banks)
        w = [evts[n - d]] if n >= d else []
        return banks[n % d], w

    def px_end(sem):
        px_state["evts"].append((sem, C(sem)))

    class SlabSteps:

        def __init__(self, wsrc, row0, dst_wt, war_waits, scr=None, scr_sem=None,
                     bt_oc=None, bt_par=None, bt_sem=None, evac_eng="vector",
                     piecewise=False, tp_waits=()):
            self.wsrc, self.row0, self.dst_wt = wsrc, row0, dst_wt
            self.war = list(war_waits)
            self.scr, self.scr_sem = scr, scr_sem
            self.bt_oc, self.bt_par, self.bt_sem = bt_oc, bt_par, bt_sem
            self.evac_eng, self.piecewise = evac_eng, piecewise
            self.tp_waits = list(tp_waits)
            self.esem = {"vector": "eWTe", "scalar": "eWTs",
                         "gpsimd": "eWTp"}[evac_eng]
            self.par0 = wfb_par0[0]
            wfb_par0[0] = (self.par0 + 8) % 4
            self.total = 8 + (1 if bt_oc is not None else 0)
            self.g = 0
            self.ck = {}

        def remaining(self):
            return self.total - self.g

        def step(self):
            g = self.g
            self.g += 1
            if g < 8:
                par = (self.par0 + g) % 4
                qsem = f"qWF{par}"
                lw = [("eTp", wfb_tp[par])] + wfb_war.get(par, [])
                if par in wfb_war:
                    wfb_war[par] = []
                E("sync", (lambda par=par, g=g, wsrc=self.wsrc, row0=self.row0:
                           nc.sync.dma_start(
                    WFb[par][:, :], wsrc[row0:row0 + 128, 512 * g:512 * g + 512])),
                  waits=lw, incs=[(qsem, 1)])
                bank, pxw = px_begin()
                tw = [(qsem, C(qsem))] + pxw
                if g < 4:
                    tw += self.tp_waits
                for t4 in range(4):
                    def tf(par=par, t4=t4, bank=bank):
                        return nc.tensor.transpose(
                            bank[:, 128 * t4:128 * t4 + 128],
                            WFb[par][:, 128 * t4:128 * t4 + 128], IDF[:, :])
                    E("tensor", tf, waits=(tw if t4 == 0 else []),
                      incs=([("eTp", 1)] if t4 == 3 else []))
                wfb_tp[par] = C("eTp")
                dc0 = g * 4
                ee, es = self.evac_eng, self.esem
                def ef(bank=bank, dc0=dc0, dst_wt=self.dst_wt, ee=ee):
                    if ee == "vector":
                        return nc.vector.tensor_copy(
                            dst_wt[:, 128 * dc0:128 * dc0 + 512], bank[:, :])
                    if ee == "gpsimd":
                        return nc.gpsimd.tensor_copy(
                            dst_wt[:, 128 * dc0:128 * dc0 + 512], bank[:, :])
                    return nc.scalar.copy(
                        dst_wt[:, 128 * dc0:128 * dc0 + 512], bank[:, :])
                E(ee, ef, waits=[("eTp", C("eTp"))] + self.war, incs=[(es, 1)])
                px_end(es)
                if self.scr is not None and self.piecewise:
                    E("gpsimd", (lambda scr=self.scr, row0=self.row0, dc0=dc0,
                                 dst_wt=self.dst_wt: nc.gpsimd.dma_start(
                        scr[row0:row0 + 128, 128 * dc0:128 * dc0 + 512],
                        dst_wt[:, 128 * dc0:128 * dc0 + 512])),
                      waits=[(es, C(es))], incs=[(self.scr_sem, 1)])
                if g == 7:
                    if self.scr is not None and not self.piecewise:
                        E("gpsimd", (lambda scr=self.scr, row0=self.row0,
                                     dst_wt=self.dst_wt: nc.gpsimd.dma_start(
                            scr[row0:row0 + 128, :], dst_wt[:, :])),
                          waits=[(self.esem, C(self.esem))], incs=[(self.scr_sem, 1)])
                    if self.scr is not None:
                        self.ck[self.scr_sem] = C(self.scr_sem)
                    self.ck["eWTe"] = C("eWTe")
                    self.ck["eWTe2"] = C("eWTe")
            else:
                E("sync", (lambda bt_oc=self.bt_oc: nc.sync.dma_start(
                    BSF[:, :].rearrange("p (n r) -> p n r", n=8),
                    bv_[:, 128 * bt_oc:128 * bt_oc + 128, :].rearrange(
                        "n o r -> o n r"))),
                  waits=[("eTp", C("eTp"))], incs=[("qBS", 1)])
                bank, pxw = px_begin()
                E("tensor", (lambda bank=bank: nc.tensor.transpose(
                    bank[:, 0:128], BSF[:, :], IDF[:, :])),
                  waits=[("qBS", C("qBS"))] + pxw, incs=[("eTp", 1)])
                E("vector", (lambda bt_par=self.bt_par, bank=bank:
                             nc.vector.tensor_copy(
                    BTT[bt_par][:, :], bank[:, 0:128])),
                  waits=[("eTp", C("eTp"))] + self.war, incs=[("eWTe", 1)])
                px_end("eWTe")
                E("gpsimd", (lambda bt_oc=self.bt_oc, bt_par=self.bt_par:
                             nc.gpsimd.dma_start(
                    bt_scr[128 * bt_oc:128 * bt_oc + 128, :], BTT[bt_par][:, :])),
                  waits=[("eWTe", C("eWTe"))], incs=[(self.bt_sem, 1)])
                self.ck[self.bt_sem] = C(self.bt_sem)
                self.ck["eWTe"] = C("eWTe")

    def build_slab(wsrc, row0, dst_wt, war_waits, scr=None, scr_sem=None,
                   bt_oc=None, bt_par=None, bt_sem=None, evac_eng="vector",
                   piecewise=False, tp_waits=()):
        s = SlabSteps(wsrc, row0, dst_wt, war_waits, scr, scr_sem,
                      bt_oc, bt_par, bt_sem, evac_eng, piecewise, tp_waits)
        while s.remaining():
            s.step()
        return s.ck

    def ln_stream(x_dram, base_row, ntiles, kv, tile_cb=None, first_waits=()):
        tp_after = {}
        ap_after = {}
        evacs_after = {}

        def xload(tt):
            par = tt % 2
            row = base_row + 128 * tt
            for hh in range(2):
                buf = (XF0 if hh == 0 else XF1)[par]
                ww = ([("eXMa", ap_after[tt - 2])] if tt >= 2
                      else list(first_waits))
                E("sync", (lambda buf=buf, row=row, hh=hh: nc.sync.dma_start(
                    buf[:, :], x_dram[row:row + 128, 2048 * hh:2048 * hh + 2048])),
                  waits=ww, incs=[(f"qX{par}{hh}", 1)])

        xload(0)
        for tt in range(ntiles):
            par = tt % 2
            if tt + 1 < ntiles:
                xload(tt + 1)
            sb6 = BN6[par]
            statw = [("eXMa", ap_after[tt - 2])] if tt >= 2 else []
            for hh in range(2):
                buf = (XF0 if hh == 0 else XF1)[par]
                for k4 in range(4):
                    w = []
                    if k4 == 0:
                        w = [(f"qX{par}{hh}", C(f"qX{par}{hh}"))] + (statw if hh == 0 else [])
                    def bf(buf=buf, k4=k4, hh=hh, sb6=sb6):
                        return nc.vector.bn_stats(
                            sb6[:, 24 * hh + 6 * k4:24 * hh + 6 * k4 + 6],
                            buf[:, 512 * k4:512 * k4 + 512])
                    E("vector", bf, waits=w,
                      incs=([("eBN", 1)] if (hh == 1 and k4 == 3) else []))
            E("vector", (lambda par=par, sb6=sb6: nc.vector.bn_aggr(
                BN2[par][:, :], sb6[:, :])),
              waits=[("eBN", C("eBN"))], incs=[("eBN", 1)])
            E("vector", (lambda par=par: nc.vector.tensor_scalar_add(
                BN2[par][:, 1:2], BN2[par][:, 1:2], EPS)),
              waits=[("eBN", C("eBN"))], incs=[("eBN", 1)])
            E("scalar", (lambda par=par: nc.scalar.activation(
                RST[par][:, :], BN2[par][:, 1:2], AF.Sqrt, scale=1.0)),
              waits=[("eBN", C("eBN"))], incs=[("eRS", 1)])
            E("vector", (lambda par=par: nc.vector.reciprocal(
                RST[par][:, :], RST[par][:, :])),
              waits=[("eRS", C("eRS"))], incs=[("eRS", 1)])
            E("vector", (lambda par=par: nc.vector.scalar_tensor_tensor(
                NMR[par][:, :], BN2[par][:, 0:1], -1.0, RST[par][:, :],
                ALU.mult, ALU.mult)),
              waits=[("eRS", C("eRS"))], incs=[("eRS", 1)])
            rs_now = C("eRS")
            for qq in range(4):
                buf = (XF0 if qq < 2 else XF1)[par]
                bo = 1024 * (qq % 2)
                aw = [("eRS", rs_now)]
                if tt >= 1:
                    aw.append(("eTp", tp_after[(tt - 1, qq)]))
                def af(buf=buf, bo=bo, qq=qq, par=par):
                    return nc.scalar.activation(
                        XMf[:, 1024 * qq:1024 * qq + 1024], buf[:, bo:bo + 1024],
                        AF.Identity, bias=NMR[par][:, :], scale=RST[par][:, :])
                E("scalar", af, waits=aw, incs=[("eXMa", 1)])
                for g4 in range(2):
                    bank, pxw = px_begin()
                    tw = [("eXMa", C("eXMa"))] + pxw
                    for t4 in range(4):
                        dcl = g4 * 4 + t4
                        def tf(dcl=dcl, bank=bank, qq=qq):
                            return nc.tensor.transpose(
                                bank[:, 128 * (dcl % 4):128 * (dcl % 4) + 128],
                                XMf[:, 1024 * qq + 128 * dcl:1024 * qq + 128 * dcl + 128],
                                IDF[:, :])
                        E("tensor", tf, waits=(tw if t4 == 0 else []),
                          incs=([("eTp", 1)] if t4 == 3 else []))
                    dc0 = qq * 8 + g4 * 4
                    eng = "vector" if ((qq + g4) % 2 == 0) else "scalar"
                    esem = "eXNe" if eng == "vector" else "eXNs"
                    ew = [("eTp", C("eTp"))]
                    if kv and tt >= 4:
                        ew.append(("eKV", tt // 4))
                        ew.append(("eVM", tt - 4 + 1))
                    def ef(bank=bank, dc0=dc0, tt=tt, eng=eng):
                        dst = XN[:, :].rearrange("p (dcx f tc) -> p dcx f tc",
                                                 dcx=DC, f=4)[:, dc0:dc0 + 4, tt % 4, :]
                        srcv = bank[:, :].rearrange("p (f tc) -> p f tc", f=4)
                        if eng == "vector":
                            return nc.vector.tensor_copy(dst, srcv)
                        return nc.scalar.copy(dst, srcv)
                    E(eng, ef, waits=ew, incs=[(esem, 1)])
                    px_end(esem)
                tp_after[(tt, qq)] = C("eTp")
            ap_after[tt] = C("eXMa")
            evacs_after[tt] = [("eXNe", C("eXNe")), ("eXNs", C("eXNs"))]
            if kv:
                vb = [pT, pAV][tt % 2]
                for dc in range(DC + 1):
                    w = []
                    if dc == 0:
                        w = list(evacs_after[tt])
                        if tt >= 2:
                            w.append(("eKVe", C("eKVe") - 1))
                    if dc < DC:
                        def vf(dc=dc, tt=tt, vb=vb):
                            c0 = 512 * dc + 128 * (tt % 4)
                            return nc.tensor.matmul(
                                vb[:, 0:128], XN[:, c0:c0 + 128],
                                WVT[:, 128 * dc:128 * dc + 128],
                                start=(dc == 0), stop=False)
                    else:
                        def vf(tt=tt, vb=vb):
                            return nc.tensor.matmul(
                                vb[:, 0:128], ONEbr[:, :], BVB[:, :],
                                start=False, stop=True)
                    E("tensor", vf, waits=w,
                      incs=([("eVM", 1)] if dc == DC else []))
                E("scalar", (lambda tt=tt, vb=vb: nc.scalar.copy(
                    VT[:, 128 * tt:128 * tt + 128], vb[:, 0:128])),
                  waits=[("eVM", C("eVM"))], incs=[("eKVe", 1)])
                if tt % 4 == 3:
                    grp = tt // 4
                    for dc in range(DC):
                        w = []
                        if dc == 0:
                            for t_ in range(tt - 3, tt + 1):
                                w.extend(evacs_after[t_])
                            w = list({(s_, v) for s_, v in w})
                            if grp > 0:
                                w.append(("eKVe", C("eKVe") - 1))
                        def mf(dc=dc):
                            return nc.tensor.matmul(
                                pMM[:, :], WKT[:, 128 * dc:128 * dc + 128],
                                XN[:, 512 * dc:512 * dc + 512],
                                start=(dc == 0), stop=(dc == DC - 1))
                        E("tensor", mf, waits=w,
                          incs=([("eKV", 1)] if dc == DC - 1 else []))
                    E("scalar", (lambda grp=grp: nc.scalar.activation(
                        KT[:, 512 * grp:512 * grp + 512], pMM[:, :], AF.Identity,
                        bias=BK[:, :], scale=1.0)),
                      waits=[("eKV", C("eKV"))], incs=[("eKVe", 1)])
            if tile_cb is not None:
                tile_cb(tt)

    sch.phase("init")
    px_reset(PXV)
    E("gpsimd", lambda: nc.gpsimd.dma_start(
        MK[:, :].rearrange("p (j c) -> p j c", j=8),
        maskv[:, :, :].rearrange("j p c -> p j c")), incs=[("qI", 1)])
    E("gpsimd", lambda: nc.gpsimd.dma_start(IDB[:, :], idbd[:, :]), incs=[("qI", 1)])
    E("gpsimd", lambda: nc.gpsimd.dma_start(IDF[:, :], idfd[:, :]), incs=[("qI", 1)])
    E("gpsimd", lambda: nc.gpsimd.dma_start(COEF[:, :], lwr[:, :]), incs=[("qI", 1)])
    E("vector", lambda: nc.vector.memset(ONEb[:, :], 1.0), incs=[("eInit", 1)])
    E("vector", lambda: nc.vector.memset(ONEr[:, :], 1.0), incs=[("eInit", 1)])
    E("vector", lambda: nc.vector.memset(ONEbr[:, :], 1.0), incs=[("eInit", 1)])
    E("vector", lambda: nc.vector.tensor_scalar_mul(COEF[:, :], COEF[:, :], 2.0),
      waits=[("qI", 4)], incs=[("eInit", 1)])
    for src, dst, w_ in ((bqv, BQ, 32), (bov, BO, 32), (bfv, BF_, 32),
                         (bkv, BK, 1), (bvv, BVc, 1)):
        E("gpsimd", (lambda src=src, w_=w_: nc.gpsimd.dma_start(
            BST[0:w_, :], src[:, :])),
          waits=[("eInit", C("eInit"))], incs=[("qI", 1)])
        E("tensor", (lambda w_=w_: nc.tensor.transpose(
            pT[:, 0:w_], BST[0:w_, :], IDF[0:w_, 0:w_])),
          waits=[("qI", C("qI")), ("eInit", C("eInit"))], incs=[("eTp", 1)])
        E("vector", (lambda dst=dst, w_=w_: nc.vector.tensor_copy(
            dst[:, 0:w_], pT[:, 0:w_])),
          waits=[("eTp", C("eTp"))], incs=[("eInit", 1)])
    E("gpsimd", lambda: nc.gpsimd.dma_start(BSF[0:1, :], bvv[:, :]),
      waits=[("eTp", C("eTp"))], incs=[("qI", 1)])
    E("vector", lambda: nc.vector.tensor_copy(BVB[:, :], BSF[0:1, :]),
      waits=[("qI", C("qI"))], incs=[("eInit", 1)])
    build_slab(av, 0, A_T, war_waits=[])

    sch.phase("kv")
    px_reset(PXV)
    build_slab(wk, 0, WKT, war_waits=[])
    build_slab(wv, 0, WVT, war_waits=[])
    _xf0_addr = nc.lookup_mloc(XF0[0]).addr
    _xf0b_addr = nc.lookup_mloc(XF0[1]).addr
    XTBF = [nc.alloc_sbuf_tensor_at(f"XTBF{i}", [128, 512], F32,
                                    offset=_xf0_addr + 2048 * i) for i in range(4)]
    XTB = [nc.alloc_sbuf_tensor_at(f"XTB{i}", [128, 512], BF16,
                                   offset=_xf0b_addr + 1024 * i) for i in range(4)]
    SQX = [nc.alloc_sbuf_tensor_at(f"SQX{i}", [128, 512], BF16,
                                   offset=_xf0b_addr + 4096 + 1024 * i) for i in range(4)]
    _xf1_addr = nc.lookup_mloc(XF1[0]).addr
    MROW = nc.alloc_sbuf_tensor_at("MROW", [1, 512], F32, offset=_xf1_addr)
    NEGMB = nc.alloc_sbuf_tensor_at("NEGMB", [1, 512], BF16, offset=_xf1_addr + 2048)
    VROW = nc.alloc_sbuf_tensor_at("VROW", [1, 512], F32, offset=_xf1_addr + 4096)
    RSTR = nc.alloc_sbuf_tensor_at("RSTR", [1, 512], F32, offset=_xf1_addr + 6144)
    for name, wt, dst in (("k", WKT, WKR), ("v", WVT, WVR)):
        for dc in range(DC):
            def cf(wt=wt, dc=dc):
                return nc.tensor.matmul(pL[:, 0:128], ONEb[:, :],
                                        wt[:, 128 * dc:128 * dc + 128],
                                        start=(dc == 0), stop=(dc == DC - 1))
            w = ([("eWTe", C("eWTe")), ("eWTs", C("eWTs")),
                  ("eWTp", C("eWTp"))] if dc == 0 else [])
            if dc == 0 and name == "v":
                w.append(("eVM", C("eVM")))
            E("tensor", cf, waits=w, incs=([("eKV", 1)] if dc == DC - 1 else []))
        E("vector", (lambda dst=dst: nc.vector.tensor_copy(dst[:, :], pL[0:1, 0:128])),
          waits=[("eKV", C("eKV"))], incs=[("eVM", 1)])
    E("tensor", (lambda: nc.tensor.matmul(
        pAV[:, 0:128], ONEbr[:, :], BVB[:, :], start=True, stop=True)),
      waits=[("eInit", C("eInit"))], incs=[("eKV", 1)])
    E("vector", (lambda: nc.vector.tensor_copy(BVBC[:, :], pAV[:, 0:128])),
      waits=[("eKV", C("eKV"))], incs=[("eVM", 1)])

    pV = [PXV[0], PXV[1], PXV[2], PXV[3]]
    wq_prev = {0: None, 1: None}
    dcg = 0
    for g in range(8):
        for dc in range(DC):
            par = dc % 4
            qs = f"qX{par % 2}{par // 2}"
            ww = [("eCast", C("eCast") - 3)] if dcg >= 4 else []
            E("sync", (lambda dc=dc, g=g, par=par: nc.sync.dma_start(
                XTBF[par][:, :], x_allT[128 * dc:128 * dc + 128, 512 * g:512 * g + 512])),
              waits=ww, incs=[(qs, 1)])
            cw = [(qs, C(qs))]
            if dcg >= 4:
                cw.append(("eKV", C("eKV") - 3))
            E("vector", (lambda par=par: nc.vector.tensor_copy(
                XTB[par][:, :], XTBF[par][:, :])),
              waits=cw, incs=[("eCast", 1)])
            sw = [("eCast", C("eCast"))]
            if dcg >= 4:
                sw.append(("eKV", C("eKV") - 3))
            E("scalar", (lambda par=par: nc.scalar.activation(
                SQX[par][:, :], XTB[par][:, :], AF.Square)),
              waits=sw, incs=[("eSQ", 1)])
            w0 = [("eCast", C("eCast"))]
            if dc == 0:
                w0.append(("eKVe", C("eKVe")))
                w0.append(("eVM", C("eVM")))
                w0.append(("eBN", C("eBN")))
                w0.append(("eWTe", C("eWTe")))
                w0.append(("eWTs", C("eWTs")))
                w0.append(("eWTp", C("eWTp")))
            def kf(dc=dc):
                return nc.tensor.matmul(pMM[:, :], WKT[:, 128 * dc:128 * dc + 128],
                                        XTB[dc % 4][:, :], start=(dc == 0), stop=False)
            E("tensor", kf, waits=w0)
            def mf(dc=dc):
                return nc.tensor.matmul(pL[:, :], ONEb[:, :], XTB[dc % 4][:, :],
                                        start=(dc == 0), stop=(dc == DC - 1))
            E("tensor", mf)
            def qf(dc=dc):
                return nc.tensor.matmul(pT[0:1, :], ONEb[:, :], SQX[dc % 4][:, :],
                                        start=(dc == 0), stop=(dc == DC - 1))
            E("tensor", qf, waits=[("eSQ", C("eSQ"))])
            def vf(dc=dc):
                return nc.tensor.matmul(
                    pV[0][:, :], WVT[:, 128 * dc:128 * dc + 128],
                    XTB[dc % 4][:, :], start=(dc == 0), stop=False)
            E("tensor", vf, incs=[("eKV", 1)])
            dcg += 1
        E("vector", (lambda: nc.vector.tensor_scalar(
            MROW[:, :], pL[0:1, :], 1.0 / 4096.0, None, ALU.mult)),
          waits=[("eKV", C("eKV"))], incs=[("eBN", 1)])
        E("vector", (lambda: nc.vector.tensor_scalar(
            NEGMB[:, :], pL[0:1, :], -1.0 / 4096.0, None, ALU.mult)),
          waits=[("eBN", C("eBN"))], incs=[("eBN", 1)])
        E("vector", (lambda: nc.vector.tensor_scalar(
            VROW[:, :], pT[0:1, :], 1.0 / 4096.0, None, ALU.mult)),
          waits=[("eBN", C("eBN"))], incs=[("eBN", 1)])
        E("vector", (lambda: nc.vector.scalar_tensor_tensor(
            RSTR[:, :], MROW[:, :], -1.0, MROW[:, :], ALU.bypass, ALU.mult)),
          waits=[("eBN", C("eBN"))], incs=[("eBN", 1)])
        E("vector", (lambda: nc.vector.tensor_sub(VROW[:, :], VROW[:, :], RSTR[:, :])),
          waits=[("eBN", C("eBN"))], incs=[("eBN", 1)])
        E("vector", (lambda: nc.vector.tensor_scalar_add(VROW[:, :], VROW[:, :], EPS)),
          waits=[("eBN", C("eBN"))], incs=[("eBN", 1)])
        E("scalar", (lambda: nc.scalar.activation(RSTR[:, :], VROW[:, :], AF.Sqrt)),
          waits=[("eBN", C("eBN"))], incs=[("eRS", 1)])
        E("vector", (lambda: nc.vector.reciprocal(RSTR[:, :], RSTR[:, :])),
          waits=[("eRS", C("eRS"))], incs=[("eRS", 1)])
        E("tensor", (lambda: nc.tensor.matmul(
            pMM[:, :], WKR[:, :], NEGMB[:, :], start=False, stop=True)),
          waits=[("eBN", C("eBN"))], incs=[("eVM", 1)])
        E("tensor", (lambda: nc.tensor.matmul(
            pV[0][:, :], WVR[:, :], NEGMB[:, :], start=False, stop=True)),
          incs=[("eVM", 1)])
        w = [("eRS", C("eRS"))]
        if g > 0:
            w.append(("eBCe", C("eBCe")))
        E("tensor", (lambda: nc.tensor.matmul(
            pAV[:, :], ONEr[:, :], RSTR[:, :], start=True, stop=True)),
          waits=w, incs=[("eBC", 1)])
        E("vector", (lambda: nc.vector.tensor_copy(LIVB[:, :], pAV[:, :])),
          waits=[("eBC", C("eBC"))], incs=[("eBCe", 1)])
        E("vector", (lambda: nc.vector.tensor_mul(SGb[:, :], pMM[:, :], LIVB[:, :])),
          waits=[("eVM", C("eVM")), ("eBCe", C("eBCe"))], incs=[("eBN", 1)])
        E("vector", (lambda g=g: nc.vector.tensor_scalar(
            KT[:, 512 * g:512 * g + 512], SGb[:, :], BK[:, :], None, ALU.add)),
          waits=[("eBN", C("eBN"))], incs=[("eKVe", 1)])
        E("vector", (lambda: nc.vector.tensor_mul(SGb[:, :], pV[0][:, :], LIVB[:, :])),
          waits=[("eVM", C("eVM")), ("eBCe", C("eBCe")), ("eKVe", C("eKVe"))],
          incs=[("eBN", 1)])
        tp_banks = [pV[1], pV[2], pV[3], pV[1]]
        evac_cnt = C("eKVe")
        for i in range(4):
            w = [("eBN", C("eBN"))]
            if i == 3:
                w.append(("eKVe", evac_cnt + 1))
            def tf(i=i):
                return nc.tensor.transpose(tp_banks[i][:, 0:128],
                                           SGb[:, 128 * i:128 * i + 128],
                                           IDF[:, :])
            E("tensor", tf, waits=w, incs=[("eTp", 1)])
            def ef(g=g, i=i):
                return nc.vector.tensor_add(
                    VT[:, 128 * (4 * g + i):128 * (4 * g + i) + 128],
                    tp_banks[i][:, 0:128], BVBC[:, :])
            E("vector", ef, waits=[("eTp", C("eTp"))], incs=[("eKVe", 1)])
        for hq in range(4):
            hh = 4 * g + hq
            par = hh % 2
            war = [("eKVe", C("eKVe"))]
            if wq_prev[par] is not None:
                war.append((f"qSA{par}", wq_prev[par]))
            ck = build_slab(wq, 128 * hh, WTA[par], war_waits=war,
                            scr=wq_scr, scr_sem=f"qSA{par}",
                            tp_waits=[("eKVe", C("eKVe"))],
                            evac_eng=("vector" if hh % 2 == 0 else "scalar"))
            wq_prev[par] = ck[f"qSA{par}"]

    E("scalar", lambda: nc.scalar.activation(KSQ[:, :], KT[:, :], AF.Square),
      waits=[("eKVe", C("eKVe"))], incs=[("eKm", 1)])
    for g8 in range(8):
        w = [("eKm", C("eKm"))]
        def mf(g8=g8):
            return nc.tensor.matmul(pL[:, :], ONEb[:, :],
                                    KSQ[:, 512 * g8:512 * g8 + 512],
                                    start=True, stop=True)
        E("tensor", mf, waits=w, incs=[("eKm", 1)])
        E("vector", (lambda g8=g8: nc.vector.reduce_max(
            KM8[:, g8:g8 + 1], pL[0:1, :], AX.X)),
          waits=[("eKm", C("eKm"))], incs=[("eKm", 1)])
    E("vector", lambda: nc.vector.reduce_max(KSC[:, :], KM8[:, :], AX.X),
      waits=[("eKm", C("eKm"))], incs=[("eKm", 1)])
    E("vector", lambda: nc.vector.tensor_scalar_mul(
        KSC[:, :], KSC[:, :], -0.04419417382415922),
      waits=[("eKm", C("eKm"))], incs=[("eKm", 1)])
    if DBG:
        E("gpsimd", lambda: nc.gpsimd.dma_start(dKT[:, :], KT[:, :]),
          waits=[("eKVe", C("eKVe"))], incs=[("qDB", 1)])
        E("gpsimd", lambda: nc.gpsimd.dma_start(dVT[:, :], VT[:, :]),
          waits=[("eVt", C("eVt"))], incs=[("qDB", 1)])

    wq_ck = {}
    for p in range(NPASS):
        nk = 8 * p + 8
        rp, wpar = p % 2, (p + 1) % 2
        chain = p < 3

        sch.phase(f"xnb{p}")
        px_reset(PXV)
        fw = [("eCast", C("eCast")), ("eKV", C("eKV")), ("eBC", C("eBC")),
              ("eTp", max(wfb_tp[2], wfb_tp[3]))]
        ln_stream(x_own, TP * p, 4, kv=False, first_waits=fw)
        wfb_war[2] = wfb_war[3] = [("eXMa", C("eXMa"))]
        if DBG and p == 0:
            E("gpsimd", lambda: nc.gpsimd.dma_start(dXN[:, :], XN[:, :]),
              waits=[("eXNe", C("eXNe"))], incs=[("qDB", 1)])

        sch.phase(f"hst{p}")
        for dc in range(DC):
            def mf(dc=dc):
                return nc.tensor.matmul(pMM[:, :], A_T[:, 128 * dc:128 * dc + 128],
                                        XN[:, 512 * dc:512 * dc + 512],
                                        start=(dc == 0), stop=(dc == DC - 1))
            E("tensor", mf, incs=([("eHS", 1)] if dc == DC - 1 else []))
        E("vector", lambda: nc.vector.tensor_scalar(
            HSb[:, :], pMM[:, :], COEF[:, :], None, ALU.mult),
          waits=[("eHS", C("eHS"))], incs=[("eHS", 1)])
        if DBG and p == 0:
            E("gpsimd", lambda: nc.gpsimd.dma_start(dHS[:, :], HSb[:, :]),
              waits=[("eHS", C("eHS"))], incs=[("qDB", 1)])

        sch.phase(f"att{p}")
        px_reset([PXV[2], PXV[3]])
        B0 = {s: C(s) for s in semnames}
        jg = 0
        NPSp = 2 if (p == 0 and BUILD_IN_ATT0) else 3
        wq_gate = [("qSA0", C("qSA0")), ("qSA1", C("qSA1"))]

        def qload(h, p=p, wq_gate=wq_gate):
            w = list(wq_gate) if h < 2 else []
            if h >= 2:
                w.append(("eQ", B0["eQ"] + h - 1))
            E("sync", (lambda h=h: nc.sync.dma_start(
                WTA[h % 2][:, :], wq_scr[128 * h:128 * h + 128, :])),
              waits=w, incs=[(f"qWA{h % 2}", 1)])

        def qproj(h, p=p):
            w0 = [("eQev", B0["eQev"] + h)] if h >= 1 else []
            w0.append((f"qWA{h % 2}", C(f"qWA{h % 2}")))
            for dc in range(DC):
                def mf(h=h, dc=dc):
                    return nc.tensor.matmul(
                        pMM[:, :], WTA[h % 2][:, 128 * dc:128 * dc + 128],
                        XN[:, 512 * dc:512 * dc + 512],
                        start=(dc == 0), stop=(dc == DC - 1))
                E("tensor", mf, waits=(w0 if dc == 0 else []),
                  incs=([("eQ", 1)] if dc == DC - 1 else []))

        def qevac(h, p=p, chain=chain):
            E("scalar", (lambda h=h: nc.scalar.activation(
                QT[:, 512 * (h % 2):512 * (h % 2) + 512], pMM[:, :], AF.Identity,
                bias=BQ[:, h:h + 1], scale=1.0)),
              waits=[("eQ", B0["eQ"] + h + 1)], incs=[("eQev", 1)])
            if chain:
                w = [("eQev", C("eQev"))]
                if h >= 1:
                    w.append(("eB1", B0["eB1"] + h))
                E("scalar", (lambda h=h: nc.scalar.activation(
                    SQb[:, :], QT[:, 512 * (h % 2):512 * (h % 2) + 512], AF.Square)),
                  waits=w, incs=[("eSQ", 1)])

        wob_prev = {"qSO0": None, "qSF": None}

        def build_steps(h):
            war = [("qSF", wob_prev["qSF"])] if wob_prev["qSF"] else []
            if h < 1:
                war += [("qSA0", C("qSA0")), ("qSA1", C("qSA1"))]
            if wob_prev.get(f"qSB{h % 2}"):
                war.append((f"qSB{h % 2}", wob_prev[f"qSB{h % 2}"]))
            s1 = SlabSteps(wo, 128 * h, WTB, war, scr=wo_scr, scr_sem="qSO0",
                           piecewise=True, bt_oc=h, bt_par=h % 2,
                           bt_sem=f"qSB{h % 2}",
                           evac_eng=("vector" if h % 2 == 0 else "scalar"))
            while s1.remaining():
                s1.step()
                yield
            wob_prev["qSO0"] = s1.ck["qSO0"]
            wob_prev[f"qSB{h % 2}"] = s1.ck[f"qSB{h % 2}"]
            s2 = SlabSteps(wf, 128 * h, WTB, [("qSO0", wob_prev["qSO0"])],
                           scr=wf_scr, scr_sem="qSF", piecewise=True,
                           evac_eng=("scalar" if h % 2 == 0 else "vector"))
            while s2.remaining():
                s2.step()
                yield
            wob_prev["qSF"] = s2.ck["qSF"]

        qload(0)
        qload(1)
        qproj(0)
        qevac(0)

        for h in range(NH):
            qcol = 512 * (h % 2)
            filler = build_steps(h) if (p == 0 and BUILD_IN_ATT0) else iter(())
            if chain:
                w = [("eSQ", B0["eSQ"] + h + 1)]
                if h >= 1:
                    w.append(("eBCe", B0["eBCe"] + h))
                E("tensor", (lambda: nc.tensor.matmul(
                    pT[0:1, :], ONEb[:, :], SQb[:, :], start=True, stop=True)),
                  waits=w, incs=[("eB1", 1)])
                w = [("eB1", B0["eB1"] + h + 1)]
                if h >= 1:
                    w.append(("eB4", B0["eB4"] + h))
                E("vector", (lambda: nc.vector.reduce_max(QM2[:, :], pT[0:1, :], AX.X)),
                  waits=w, incs=[("eB2", 1)])
                w = [("eB2", B0["eB2"] + h + 1)]
                if h >= 1:
                    w.append(("eB5", B0["eB5"] + h))
                E("vector", (lambda: nc.vector.tensor_scalar(
                    NB1[:, :], QM2[:, :], -0.04419417382415922, KSC[:, :],
                    ALU.mult, ALU.add)),
                  waits=w, incs=[("eB4", 1)])
            if chain and p == 0 and h == 0:
                w = [("eB4", B0["eB4"] + h + 1), ("eB2", B0["eB2"] + h + 1)]
                E("tensor", (lambda: nc.tensor.matmul(
                    pT[:, 0:1], ONEr[:, :], NB1[:, :], start=True, stop=True)),
                  waits=w, incs=[("eB5", 1)])
                E("vector", (lambda h=h, wpar=wpar: nc.vector.tensor_copy(
                    BALL[:, 32 * wpar + h:32 * wpar + h + 1], pT[:, 0:1])),
                  waits=[("eB5", B0["eB5"] + h + 1)], incs=[("eB6", 1)])
                E("vector", (lambda h=h, rp=rp: nc.vector.tensor_copy(
                    BALL[:, 32 * rp + h:32 * rp + h + 1], pT[:, 0:1])),
                  waits=[("eB6", C("eB6"))], incs=[("eB6", 1)])

            if p == 0:
                nku = nk // 2
                jgp = jg // 2
                hb = max(h - 1, 0)
                for half in range(2):
                    w = [("eQev", B0["eQev"] + h + 1)] if half == 0 else []
                    if half == 0 and jgp >= 1:
                        w.append(("eExp", B0["eExp"] + jgp))
                    def sf(j=half, half=half, qcol=qcol):
                        return nc.tensor.matmul(
                            pSP[:, 512 * half:512 * half + 512],
                            KT[:, 128 * j:128 * j + 128],
                            QT[:, qcol:qcol + 512], start=True, stop=True)
                    E("tensor", sf, waits=w, incs=[("eSc", 1)])
                for u in range(nku):
                    ug = jgp + u
                    w = [("eSc", B0["eSc"] + jg + 2 * u + 2)]
                    if ug >= 3:
                        w.append(("eAV", B0["eAV"] + 2 * (ug - 3) + 2))
                        w.append(("ePr", B0["ePr"] + ug - 2))
                    if chain:
                        w.append(("eB6", B0["eB6"] + 2 * (hb + 1)))
                    def xf(ug=ug, hb=hb, rp=rp):
                        return nc.scalar.activation(
                            PT[:, 1024 * (ug % 3):1024 * (ug % 3) + 1024],
                            pSP[:, :], AF.Exp,
                            bias=BALL[:, 32 * rp + hb:32 * rp + hb + 1], scale=SC)
                    E("scalar", xf, waits=w, incs=[("eExp", 1)])
                    for half in range(2):
                        j = 2 * u + half
                        def mkf(ug=ug, half=half, jo=j):
                            c0 = 1024 * (ug % 3) + 512 * half
                            return nc.vector.tensor_mul(
                                PT[:, c0:c0 + 512], PT[:, c0:c0 + 512],
                                MK[:, 512 * jo:512 * jo + 512])
                        E("vector", mkf, waits=[("eExp", B0["eExp"] + ug + 1)],
                          incs=[("eMsk", 1)])
                    for half in range(2):
                        j = 2 * u + half
                        w2 = [("eMsk", B0["eMsk"] + 8 * h + j + 1)]
                        if j == 0 and h >= 1:
                            w2.append(("eAVe", B0["eAVe"] + h))
                        def avf(ug=ug, half=half, j=j, nk=nk):
                            c0 = 1024 * (ug % 3) + 512 * half
                            return nc.tensor.matmul(
                                pAV[:, :], VT[:, 128 * j:128 * j + 128],
                                PT[:, c0:c0 + 512],
                                start=(j == 0), stop=(j == nk - 1))
                        E("tensor", avf, waits=w2, incs=[("eAV", 1)])
                    w3 = [("eMsk", B0["eMsk"] + 8 * h + 2 * u + 1),
                          ("eMsk", B0["eMsk"] + 8 * h + 2 * u + 2)]
                    if ug >= 2:
                        w3.append(("eLS", B0["eLS"] + ug - 1))
                    def prf(ug=ug):
                        c0 = 1024 * (ug % 3)
                        return nc.vector.tensor_add(
                            PTS[:, 512 * (ug % 2):512 * (ug % 2) + 512],
                            PT[:, c0:c0 + 512], PT[:, c0 + 512:c0 + 1024])
                    E("vector", prf, waits=w3, incs=[("ePr", 1)])
                    if u + 1 < nku:
                        for half in range(2):
                            j2 = 2 * (u + 1) + half
                            w5 = ([("eExp", B0["eExp"] + ug + 1)]
                                  if half == 0 else [])
                            def sf(j2=j2, half=half, qcol=qcol):
                                return nc.tensor.matmul(
                                    pSP[:, 512 * half:512 * half + 512],
                                    KT[:, 128 * j2:128 * j2 + 128],
                                    QT[:, qcol:qcol + 512], start=True, stop=True)
                            E("tensor", sf, waits=w5, incs=[("eSc", 1)])
                    w4 = [("ePr", B0["ePr"] + ug + 1)]
                    if u == 0 and h >= 1:
                        w4.append(("eRec", B0["eRec"] + 2 * h - 1))
                    def lf(ug=ug, u=u, nku=nku):
                        return nc.tensor.matmul(
                            pL[:, :], ONEb[:, :],
                            PTS[:, 512 * (ug % 2):512 * (ug % 2) + 512],
                            start=(u == 0), stop=(u == nku - 1))
                    E("tensor", lf, waits=w4, incs=[("eLS", 1)])
                    for _ in range(4):
                        next(filler, None)
                    if chain and u == 1 and h >= 1:
                        w6 = [("eB4", B0["eB4"] + h + 1),
                              ("eB2", B0["eB2"] + h + 1)]
                        E("tensor", (lambda: nc.tensor.matmul(
                            pT[:, 0:1], ONEr[:, :], NB1[:, :],
                            start=True, stop=True)),
                          waits=w6, incs=[("eB5", 1)])
                        E("vector", (lambda h=h, wpar=wpar: nc.vector.tensor_copy(
                            BALL[:, 32 * wpar + h:32 * wpar + h + 1], pT[:, 0:1])),
                          waits=[("eB5", B0["eB5"] + h + 1)], incs=[("eB6", 1)])
                        E("vector", (lambda h=h, rp=rp: nc.vector.tensor_copy(
                            BALL[:, 32 * rp + h:32 * rp + h + 1], pT[:, 0:1])),
                          waits=[("eB6", C("eB6"))], incs=[("eB6", 1)])
                    if h + 1 < NH:
                        for dcq in range(8 * u, 8 * u + 8):
                            w0q = []
                            if dcq == 0:
                                w0q = [("eQev", B0["eQev"] + h + 1),
                                       (f"qWA{(h + 1) % 2}",
                                        C(f"qWA{(h + 1) % 2}"))]
                            def mfq(h1=h + 1, dc=dcq):
                                return nc.tensor.matmul(
                                    pMM[:, :],
                                    WTA[h1 % 2][:, 128 * dc:128 * dc + 128],
                                    XN[:, 512 * dc:512 * dc + 512],
                                    start=(dc == 0), stop=(dc == DC - 1))
                            E("tensor", mfq, waits=w0q,
                              incs=([("eQ", 1)] if dcq == DC - 1 else []))
            else:
                nku = nk // 2
                jgp = jg // 2
                for up in range(2):
                    for half in range(2):
                        jch = 2 * up + half
                        w = [("eQev", B0["eQev"] + h + 1)] if jch == 0 else []
                        if jgp + up >= 2 and half == 0:
                            w.append(("eExp", B0["eExp"] + jgp + up - 1))
                        def sf(j=jch, u=jgp + up, half=half, qcol=qcol):
                            bank = [pSP, pSQ][u % 2]
                            return nc.tensor.matmul(
                                bank[:, 512 * half:512 * half + 512],
                                KT[:, 128 * j:128 * j + 128],
                                QT[:, qcol:qcol + 512], start=True, stop=True)
                        E("tensor", sf, waits=w, incs=[("eSc", 1)])
                for u in range(nku):
                    ug = jgp + u
                    w = [("eSc", B0["eSc"] + jg + 2 * u + 2)]
                    if u >= 3:
                        w.append(("eAV", B0["eAV"] + jg + 2 * (u - 3) + 2))
                        w.append(("ePr", B0["ePr"] + jgp + u - 2))
                    def xf(ug=ug, h=h, rp=rp):
                        bank = [pSP, pSQ][ug % 2]
                        return nc.scalar.activation(
                            PT[:, 1024 * (ug % 3):1024 * (ug % 3) + 1024],
                            bank[:, :], AF.Exp,
                            bias=BALL[:, 32 * rp + h:32 * rp + h + 1], scale=SC)
                    E("scalar", xf, waits=w, incs=[("eExp", 1)])
                    for half in range(2):
                        j = 2 * u + half
                        if j >= nk - 8:
                            jo = j - (nk - 8)
                            def mkf(ug=ug, half=half, jo=jo):
                                c0 = 1024 * (ug % 3) + 512 * half
                                return nc.vector.tensor_mul(
                                    PT[:, c0:c0 + 512], PT[:, c0:c0 + 512],
                                    MK[:, 512 * jo:512 * jo + 512])
                            E("vector", mkf, waits=[("eExp", B0["eExp"] + ug + 1)],
                              incs=[("eMsk", 1)])
                    for half in range(2):
                        j = 2 * u + half
                        if j >= nk - 8:
                            w2 = [("eMsk", B0["eMsk"] + 8 * h + (j - (nk - 8)) + 1)]
                        else:
                            w2 = [("eExp", B0["eExp"] + ug + 1)]
                        if j == 0 and h >= 1:
                            w2.append(("eAVe", B0["eAVe"] + h))
                        def avf(ug=ug, half=half, j=j, nk=nk):
                            c0 = 1024 * (ug % 3) + 512 * half
                            return nc.tensor.matmul(
                                pAV[:, :], VT[:, 128 * j:128 * j + 128],
                                PT[:, c0:c0 + 512],
                                start=(j == 0), stop=(j == nk - 1))
                        E("tensor", avf, waits=w2, incs=[("eAV", 1)])
                    w3 = []
                    for half in range(2):
                        j = 2 * u + half
                        if j >= nk - 8:
                            w3.append(("eMsk", B0["eMsk"] + 8 * h + (j - (nk - 8)) + 1))
                        else:
                            w3.append(("eExp", B0["eExp"] + ug + 1))
                    if ug >= 2:
                        w3.append(("eLS", B0["eLS"] + ug - 1))
                    def prf(ug=ug):
                        c0 = 1024 * (ug % 3)
                        return nc.vector.tensor_add(
                            PTS[:, 512 * (ug % 2):512 * (ug % 2) + 512],
                            PT[:, c0:c0 + 512], PT[:, c0 + 512:c0 + 1024])
                    E("vector", prf, waits=w3, incs=[("ePr", 1)])
                    w4 = [("ePr", B0["ePr"] + ug + 1)]
                    if u == 0 and h >= 1:
                        w4.append(("eRec", B0["eRec"] + 2 * h - 1))
                    def lf(ug=ug, u=u, nku=nku):
                        return nc.tensor.matmul(
                            pL[:, :], ONEb[:, :],
                            PTS[:, 512 * (ug % 2):512 * (ug % 2) + 512],
                            start=(u == 0), stop=(u == nku - 1))
                    E("tensor", lf, waits=w4, incs=[("eLS", 1)])
                    if u + 2 < nku:
                        for half in range(2):
                            j2 = 2 * (u + 2) + half
                            w5 = [("eExp", B0["eExp"] + ug + 1)] if half == 0 else []
                            def sf(j2=j2, u2=jgp + u + 2, half=half, qcol=qcol):
                                bank = [pSP, pSQ][u2 % 2]
                                return nc.tensor.matmul(
                                    bank[:, 512 * half:512 * half + 512],
                                    KT[:, 128 * j2:128 * j2 + 128],
                                    QT[:, qcol:qcol + 512], start=True, stop=True)
                            E("tensor", sf, waits=w5, incs=[("eSc", 1)])
                    if chain and u == 1:
                        w6 = [("eB4", B0["eB4"] + h + 1), ("eB2", B0["eB2"] + h + 1)]
                        E("tensor", (lambda: nc.tensor.matmul(
                            pT[:, 0:1], ONEr[:, :], NB1[:, :], start=True, stop=True)),
                          waits=w6, incs=[("eB5", 1)])

            if chain and p > 0:
                E("vector", (lambda h=h, wpar=wpar: nc.vector.tensor_copy(
                    BALL[:, 32 * wpar + h:32 * wpar + h + 1], pT[:, 0:1])),
                  waits=[("eB5", B0["eB5"] + h + 1)], incs=[("eB6", 1)])

            w = [("eAV", B0["eAV"] + jg + nk)]
            if h >= 1:
                w.append(("eATw", B0["eATw"] + h))
            E("vector", (lambda: nc.vector.tensor_copy(AUN[:, :], pAV[:, :])),
              waits=w, incs=[("eAVe", 1)])
            w = [("eLS", B0["eLS"] + (jg + nk) // 2)]
            if h >= 1:
                w.append(("eBC", B0["eBC"] + h))
            E("vector", (lambda: nc.vector.tensor_copy(LIV[:, :], pL[0:1, :])),
              waits=w, incs=[("eRec", 1)])
            E("vector", (lambda: nc.vector.reciprocal(LIV[:, :], LIV[:, :])),
              waits=[("eRec", C("eRec"))], incs=[("eRec", 1)])

            if h + 1 < NH:
                if h + 2 < NH:
                    qload(h + 2)
                if p != 0:
                    qproj(h + 1)

            w = [("eRec", B0["eRec"] + 2 * h + 2)]
            if chain:
                w.append(("eB6", B0["eB6"] + (2 * (h + 1) if p == 0 else h + 1)))
            elif h >= 1:
                w.append(("eBCe", B0["eBCe"] + h))
            E("tensor", (lambda: nc.tensor.matmul(
                pT[:, :], ONEr[:, :], LIV[:, :], start=True, stop=True)),
              waits=w, incs=[("eBC", 1)])
            E("vector", (lambda: nc.vector.tensor_copy(LIVB[:, :], pT[:, :])),
              waits=[("eBC", B0["eBC"] + h + 1)], incs=[("eBCe", 1)])
            E("vector", (lambda h=h: nc.vector.tensor_mul(
                ATT[:, 512 * h:512 * h + 512], AUN[:, :], LIVB[:, :])),
              waits=[("eBCe", B0["eBCe"] + h + 1),
                     ("eAVe", B0["eAVe"] + h + 1)], incs=[("eATw", 1)])

            if h + 1 < NH:
                qevac(h + 1)
            for _ in filler:
                pass
            jg += nk

        if DBG and p == 0:
            E("gpsimd", lambda: nc.gpsimd.dma_start(dAT[:, :], ATT[:, :]),
              waits=[("eATw", C("eATw"))], incs=[("qDB", 1)])

        if p == 0 and not BUILD_IN_ATT0:
            prevt = {"qSF": None, "qSO0": None, "qSB0": None, "qSB1": None}
            for oc in range(DC):
                war = [("qSF", prevt["qSF"])] if prevt["qSF"] else []
                if oc < 1:
                    war += [("qSA0", C("qSA0")), ("qSA1", C("qSA1"))]
                if prevt.get(f"qSB{oc % 2}"):
                    war.append((f"qSB{oc % 2}", prevt[f"qSB{oc % 2}"]))
                ck = build_slab(wo, 128 * oc, WTB, war_waits=war,
                                scr=wo_scr, scr_sem="qSO0", piecewise=True,
                                bt_oc=oc, bt_par=oc % 2, bt_sem=f"qSB{oc % 2}",
                                evac_eng=("vector" if oc % 2 == 0 else "scalar"))
                prevt["qSO0"] = ck["qSO0"]
                prevt[f"qSB{oc % 2}"] = ck[f"qSB{oc % 2}"]
                war = [("qSO0", prevt["qSO0"])]
                ck = build_slab(wf, 128 * oc, WTB, war_waits=war,
                                scr=wf_scr, scr_sem="qSF", piecewise=True,
                                evac_eng=("scalar" if oc % 2 == 0 else "vector"))
                prevt["qSF"] = ck["qSF"]

        sch.phase(f"epi{p}")
        B1 = {s: C(s) for s in semnames}
        w_gate = [(s_, C(s_)) for s_ in
                  ("qSO0", "qSO1", "qSF", "qSB0", "qSB1")]

        def eload(oc, p=p, w_gate=w_gate):
            w = list(w_gate) if oc < 2 else []
            if oc >= 2:
                w.append(("eS", B1["eS"] + oc - 1))
            E("sync", (lambda oc=oc: nc.sync.dma_start(
                WTA[oc % 2][:, :], wo_scr[128 * oc:128 * oc + 128, :])),
              waits=w, incs=[(f"qWA{oc % 2}", 1)])
            w = list(w_gate) if oc < 1 else []
            if oc >= 1:
                w.append(("eG", B1["eG"] + oc))
            E("sync", (lambda oc=oc: nc.sync.dma_start(
                WTB[:, :], wf_scr[128 * oc:128 * oc + 128, :])),
              waits=w, incs=[("qWB", 1)])
            w = list(w_gate) if oc < 2 else []
            if oc >= 2:
                w.append(("eG", B1["eG"] + oc - 1))
            E("sync", (lambda oc=oc: nc.sync.dma_start(
                BTT[oc % 2][:, :], bt_scr[128 * oc:128 * oc + 128, :])),
              waits=w, incs=[(f"qBT{oc % 2}", 1)])

        def xrload(oc, p=p):
            for i in range(4):
                w = [("eOS", C("eOS"))] if oc >= 1 else []
                def lf(oc=oc, i=i):
                    return nc.sync.dma_start(
                        XRb[i][:, :],
                        x_own[TP * p + 128 * i:TP * p + 128 * i + 128,
                              128 * oc:128 * oc + 128])
                E("sync", lf, waits=w, incs=[(f"qR{oc % 2}", 1)])

        def sgtrans(oc, p=p):
            for i in range(4):
                w = [("eSG", B1["eSG"] + oc + 1)] if i == 0 else []
                if i == 0 and oc >= 1:
                    w.append(("eOS", B1["eOS"] + 4 * oc))
                def tf(i=i):
                    return nc.tensor.transpose(
                        pAV[:, 128 * i:128 * i + 128], SGb[:, 128 * i:128 * i + 128],
                        IDF[:, :])
                E("tensor", tf, waits=w, incs=[("eTsg", 1)])
            for i in range(4):
                w = [("eTsg", B1["eTsg"] + 4 * oc + 4),
                     (f"qR{oc % 2}", C(f"qR{oc % 2}")),
                     ("qO0", C("qO0")), ("qO1", C("qO1"))]
                def vf(oc=oc, i=i):
                    return nc.vector.tensor_add(
                        OSb[i][:, :], pAV[:, 128 * i:128 * i + 128], XRb[i][:, :])
                E("vector", vf, waits=w, incs=[("eOS", 1)])
            for i in range(4):
                def of(oc=oc, i=i, p=p):
                    return nc.gpsimd.dma_start(
                        outp[TP * p + 128 * i:TP * p + 128 * i + 128,
                             128 * oc:128 * oc + 128],
                        OSb[i][:, :])
                E("gpsimd", of, waits=[("eOS", B1["eOS"] + 4 * oc + i + 1)],
                  incs=[(f"qO{oc % 2}", 1)])

        eload(0)
        xrload(0)

        for oc in range(DC):
            w0 = [("eGOe", B1["eGOe"] + oc)] if oc >= 1 else []
            w0.append(("qWB", C("qWB")))
            w0.append((f"qBT{oc % 2}", C(f"qBT{oc % 2}")))
            for dc in range(DC):
                def gf(dc=dc):
                    return nc.tensor.matmul(
                        pMM[:, :], WTB[:, 128 * dc:128 * dc + 128],
                        XN[:, 512 * dc:512 * dc + 512],
                        start=(dc == 0), stop=False)
                E("tensor", gf, waits=(w0 if dc == 0 else []))
            def gl(oc=oc):
                return nc.tensor.matmul(pMM[:, :], BTT[oc % 2][:, :], HSb[:, :],
                                        start=False, stop=True)
            E("tensor", gl, incs=[("eG", 1)])
            w1 = [("eSTe", B1["eSTe"] + oc)] if oc >= 1 else []
            w1.append((f"qWA{oc % 2}", C(f"qWA{oc % 2}")))
            for hc in range(DC):
                def sf(hc=hc, oc=oc):
                    return nc.tensor.matmul(
                        pT[:, :], WTA[oc % 2][:, 128 * hc:128 * hc + 128],
                        ATT[:, 512 * hc:512 * hc + 512],
                        start=(hc == 0), stop=(hc == DC - 1))
                E("tensor", sf, waits=(w1 if hc == 0 else []),
                  incs=([("eS", 1)] if hc == DC - 1 else []))
            w = [("eG", B1["eG"] + oc + 1)]
            if oc >= 1:
                w.append(("eSG", B1["eSG"] + oc))
            E("scalar", (lambda oc=oc: nc.scalar.activation(
                GOb[:, :], pMM[:, :], AF.Silu, bias=BF_[:, oc:oc + 1], scale=1.0)),
              waits=w, incs=[("eGOe", 1)])
            w = [("eS", B1["eS"] + oc + 1)]
            if oc >= 2:
                w.append(("eSG", B1["eSG"] + oc - 1))
            E("scalar", (lambda oc=oc: nc.scalar.activation(
                STb[oc % 2][:, :], pT[:, :], AF.Identity, bias=BO[:, oc:oc + 1], scale=1.0)),
              waits=w, incs=[("eSTe", 1)])
            w = [("eGOe", B1["eGOe"] + oc + 1), ("eSTe", B1["eSTe"] + oc + 1)]
            if oc >= 1:
                w.append(("eTsg", B1["eTsg"] + 4 * oc))
            E("vector", (lambda oc=oc: nc.vector.tensor_mul(
                SGb[:, :], STb[oc % 2][:, :], GOb[:, :])),
              waits=w, incs=[("eSG", 1)])
            if oc + 1 < DC:
                eload(oc + 1)
            sgtrans(oc)
            if oc + 1 < DC:
                xrload(oc + 1)
        if p == 0:
            scr_gate = [(s_, C(s_)) for s_ in
                        ("qSA0", "qSA1", "qSO0", "qSO1", "qSF", "qSB0", "qSB1")]

    sch.phase("fin")
    fw = [("qO0", C("qO0")), ("qO1", C("qO1"))]
    if C("qDB"):
        fw.append(("qDB", C("qDB")))
    E("gpsimd", None, waits=fw)

    phases = sch.phases if PHASE_LIMIT is None else sch.phases[:PHASE_LIMIT]
    if OP_LIMIT is not None:
        nm, ops0 = phases[-1]
        phases = phases[:-1] + [(nm, [(e, f, tuple(w for w in ws if _op_ok(w, ops0[:OP_LIMIT])), i)
                                      for (e, f, ws, i) in ops0[:OP_LIMIT]])]
    for name, ops in phases:
        engs = [e for e in ENGS if any(o[0] == e for o in ops)]
        if not engs:
            continue
        with nc.Block(name) as blk:
            for eng in engs:
                myops = [(fn, w, i) for (e, fn, w, i) in ops if e == eng]

                def body(engine, myops=myops):
                    for fn, waits, incs in myops:
                        for s_, v in waits:
                            engine.wait_ge(S[s_], 16 * v if s_ in DMASEMS else v)
                        if fn is None:
                            continue
                        ins = fn()
                        for s_, n in incs:
                            ins = ins.then_inc(S[s_], 16 * n if s_ in DMASEMS else n)

                getattr(blk, eng)(body)
    return nc, ctx


_CACHE = {}


def _get_nc():
    if "nc" not in _CACHE:
        nc, ctx = _build()
        _CACHE["nc"] = nc
        _CACHE["ctx"] = ctx
    return _CACHE["nc"]


def make_core_inputs(L0_attn_x, L0_attn_wq_w, L0_attn_wq_b, L0_attn_wk_w,
                     L0_attn_wk_b, L0_attn_wv_w, L0_attn_wv_b, L0_attn_wo_w,
                     L0_attn_wo_b, L0_ffn_lora_A, L0_ffn_lora_B, L0_ffn_W,
                     L0_ffn_bias, L0_ffn_lora_weights):
    x = np.ascontiguousarray(np.asarray(L0_attn_x, dtype=np.float32))
    f32 = lambda a: np.ascontiguousarray(np.asarray(a, dtype=np.float32))
    shared = {
        "wq": f32(L0_attn_wq_w), "wk": f32(L0_attn_wk_w), "wv": f32(L0_attn_wv_w),
        "wo": f32(L0_attn_wo_w), "wf": f32(L0_ffn_W),
        "bqv": f32(L0_attn_wq_b).reshape(32, 128),
        "bov": f32(L0_attn_wo_b).reshape(32, 128),
        "bfv": f32(L0_ffn_bias).reshape(32, 128),
        "bkv": f32(L0_attn_wk_b).reshape(1, 128),
        "bvv": f32(L0_attn_wv_b).reshape(1, 128),
        "av": f32(L0_ffn_lora_A).reshape(128, D),
        "bv_": f32(L0_ffn_lora_B),
        "lwr": np.ascontiguousarray(
            np.repeat(np.asarray(L0_ffn_lora_weights, dtype=np.float32), 16)[:, None]),
        "idbd": np.eye(128, dtype=ml_dtypes.bfloat16),
        "idfd": np.eye(128, dtype=np.float32),
    }
    kk = np.arange(128)[:, None]
    qq = np.arange(256)[None, :]
    tri = (kk <= qq).astype(np.float32)
    tri1 = (kk + 128 <= qq).astype(np.float32)
    ones = np.ones((128, 256), np.float32)
    zero = np.zeros((128, 256), np.float32)

    def mk(h):
        if h == 0:
            pat = [(tri, ones), (tri1, ones), (zero, ones), (zero, ones),
                   (zero, tri), (zero, tri1), (zero, zero), (zero, zero)]
        else:
            pat = [(ones, ones), (ones, ones), (tri, ones), (tri1, ones),
                   (zero, ones), (zero, ones), (zero, tri), (zero, tri1)]
        m = np.stack([np.concatenate(pq, axis=1) for pq in pat])
        return np.ascontiguousarray(m.astype(ml_dtypes.bfloat16))

    masks = [mk(0), mk(1)]
    in_maps = []
    xaT_cache = {}
    for c in range(8):
        b, h = c // 2, c % 2
        xb = x[b]
        if b not in xaT_cache:
            xaT_cache[b] = np.ascontiguousarray(xb.T)
        xaT = xaT_cache[b]
        m = dict(shared)
        m["x_all"] = xb
        m["x_allT"] = xaT
        m["x_own"] = np.ascontiguousarray(xb.reshape(16, 256, D)[h::2].reshape(TOWN, D))
        m["maskv"] = masks[h]
        in_maps.append(m)
    return in_maps


def kernel(**inputs):
    nc = _get_nc()
    in_maps = make_core_inputs(**inputs)
    res = run_bass_kernel_spmd(nc, in_maps, core_ids=list(range(8)))
    out = np.empty((4, TALL, D), dtype=np.float32)
    for c in range(8):
        b, h = c // 2, c % 2
        r = np.asarray(res.results[c]["out_own"], dtype=np.float32)
        out[b].reshape(16, 256, D)[h::2] = r.reshape(8, 256, D)
    return out
